# version 16
# baseline (speedup 1.0000x reference)
import numpy as np
from contextlib import ExitStack

import concourse.bass as bass
import concourse.mybir as mybir
from concourse.bass_utils import run_bass_kernel_spmd

F32 = mybir.dt.float32
BF16 = mybir.dt.bfloat16
AF = mybir.ActivationFunctionType
ALU = mybir.AluOpType
AX = mybir.AxisListType

EPS = 1e-6
LN_EPS = 1e-5


class Cfg:
    def __init__(self, T=2048, D=1024, DFF=2048, H=16, E=8):
        assert D % 512 == 0 and DFF % 128 == 0 and T % 1024 == 0
        assert D // H == 64
        self.T, self.D, self.DFF, self.H, self.E = T, D, DFF, H, E
        self.DC = D // 128
        self.FC = DFF // 128
        self.TC = T // 128
        self.NSL = T // 512
        self.SL = 512
        self.TPS = self.SL // 128


CFG = Cfg()
NDUM = (2, 1)
ENGS = ("tensor", "vector", "scalar", "gpsimd", "sync")
_blk_ctr = [0]


class Prog:
    def __init__(self, nc, ctx):
        self.nc = nc
        self.ctx = ctx
        self.sems = {}
        self.cnt = {}

    def sem(self, name):
        if name not in self.sems:
            self.sems[name] = self.ctx.enter_context(self.nc.semaphore(name))
            self.cnt[name] = 0
        return self.sems[name]

    def now(self, name):
        self.sem(name)
        return (name, self.cnt[name])


class Phase:

    def __init__(self, p, name=""):
        self.p = p
        _blk_ctr[0] += 1
        self.name = "%s_%d" % (name, _blk_ctr[0])
        self.ops = {e: [] for e in ENGS}

    def op(self, eng, fn, inc=(), waits=()):
        p = self.p
        evs = []
        for sname, by in inc:
            p.sem(sname)
            p.cnt[sname] += by
            evs.append((sname, p.cnt[sname]))
        wd = {}
        for wv in waits:
            if wv is None:
                continue
            sname, val = wv
            wd[sname] = max(wd.get(sname, 0), val)
        self.ops[eng].append((sorted(wd.items()), fn, [(s, b) for s, b in inc]))
        return evs[0] if evs else None

    def run(self):
        p = self.p

        def mk_body(oplist):
            def body(eng):
                for waits, fn, incs in oplist:
                    for sname, val in waits:
                        if val > 0:
                            eng.wait_ge(p.sems[sname], val)
                    inst = fn(eng)
                    for sname, by in incs:
                        inst.then_inc(p.sems[sname], by)
            return body

        with p.nc.Block(self.name) as blk:
            for e in ENGS:
                if self.ops[e]:
                    getattr(blk, e)(mk_body(self.ops[e]))


def build(cfg=CFG, dumps=(), stop=None):
    T, D, DFF, H, E = cfg.T, cfg.D, cfg.DFF, cfg.H, cfg.E
    DC, FC, TC = cfg.DC, cfg.FC, cfg.TC
    NSL, SL, TPS = cfg.NSL, cfg.SL, cfg.TPS
    KC = T // 128
    NS = T // 512
    QH = T // 1024

    nc = bass.Bass("TRN2", target_bir_lowering=False)

    xT_d = nc.dram_tensor("xT", [D, T], F32, kind="ExternalInput")
    WrT_d = nc.dram_tensor("WrT", [D, E], F32, kind="ExternalInput")
    rb_d = nc.dram_tensor("rb", [1, E], F32, kind="ExternalInput")
    WeT_d = nc.dram_tensor("WeT", [E, D, D], F32, kind="ExternalInput")
    Be_d = nc.dram_tensor("Be", [E, D], F32, kind="ExternalInput")
    W1T_d = nc.dram_tensor("W1T", [D, DFF], F32, kind="ExternalInput")
    b1T_d = nc.dram_tensor("b1T", [128, FC], F32, kind="ExternalInput")
    W2T_d = nc.dram_tensor("W2T", [DFF, D], F32, kind="ExternalInput")
    b2T_d = nc.dram_tensor("b2T", [128, DC], F32, kind="ExternalInput")
    WqT_d = nc.dram_tensor("WqT", [D, D], F32, kind="ExternalInput")
    WkT_d = nc.dram_tensor("WkT", [D, D], F32, kind="ExternalInput")
    WvT_d = nc.dram_tensor("WvT", [D, D], F32, kind="ExternalInput")
    WoT_d = nc.dram_tensor("WoT", [D, D], F32, kind="ExternalInput")
    yT_d = nc.dram_tensor("yT", [D, T], F32, kind="ExternalOutput")

    WeC_d = nc.dram_tensor("WeC", [E, D, D], BF16, kind="Internal")
    xsp_d = nc.dram_tensor("xsp", [D, T], F32, kind="Internal")

    dump_outs = {}

    def want(tag):
        return tag in dumps

    with ExitStack() as ctx0:
        p = Prog(nc, ctx0)

        def sb(name, shape, dtype, c=None, side="left"):
            return (c or ctx0).enter_context(
                nc.sbuf_tensor(name, shape, dtype, side=side))

        def ps(name, shape, dtype, c=None, side="left"):
            return (c or ctx0).enter_context(
                nc.psum_tensor(name, shape, dtype, side=side))

        ident = sb("ident", [128, 128], BF16)
        ones_col = sb("ones_col", [128, 1], BF16)
        ones_col0 = sb("ones_col0", [128, 1], BF16)
        mean_col = sb("mean_col", [128, 1], BF16)
        ones_row = sb("ones_row", [1, 128], BF16)
        ones_rowP = sb("ones_rowP", [65, 128], BF16)
        sel_sb = sb("sel_sb", [8, E * 128], BF16)
        oh16 = sb("oh16", [H, H * 64], BF16)
        b1T = sb("b1T_s", [128, FC], F32)
        b2T = sb("b2T_s", [128, DC], F32)
        Be_bf = sb("Be_bf", [8, D], BF16)
        wr32 = sb("wr32", [128, DC * E], F32)
        wr_hi = sb("wr_hi", [128, DC * E], BF16)
        wr_lo = sb("wr_lo", [128, DC * E], BF16)
        rb32 = sb("rb32", [1, E], F32)
        rb_hi = sb("rb_hi", [1, E], BF16)
        rb_lo = sb("rb_lo", [1, E], BF16)
        negB = sb("negB", [128, 1], F32)
        c_lneps = sb("c_lneps", [128, 1], F32)
        c_1eps = sb("c_1eps", [128, 1], F32)
        c_neg1 = sb("c_neg1", [128, 1], F32)
        c_pos1 = sb("c_pos1", [128, 1], F32)

        ctx_xr = ExitStack()
        xr = sb("xr", [128, DC * T], F32, ctx_xr)

        def mk_sl(t):
            def sl(c, t0=0, tn=None):
                tn = T if tn is None else tn
                return t[:, c * T + t0: c * T + t0 + tn]
            return sl
        xr_sl = mk_sl(xr)

        ph = Phase(p, "prologue")
        ev_xT = ph.op("sync", lambda e: e.dma_start(
            xr[:, :].rearrange("p (c t) -> p c t", c=DC),
            xT_d[:, :].rearrange("(c p) t -> p c t", p=128)), inc=[("dxp", 16)])
        ph.op("sync", lambda e: e.dma_start(
            wr32[:, :].rearrange("p (c e) -> p c e", c=DC),
            WrT_d[:, :].rearrange("(c p) e -> p c e", p=128)), inc=[("dxp", 16)])
        ph.op("sync", lambda e: e.dma_start(rb32[:, :], rb_d[:, :]), inc=[("dxp", 16)])
        ph.op("sync", lambda e: e.dma_start(b1T[:, :], b1T_d[:, :]), inc=[("dxp", 16)])
        ph.op("sync", lambda e: e.dma_start(b2T[:, :], b2T_d[:, :]), inc=[("dxp", 16)])
        ev_be = ph.op("gpsimd", lambda e: e.dma_start(Be_bf[:, :], Be_d[:, :]), inc=[("dgp", 16)])
        ev = ph.op("gpsimd", lambda e: e.memset(ident[:, :], 0.0), inc=[("pl", 1)])
        ph.op("gpsimd", lambda e: e.affine_select(
            out=ident[:, :], in_=ident[:, :], compare_op=ALU.not_equal,
            fill=1.0, base=0, pattern=[[-1, 128]], channel_multiplier=1),
            inc=[("pl", 1)], waits=[ev])
        ph.op("gpsimd", lambda e: e.memset(ones_col[:, :], 1.0), inc=[("pl", 1)])
        ev = ph.op("gpsimd", lambda e: e.memset(ones_col0[:, :], 1.0), inc=[("pl", 1)])
        ph.op("gpsimd", lambda e: e.memset(ones_col0[0:1, :], 0.0),
              inc=[("pl", 1)], waits=[ev])
        ph.op("gpsimd", lambda e: e.memset(mean_col[:, :], 1.0 / D), inc=[("pl", 1)])
        ph.op("gpsimd", lambda e: e.memset(ones_row[:, :], 1.0), inc=[("pl", 1)])
        ph.op("gpsimd", lambda e: e.memset(ones_rowP[:, :], 1.0), inc=[("pl", 1)])
        ph.op("gpsimd", lambda e: e.memset(c_lneps[:, :], LN_EPS), inc=[("pl", 1)])
        ph.op("gpsimd", lambda e: e.memset(c_1eps[:, :], 1.0 + EPS), inc=[("pl", 1)])
        ph.op("gpsimd", lambda e: e.memset(c_neg1[:, :], -1.0), inc=[("pl", 1)])
        ph.op("gpsimd", lambda e: e.memset(c_pos1[:, :], 1.0), inc=[("pl", 1)])
        ev = ph.op("gpsimd", lambda e: e.memset(sel_sb[:, :], 0.0), inc=[("pl", 1)])
        ph.op("gpsimd", lambda e: e.affine_select(
            out=bass.AP(sel_sb, 0, [[E * 128, 8], [128, E], [1, 128]]),
            in_=bass.AP(sel_sb, 0, [[E * 128, 8], [128, E], [1, 128]]),
            compare_op=ALU.not_equal, fill=1.0, base=0,
            pattern=[[-1, E], [0, 128]], channel_multiplier=1),
            inc=[("pl", 1)], waits=[ev])
        ev = ph.op("gpsimd", lambda e: e.memset(oh16[:, :], 0.0), inc=[("pl", 1)])
        ph.op("gpsimd", lambda e: e.affine_select(
            out=bass.AP(oh16, 0, [[H * 64, H], [64, H], [1, 64]]),
            in_=bass.AP(oh16, 0, [[H * 64, H], [64, H], [1, 64]]),
            compare_op=ALU.not_equal, fill=1.0, base=0,
            pattern=[[-1, H], [0, 64]], channel_multiplier=1),
            inc=[("pl", 1)], waits=[ev])
        ph.run()

        ph = Phase(p, "loads_wait")
        wall = [p.now("dxp"), p.now("dgp"), p.now("pl")]
        for e_ in ENGS:
            ph.op(e_, lambda e: e.nop(), waits=wall)
        ph.run()

        ph = Phase(p, "router_split")
        ev_hi = ph.op("scalar", lambda e: e.copy(wr_hi[:, :], wr32[:, :]),
                      inc=[("act", 1)])
        ph.op("vector", lambda e: e.tensor_sub(wr_lo[:, :], wr32[:, :], wr_hi[:, :]),
              inc=[("dve", 1)], waits=[ev_hi])
        ev_hi = ph.op("scalar", lambda e: e.copy(rb_hi[:, :], rb32[:, :]),
                      inc=[("act", 1)])
        ph.op("vector", lambda e: e.tensor_sub(rb_lo[:, :], rb32[:, :], rb_hi[:, :]),
              inc=[("dve", 1)], waits=[ev_hi])
        ph.run()

        def dump_sbuf(tag, ap, shape):
            d = nc.dram_tensor("dump_" + tag, list(shape), F32, kind="ExternalOutput")
            dump_outs[tag] = d
            ph = Phase(p, "dump_" + tag)
            ph.op("gpsimd", lambda e: e.dma_start(d[:, :], ap), inc=[("dgd", 16)])
            ph.op("gpsimd", lambda e: e.nop(), waits=[p.now("dgd")])
            ph.run()

        if want("xr0"):
            dump_sbuf("xr0", xr[:, :], [128, DC * T])

        def layernorm(tag, xcur_sl, out_hi, out_lo, chunk_consumer=None,
                      after_slice=None, psum_ctx=None):
            with ExitStack() as c1:
                xbf = [sb("xbf%s%d" % (tag, i), [128, SL], BF16, c1)
                       for i in range(2)]
                sqb = [sb("sqb%s%d" % (tag, i), [128, SL], BF16, c1)
                       for i in range(2)]
                m_row = sb("m_row" + tag, [65, SL], F32, c1)
                v2_row = sb("v2_row" + tag, [65, SL], F32, c1)
                ra = sb("ra_row" + tag, [65, SL], F32, c1)
                m_bf = sb("m_bf" + tag, [65, SL], BF16, c1)
                r_bf = sb("r_bf" + tag, [65, SL], BF16, c1)
                t32 = [sb("t32%s%d" % (tag, i), [128, SL], F32, c1)
                       for i in range(2)]
                with ExitStack() as c2:
                    pm = ps("pm" + tag, [65, SL], F32, c2)
                    pv = ps("pv" + tag, [65, SL], F32, c2)
                    pmb = ps("pmb" + tag, [128, SL], F32, c2)
                    prb = ps("prb" + tag, [128, SL], F32, c2)
                    ph = Phase(p, "ln_" + tag)
                    ev_mm = {}
                    ev_cs = {}
                    ev_hi = {}
                    ev_lo = {}
                    ev_ap = {}
                    prb_readers = {}
                    ev_mv = {}
                    for s in range(NSL):
                        t0 = s * SL
                        r = (s % 2) * 64
                        sr = slice(r, r + 1)
                        for c in range(DC):
                            g = s * DC + c
                            b = g % 2
                            w = [ev_mm[g - 2]] if (g - 2) in ev_mm else []
                            ev_cast = ph.op("scalar", lambda e, c=c, b=b, t0=t0:
                                            e.copy(xbf[b][:, :],
                                                   xcur_sl(c, t0, SL)),
                                            inc=[("act", 1)], waits=w)
                            ev_sq = ph.op("scalar", lambda e, c=c, b=b, t0=t0:
                                          e.activation(sqb[b][:, :],
                                                       xcur_sl(c, t0, SL),
                                                       AF.Square),
                                          inc=[("act", 1)], waits=w)
                            ev_cs[g] = (ev_cast, ev_sq)
                            wm_ = list(ev_cs[g])
                            if c == 0 and (s - 2) in ev_mv:
                                wm_ += ev_mv[s - 2]

                            def mm(e, sr=sr, c=c, b=b):
                                e.matmul(pm[sr, :], mean_col[:, :], xbf[b][:, :],
                                         start=(c == 0), stop=(c == DC - 1))
                                return e.matmul(pv[sr, :], mean_col[:, :],
                                                sqb[b][:, :],
                                                start=(c == 0),
                                                stop=(c == DC - 1))
                            ev_mm[g] = ph.op("tensor", mm, inc=[("pe", 1)],
                                             waits=wm_)
                        ev = ev_mm[s * DC + DC - 1]
                        wst = [ev]
                        if (s - 2) in prb_readers:
                            wst.append(prb_readers[s - 2][-1])
                        ev_m = ph.op("scalar", lambda e, sr=sr: e.copy(
                            m_row[sr, :], pm[sr, :]), inc=[("act", 1)], waits=wst)
                        ev_v2 = ph.op("scalar", lambda e, sr=sr: e.copy(
                            v2_row[sr, :], pv[sr, :]), inc=[("act", 1)], waits=wst)
                        ev_mv[s] = [ev_m, ev_v2]
                        wsq = [ev_m]
                        if (s - 2) in ev_mv:
                            wsq.append(ev_mv[s - 2][0])
                        ev_msq = ph.op("vector", lambda e, sr=sr: e.tensor_mul(
                            ra[sr, :], m_row[sr, :], m_row[sr, :]),
                            inc=[("dve", 1)], waits=wsq)
                        ev_var = ph.op("vector", lambda e, sr=sr: e.tensor_sub(
                            v2_row[sr, :], v2_row[sr, :], ra[sr, :]),
                            inc=[("dve", 1)], waits=[ev_msq, ev_v2])
                        ev_ln = ph.op("scalar", lambda e, sr=sr: e.activation(
                            ra[sr, :], v2_row[sr, :], AF.Ln,
                            bias=c_lneps[sr, 0:1]), inc=[("act", 1)],
                            waits=[ev_var])
                        ev_rs = ph.op("scalar", lambda e, sr=sr: e.activation(
                            ra[sr, :], ra[sr, :], AF.Exp, scale=-0.5),
                            inc=[("act", 1)], waits=[ev_ln])
                        ev_mbf = ph.op("scalar", lambda e, sr=sr: e.copy(
                            m_bf[sr, :], m_row[sr, :]), inc=[("act", 1)],
                            waits=[ev_m])
                        ev_rbf = ph.op("scalar", lambda e, sr=sr: e.copy(
                            r_bf[sr, :], ra[sr, :]), inc=[("act", 1)],
                            waits=[ev_rs])
                        wbc = [ev_mbf, ev_rbf] + prb_readers.get(s - 1, [])

                        def bc(e, sr=sr):
                            e.matmul(pmb[:, :], ones_rowP[sr, :], m_bf[sr, :],
                                     start=True, stop=True)
                            return e.matmul(prb[:, :], ones_rowP[sr, :],
                                            r_bf[sr, :], start=True, stop=True)
                        ev_bc = ph.op("tensor", bc, inc=[("pe", 1)], waits=wbc)
                        rdrs = []
                        for c in range(DC):
                            g = s * DC + c
                            b = g % 2
                            w = [ev_bc]
                            if (g - 2) in ev_ap:
                                w.append(ev_ap[g - 2])
                                w.append(ev_hi[g - 2])
                            ev_sub = ph.op("vector",
                                           lambda e, c=c, b=b, t0=t0: e.tensor_sub(
                                               t32[b][:, :], xcur_sl(c, t0, SL),
                                               pmb[:, :]),
                                           inc=[("dve", 1)], waits=w)
                            if out_lo is None:
                                ev_p = ph.op(
                                    "vector", lambda e, c=c, b=b, t0=t0: e.tensor_mul(
                                        out_hi[:, c * T + t0: c * T + t0 + SL],
                                        t32[b][:, :], prb[:, :]),
                                    inc=[("dve", 1)], waits=[ev_sub])
                                ev_hi[g] = ev_p
                            else:
                                ev_p = ph.op(
                                    "vector", lambda e, b=b: e.tensor_mul(
                                        t32[b][:, :], t32[b][:, :], prb[:, :]),
                                    inc=[("dve", 1)], waits=[ev_sub])
                                ev_hi[g] = ph.op(
                                    "scalar", lambda e, c=c, b=b, t0=t0: e.copy(
                                        out_hi[:, c * T + t0: c * T + t0 + SL],
                                        t32[b][:, :]), inc=[("act", 1)],
                                    waits=[ev_p])
                            rdrs.append(ev_p)
                            last = ev_p
                            if out_lo is not None:
                                ev_lo[g] = ph.op(
                                    "vector", lambda e, c=c, b=b, t0=t0: e.tensor_sub(
                                        out_lo[:, c * T + t0: c * T + t0 + SL],
                                        t32[b][:, :],
                                        out_hi[:, c * T + t0: c * T + t0 + SL]),
                                    inc=[("dve", 1)], waits=[ev_hi[g]])
                                last = ev_lo[g]
                            ev_ap[g] = last
                            if chunk_consumer is not None:
                                chunk_consumer(ph, c, s, ev_hi[g], ev_lo.get(g))
                        prb_readers[s] = rdrs
                        if after_slice is not None:
                            after_slice(ph, s)
                    ph.run()
            return ev_hi, ev_ap

        ffn_state = {"cached": False}

        def ffn_sublayer(tag, xcur_sl):
            with ExitStack() as cf:
                xnhi = sb("xnhi" + tag, [128, DC * T], BF16, cf)
                xnlo_ctx = ExitStack()
                xnlo = sb("xnlo" + tag, [128, DC * T], BF16, xnlo_ctx, side="right")
                wm_tok = sb("wm_tok" + tag, [128, TC * E], BF16, cf)
                wmT = sb("wmT" + tag, [8, T], BF16, cf)
                rmax_t = sb("rmax_t" + tag, [128, TC], F32, cf)
                sume_t = sb("sume_t" + tag, [128, TC], F32, cf)
                scr_s = sb("scr_s" + tag, [128, TC * E], F32, cf)
                scr_m = sb("scr_m" + tag, [128, TC * E], F32, cf)
                maskb = sb("maskb" + tag, [128, E * T], BF16, cf)
                rt_ctx = ExitStack()
                rt_last = {}
                plg_box = {}
                pwt = ps("pwt" + tag, [8, SL], BF16, rt_ctx)
                pbc = ps("pbc" + tag, [128, SL], F32, rt_ctx)
                ev_wmT = {}
                maskb_ev = {}
                mk_prev = {}

                def router_consumer(ph, c, s, ev_hi_e, ev_lo_e):
                    if "plg" not in plg_box:
                        plg_box["plg"] = ps("plg" + tag, [128, TC * E], F32,
                                            rt_ctx, side="right")
                    plg = plg_box["plg"]
                    tcs = list(range(s * TPS, (s + 1) * TPS))

                    def rmm(e, c=c, s=s, tcs=tuple(tcs)):
                        i = None
                        for tc in tcs:
                            for j, (lh, rh) in enumerate(
                                    ((xnhi, wr_hi), (xnhi, wr_lo), (xnlo, wr_hi))):
                                i = e.matmul(
                                    plg[:, tc * E:(tc + 1) * E],
                                    lh[:, c * T + tc * 128: c * T + (tc + 1) * 128],
                                    rh[:, c * E:(c + 1) * E],
                                    start=(c == 0 and s == 0
                                           and tc == tcs[0] and j == 0),
                                    stop=False)
                            if c == DC - 1:
                                e.matmul(plg[:, tc * E:(tc + 1) * E], ones_row[:, :],
                                         rb_hi[:, :], start=False, stop=False)
                                i = e.matmul(plg[:, tc * E:(tc + 1) * E],
                                             ones_row[:, :], rb_lo[:, :],
                                             start=False,
                                             stop=(s == NSL - 1
                                                   and tc == tcs[-1]))
                        return i
                    rt_last[s] = ph.op("tensor", rmm, inc=[("pe", 1)],
                                       waits=[ev_hi_e, ev_lo_e])

                def bc3(t, s):
                    return bass.AP(t, s * TPS, [[TC, 128], [1, TPS], [0, E]])

                def ap3(t, s):
                    return bass.AP(t, s * TPS * E,
                                   [[TC * E, 128], [E, TPS], [1, E]])

                def after_slice(ph, s):
                    plg = plg_box["plg"]
                    ev_lg = rt_last[s]
                    cl, ch = s * TPS, (s + 1) * TPS
                    ev1 = ph.op("vector", lambda e, s=s, cl=cl, ch=ch:
                                e.tensor_reduce(rmax_t[:, cl:ch], ap3(plg, s),
                                                axis=AX.X, op=ALU.max),
                                inc=[("dve", 1)], waits=[ev_lg])
                    ev2 = ph.op("vector", lambda e, s=s: e.tensor_tensor(
                        ap3(scr_s, s), ap3(plg, s), bc3(rmax_t, s),
                        ALU.subtract), inc=[("dve", 1)], waits=[ev1])
                    ev3 = ph.op("vector", lambda e, s=s: e.tensor_tensor(
                        ap3(scr_m, s), ap3(plg, s), bc3(rmax_t, s),
                        ALU.is_equal), inc=[("dve", 1)], waits=[ev1])
                    ev4 = ph.op("scalar", lambda e, s=s: e.activation(
                        scr_s[:, s * TPS * E:(s + 1) * TPS * E],
                        scr_s[:, s * TPS * E:(s + 1) * TPS * E], AF.Exp),
                        inc=[("act", 1)], waits=[ev2])
                    ev5 = ph.op("vector", lambda e, s=s, cl=cl, ch=ch:
                                e.tensor_reduce(sume_t[:, cl:ch], ap3(scr_s, s),
                                                axis=AX.X, op=ALU.add),
                                inc=[("dve", 1)], waits=[ev4])
                    ev6 = ph.op("vector", lambda e, cl=cl, ch=ch: e.reciprocal(
                        sume_t[:, cl:ch], sume_t[:, cl:ch]),
                        inc=[("dve", 1)], waits=[ev5])
                    ev_wm = ph.op("vector", lambda e, s=s: e.scalar_tensor_tensor(
                        ap3(wm_tok, s), ap3(scr_m, s), 0.5, bc3(sume_t, s),
                        ALU.mult, ALU.mult),
                        inc=[("dve", 1)], waits=[ev3, ev6])
                    wtr = [ev_wm]
                    if (s - 1) in ev_wmT:
                        wtr.append(ev_wmT[s - 1])

                    def tr(e, s=s):
                        for k in range(TPS):
                            i = e.transpose(pwt[:, k * 128:(k + 1) * 128],
                                            wm_tok[:, (s * TPS + k) * E:
                                                   (s * TPS + k + 1) * E],
                                            ident[:, :])
                        return i
                    ev_t = ph.op("tensor", tr, inc=[("pe", 1)], waits=wtr)
                    ev_wmT[s] = ph.op("scalar", lambda e, s=s: e.copy(
                        wmT[:, s * SL:(s + 1) * SL], pwt[:, :]),
                        inc=[("act", 1)], waits=[ev_t])
                    for e_i in range(E):
                        idx = s * E + e_i
                        w = [ev_wmT[s]]
                        if (idx - 1) in mk_prev:
                            w.append(mk_prev[idx - 1])

                        def bmm(e, e_i=e_i, s=s):
                            return e.matmul(pbc[:, :],
                                            sel_sb[:, e_i * 128:(e_i + 1) * 128],
                                            wmT[:, s * SL:(s + 1) * SL],
                                            start=True, stop=True)
                        ev_mmb = ph.op("tensor", bmm, inc=[("pe", 1)], waits=w)
                        mk_prev[idx] = ph.op(
                            "scalar", lambda e, e_i=e_i, s=s: e.copy(
                                maskb[:, e_i * T + s * SL:
                                      e_i * T + (s + 1) * SL], pbc[:, :]),
                            inc=[("act", 1)], waits=[ev_mmb])
                        maskb_ev[(s, e_i)] = mk_prev[idx]

                ev_hi_ln, _ = layernorm(tag, xcur_sl, xnhi, xnlo,
                                        chunk_consumer=router_consumer,
                                        after_slice=after_slice)
                rt_ctx.close()
                xnlo_ctx.close()
                if want("ln" + tag):
                    dump_sbuf("ln" + tag, xnhi[:, :], [128, DC * T])
                if want("wm" + tag):
                    dump_sbuf("wm" + tag, wm_tok[:, :], [128, TC * E])

                wbuf = [sb("wbuf%s%d" % (tag, i), [128, DC * D], BF16, cf)
                        for i in range(2)]
                xnm = [sb("xnm%s%d" % (tag, i), [128, DC * 512], BF16, cf)
                       for i in range(2)]
                with ExitStack() as c2:
                    py = ps("py" + tag, [128, DC * 512], F32, c2)
                    ph = Phase(p, "experts_" + tag)
                    cached = ffn_state["cached"]
                    load_ev = {}
                    wb_ev = {}
                    mm_ev = {}
                    res_ev = {}
                    TQn = T // 512
                    nli = TQn * E

                    def emit_load(li):
                        tc_, e_ = divmod(li, E)
                        w = []
                        if li >= 2:
                            w.append(mm_ev[li - 2])
                            if (li - 2) in wb_ev:
                                w.append(wb_ev[li - 2])
                        if not cached and tc_ == 0:
                            load_ev[li] = ph.op(
                                "gpsimd", lambda e, e_=e_, li=li: e.dma_start(
                                    wbuf[li % 2][:, :].rearrange(
                                        "p (c n) -> p c n", c=DC),
                                    WeT_d[e_].rearrange("(c p) n -> p c n", p=128)),
                                inc=[("dg%d" % (li % 2), 16)], waits=w)
                            wb_ev[li] = ph.op("sync", lambda e, e_=e_, li=li: e.dma_start(
                                WeC_d[e_].rearrange("(c p) n -> p c n", p=128),
                                wbuf[li % 2][:, :].rearrange(
                                    "p (c n) -> p c n", c=DC)),
                                inc=[("dhw%d" % (li % 2), 16)], waits=[load_ev[li]])
                        else:
                            load_ev[li] = ph.op(
                                "sync", lambda e, e_=e_, li=li: e.dma_start(
                                    wbuf[li % 2][:, :].rearrange(
                                        "p (c n) -> p c n", c=DC),
                                    WeC_d[e_].rearrange("(c p) n -> p c n", p=128)),
                                inc=[("dh%d" % (li % 2), 16)], waits=w)

                    emit_load(0)
                    emit_load(1)
                    for tc in range(TQn):
                        for e_i in range(E):
                            li = tc * E + e_i
                            w = [maskb_ev[(tc, e_i)],
                                 ev_hi_ln[(tc + 1) * DC - 1]]
                            if li >= 2:
                                w.append(mm_ev[li - 2])

                            def vmask(e, tc=tc, e_i=e_i, li=li):
                                for c in range(DC):
                                    i = e.tensor_mul(
                                        xnm[li % 2][:, c * 512:(c + 1) * 512],
                                        xnhi[:, c * T + tc * 512:
                                             c * T + tc * 512 + 512],
                                        maskb[:, e_i * T + tc * 512:
                                              e_i * T + tc * 512 + 512])
                                return i
                            ev_vm = ph.op("vector", vmask, inc=[("dve", 1)], waits=w)
                            w = [ev_vm, load_ev[li]]
                            if tc > 0 and e_i == 0:
                                w.append(res_ev[(tc - 1, 0)])

                            def emma(e, tc=tc, e_i=e_i, li=li):
                                for m in range(DC // 2):
                                    for c in range(DC):
                                        i = e.matmul(
                                            py[:, m * 512:(m + 1) * 512],
                                            wbuf[li % 2][:, c * D + m * 128:
                                                         c * D + (m + 1) * 128],
                                            xnm[li % 2][:, c * 512:(c + 1) * 512],
                                            start=(e_i == 0 and c == 0), stop=False)
                                return i

                            def emmb(e, tc=tc, e_i=e_i, li=li):
                                for m in range(DC // 2, DC):
                                    for c in range(DC):
                                        i = e.matmul(
                                            py[:, m * 512:(m + 1) * 512],
                                            wbuf[li % 2][:, c * D + m * 128:
                                                         c * D + (m + 1) * 128],
                                            xnm[li % 2][:, c * 512:(c + 1) * 512],
                                            start=(e_i == 0 and c == 0), stop=False)
                                return i
                            ph.op("tensor", emma, inc=[("pe", 1)], waits=w)
                            wb_ = [] if not (tc > 0 and e_i == 0) \
                                else [res_ev[(tc - 1, 1)]]
                            mm_ev[li] = ph.op("tensor", emmb, inc=[("pe", 1)],
                                              waits=wb_)
                            if li + 2 < nli:
                                emit_load(li + 2)

                        def bmm2a(e, tc=tc):
                            for m in range(DC // 2):
                                i = e.matmul(py[:, m * 512:(m + 1) * 512],
                                             Be_bf[:, m * 128:(m + 1) * 128],
                                             wmT[:, tc * 512: tc * 512 + 512],
                                             start=False, stop=True)
                            return i

                        def bmm2b(e, tc=tc):
                            for m in range(DC // 2, DC):
                                i = e.matmul(py[:, m * 512:(m + 1) * 512],
                                             Be_bf[:, m * 128:(m + 1) * 128],
                                             wmT[:, tc * 512: tc * 512 + 512],
                                             start=False, stop=True)
                            return i
                        ev_ba = ph.op("tensor", bmm2a, inc=[("pe", 1)],
                                      waits=[ev_wmT[tc]])
                        ev_bb = ph.op("tensor", bmm2b, inc=[("pe", 1)], waits=[])

                        def radda(e, tc=tc):
                            for m in range(DC // 2):
                                i = e.tensor_add(
                                    xcur_sl(m, tc * 512, 512),
                                    xcur_sl(m, tc * 512, 512),
                                    py[:, m * 512:(m + 1) * 512])
                            return i

                        def raddb(e, tc=tc):
                            for m in range(DC // 2, DC):
                                i = e.tensor_add(
                                    xcur_sl(m, tc * 512, 512),
                                    xcur_sl(m, tc * 512, 512),
                                    py[:, m * 512:(m + 1) * 512])
                            return i
                        res_ev[(tc, 0)] = ph.op("vector", radda, inc=[("dve", 1)],
                                                waits=[ev_ba])
                        res_ev[(tc, 1)] = ph.op("vector", raddb, inc=[("dve", 1)],
                                                waits=[ev_bb])
                    ph.run()
                return res_ev

        f1_res = ffn_sublayer("f1", xr_sl)
        ffn_state["cached"] = True
        if want("ffn1"):
            dump_sbuf("ffn1", xr[:, :], [128, DC * T])
        if stop == "ffn1":
            _finish(p)
            ctx_xr.close()
            return nc, dump_outs

        ctx_at1 = ExitStack()
        xnT = sb("xnT", [128, DC * T], BF16, ctx_at1, side="right")
        g_sb = sb("g_sb", [128, DC * T], BF16, ctx_at1, side="right")

        ph = Phase(p, "spill")
        ev_spill = ph.op("sync", lambda e: e.dma_start(
            xsp_d[:, :].rearrange("(c p) t -> p c t", p=128),
            xr[:, :].rearrange("p (c t) -> p c t", c=DC)), inc=[("dsp", 16)],
            waits=[f1_res[(T // 512 - 1, 0)], f1_res[(T // 512 - 1, 1)]])
        ph.run()

        vch_ctx = ExitStack()
        clipb = [sb("clipb%d" % i, [128, SL], BF16, vch_ctx, side="right")
                 for i in range(2)]
        sqvb = [sb("sqvb%d" % i, [128, SL], BF16, vch_ctx, side="right")
                for i in range(2)]
        pss_ctx = ExitStack()
        pss_box = {}
        vch_ev = {"cl": {}, "sq": {}, "mm": {}}

        def vchain_consumer(ph, c, s, ev_hi_e, ev_lo_e):
            if "pss" not in pss_box:
                pss_box["pss"] = ps("pss", [1, T], F32, pss_ctx, side="right")
            pss = pss_box["pss"]
            t0 = s * SL
            gi = s * DC + c
            b = gi % 2
            w = [ev_hi_e]
            if gi >= 2:
                w.append(vch_ev["sq"][gi - 2])
            ev_cl = ph.op("vector", lambda e, c=c, t0=t0, b=b: e.tensor_scalar(
                clipb[b][:, :], xnT[:, c * T + t0: c * T + t0 + SL],
                8.0, -8.0, ALU.min, ALU.max), inc=[("dve", 1)], waits=w)
            vch_ev["cl"][gi] = ev_cl
            w2 = [ev_cl]
            if gi >= 2:
                w2.append(vch_ev["mm"][gi - 2])
            ev_sq = ph.op("scalar", lambda e, b=b: e.activation(
                sqvb[b][:, :], clipb[b][:, :], AF.Square),
                inc=[("act", 1)], waits=w2)
            vch_ev["sq"][gi] = ev_sq

            def smm(e, c=c, t0=t0, b=b):
                lhs = ones_col0 if c == 0 else ones_col
                return e.matmul(pss[:, t0: t0 + SL], lhs[:, :], sqvb[b][:, :],
                                start=(c == 0), stop=(c == DC - 1))
            vch_ev["mm"][gi] = ph.op("tensor", smm, inc=[("pe", 1)], waits=[ev_sq])

        layernorm("a", xr_sl, xnT, None, chunk_consumer=vchain_consumer)
        ev_ssdone = [vch_ev["mm"][s * DC + DC - 1] for s in range(NSL)]

        ph = Phase(p, "spill_wait")
        for e_ in ENGS:
            ph.op(e_, lambda e: e.nop(), waits=[ev_spill])
        ph.run()
        ctx_xr.close()

        if want("lna"):
            dump_sbuf("lna", xnT[:, :], [128, DC * T])
        if stop == "lna":
            _finish(p)
            ctx_at1.close()
            return nc, dump_outs

        with ExitStack() as cm:
            W1s = sb("W1s", [128, DC * DFF], BF16, cm)
            W2s = sb("W2s", [128, FC * D], BF16, cm)
            h1 = sb("h1", [128, FC * 512], BF16, cm)
            th = [sb("th%d" % i, [128, 512], F32, cm) for i in range(2)]
            ph = Phase(p, "metric")
            HC2 = DC // 2
            ev_w1h = []
            for hw in range(2):
                ev_w1h.append(ph.op("gpsimd", lambda e, hw=hw: e.dma_start(
                    W1s[:, hw * HC2 * DFF:(hw + 1) * HC2 * DFF].rearrange(
                        "p (c n) -> p c n", c=HC2),
                    W1T_d[hw * (D // 2):(hw + 1) * (D // 2), :].rearrange(
                        "(c p) n -> p c n", p=128)),
                    inc=[("dw1%d" % hw, 16)]))
            ev_w1 = ev_w1h[1]
            ev_w2 = ph.op("gpsimd", lambda e: e.dma_start(
                W2s[:, :].rearrange("p (c n) -> p c n", c=FC),
                W2T_d[:, :].rearrange("(c p) n -> p c n", p=128)), inc=[("dgw", 16)])
            ev_adj = ph.op("vector", lambda e: e.tensor_scalar_add(
                xnT[0:1, 0:T], xnT[0:1, 0:T], -1.0), inc=[("dve", 1)])
            with ExitStack() as c2:
                ph1 = [ps("ph1%d" % i, [128, 512], F32, c2) for i in range(2)]
                pg = [ps("pg%d" % i, [128, 512], F32, c2) for i in range(2)]
                ev_h1ev = {}
                ev_thev = {}
                ev_gw = {}
                for q in range(T // 512):
                    for m in range(FC):
                        b = m % 2
                        w = [ev_adj]
                        if (q, m - 2) in ev_h1ev:
                            w.append(ev_h1ev[(q, m - 2)])
                        elif q > 0:
                            w.append(ev_h1ev[(q - 1, FC - 2 + m)])
                        if q > 0 and m < 2:
                            w.append(ev_gw[(q - 1, DC - 1)])

                        def h1mma(e, q=q, m=m, b=b):
                            for c in range(HC2):
                                i = e.matmul(
                                    ph1[b][:, :],
                                    W1s[:, c * DFF + m * 128: c * DFF + (m + 1) * 128],
                                    xnT[:, c * T + q * 512: c * T + q * 512 + 512],
                                    start=(c == 0), stop=False)
                            return i

                        def h1mmb(e, q=q, m=m, b=b):
                            for c in range(HC2, DC):
                                i = e.matmul(
                                    ph1[b][:, :],
                                    W1s[:, c * DFF + m * 128: c * DFF + (m + 1) * 128],
                                    xnT[:, c * T + q * 512: c * T + q * 512 + 512],
                                    start=False, stop=(c == DC - 1))
                            return i
                        wa = [x for x in w if x is not ev_w2] + [ev_w1h[0]]
                        ph.op("tensor", h1mma, inc=[("pe", 1)], waits=wa)
                        ev_mm = ph.op("tensor", h1mmb, inc=[("pe", 1)],
                                      waits=[ev_w1h[1]])
                        ev_h1ev[(q, m)] = ph.op(
                            "scalar", lambda e, q=q, m=m, b=b: e.activation(
                                h1[:, m * 512:(m + 1) * 512], ph1[b][:, :], AF.Tanh,
                                bias=b1T[:, m:m + 1]), inc=[("act", 1)],
                            waits=[ev_mm])
                    for md in range(DC):
                        b = md % 2
                        w = [ev_w2, ev_h1ev[(q, FC - 1)]]
                        if (q, md - 2) in ev_thev:
                            w.append(ev_thev[(q, md - 2)])
                        elif q > 0:
                            w.append(ev_thev[(q - 1, DC - 2 + md)])

                        def gmm(e, q=q, md=md, b=b):
                            for c in range(FC):
                                i = e.matmul(
                                    pg[b][:, :],
                                    W2s[:, c * D + md * 128: c * D + (md + 1) * 128],
                                    h1[:, c * 512:(c + 1) * 512],
                                    start=(c == 0), stop=(c == FC - 1))
                            return i
                        ev_gw[(q, md)] = ph.op("tensor", gmm, inc=[("pe", 1)], waits=w)
                        ev_th = ph.op("scalar", lambda e, q=q, md=md, b=b: e.activation(
                            th[b][:, :], pg[b][:, :], AF.Tanh, bias=b2T[:, md:md + 1]),
                            inc=[("act", 1)], waits=[ev_gw[(q, md)]])
                        ev_thev[(q, md)] = ev_th
                        ev_ab = ph.op("scalar", lambda e, q=q, md=md, b=b: e.activation(
                            g_sb[:, md * T + q * 512: md * T + q * 512 + 512],
                            th[b][:, :], AF.Abs, scale=0.1, bias=c_pos1[:, 0:1]),
                            inc=[("act", 1)], waits=[ev_th])
                        if md == 0:
                            ev_thev[(q, md)] = ph.op(
                                "scalar", lambda e, q=q, b=b: e.activation(
                                    g_sb[0:1, q * 512: q * 512 + 512],
                                    th[b][0:1, :], AF.Abs, scale=0.1, bias=c_neg1[0:1, 0:1]),
                                inc=[("act", 1)], waits=[ev_ab])
                ph.run()
        if want("g"):
            dump_sbuf("g", g_sb[:, :], [128, DC * T])
        if stop == "g":
            _finish(p)
            ctx_at1.close()
            return nc, dump_outs

        ctx_at2 = ExitStack()

        with ExitStack() as cv:
            r_y0 = sb("r_y0", [1, T], F32, cv)
            r_t = sb("r_t", [1, T], F32, cv)
            r_a = sb("r_a", [1, T], F32, cv)
            r_d = sb("r_d", [1, T], F32, cv)
            r_dv0 = sb("r_dv0", [1, T], F32, cv)
            r_ss = sb("r_ss", [1, T], F32, cv)
            r_s = sb("r_s", [1, T], F32, cv)
            s_bf = sb("s_bf", [1, T], BF16, cv)
            dv0s_bf = sb("dv0s_bf", [1, T], BF16, cv)
            clipv = [sb("clipv%d" % i, [128, T], BF16, cv) for i in range(2)]
            ph = Phase(p, "vchain")
            pss = pss_box["pss"]
            ev_ss = ph.op("scalar", lambda e: e.copy(r_ss[:, :], pss[:, :]),
                          inc=[("act", 1)], waits=ev_ssdone)
            ev1 = ph.op("scalar", lambda e: e.activation(
                r_y0[:, :], r_ss[:, :], AF.Ln, bias=c_1eps[0:1, 0:1]),
                inc=[("act", 1)], waits=[ev_ss])
            ev2 = ph.op("scalar", lambda e: e.activation(
                r_y0[:, :], r_y0[:, :], AF.Exp, scale=0.5),
                inc=[("act", 1)], waits=[ev1])
            ev3 = ph.op("vector", lambda e: e.tensor_scalar_max(
                r_t[:, :], r_y0[:, :], 1.0 + EPS), inc=[("dve", 1)], waits=[ev2])
            ev4 = ph.op("scalar", lambda e: e.activation(
                r_a[:, :], r_t[:, :], AF.Square), inc=[("act", 1)], waits=[ev3])
            ev5 = ph.op("scalar", lambda e: e.activation(
                r_a[:, :], r_a[:, :], AF.Ln, bias=c_neg1[0:1, 0:1]),
                inc=[("act", 1)], waits=[ev4])
            ev6 = ph.op("scalar", lambda e: e.activation(
                r_a[:, :], r_a[:, :], AF.Exp, scale=0.5), inc=[("act", 1)],
                waits=[ev5])
            ev7 = ph.op("vector", lambda e: e.tensor_add(
                r_a[:, :], r_a[:, :], r_t[:, :]), inc=[("dve", 1)], waits=[ev6])
            ev8 = ph.op("scalar", lambda e: e.activation(
                r_a[:, :], r_a[:, :], AF.Ln), inc=[("act", 1)], waits=[ev7])
            ev9 = ph.op("vector", lambda e: e.tensor_scalar_max(
                r_d[:, :], r_a[:, :], 0.001), inc=[("dve", 1)], waits=[ev8])
            ev10 = ph.op("vector", lambda e: e.tensor_sub(
                r_dv0[:, :], r_y0[:, :], r_t[:, :]), inc=[("dve", 1)],
                waits=[ev3, ev9])
            ev11 = ph.op("scalar", lambda e: e.activation(
                r_a[:, :], r_dv0[:, :], AF.Square), inc=[("act", 1)],
                waits=[ev10, ev9])
            ev12 = ph.op("vector", lambda e: e.tensor_sub(
                r_a[:, :], r_ss[:, :], r_a[:, :]), inc=[("dve", 1)], waits=[ev11])
            ev13 = ph.op("vector", lambda e: e.tensor_scalar_max(
                r_a[:, :], r_a[:, :], EPS), inc=[("dve", 1)], waits=[ev12])
            ev14 = ph.op("scalar", lambda e: e.activation(
                r_a[:, :], r_a[:, :], AF.Ln), inc=[("act", 1)], waits=[ev13])
            ev15 = ph.op("scalar", lambda e: e.activation(
                r_a[:, :], r_a[:, :], AF.Exp, scale=0.5), inc=[("act", 1)],
                waits=[ev14])
            ev16 = ph.op("vector", lambda e: e.tensor_scalar_add(
                r_a[:, :], r_a[:, :], EPS), inc=[("dve", 1)], waits=[ev15])
            ev17 = ph.op("vector", lambda e: e.reciprocal(
                r_a[:, :], r_a[:, :]), inc=[("dve", 1)], waits=[ev16])
            ev18 = ph.op("vector", lambda e: e.tensor_mul(
                r_s[:, :], r_d[:, :], r_a[:, :]), inc=[("dve", 1)],
                waits=[ev17, ev9])
            ev_sbf = ph.op("scalar", lambda e: e.copy(s_bf[:, :], r_s[:, :]),
                           inc=[("act", 1)], waits=[ev18])
            ev19 = ph.op("vector", lambda e: e.tensor_mul(
                r_dv0[:, :], r_dv0[:, :], r_s[:, :]), inc=[("dve", 1)],
                waits=[ev18])
            ev_d0 = ph.op("scalar", lambda e: e.copy(dv0s_bf[:, :], r_dv0[:, :]),
                          inc=[("act", 1)], waits=[ev19])
            ph.run()
            pss_ctx.close()
            vch_ctx.close()
            wpb = [sb("wpb%d" % i, [128, DC * D], BF16, ctx_at1, side="right")
                   for i in range(2)]
            with ExitStack() as c2:
                psb = ps("psb", [128, T], F32, c2)
                ph = Phase(p, "vbuild")
                ev_lq = ph.op("gpsimd", lambda e: e.dma_start(
                    wpb[0][:, :].rearrange("p (c n) -> p c n", c=DC),
                    WqT_d[:, :].rearrange("(c p) n -> p c n", p=128)),
                    inc=[("dgw", 16)])
                ev_lk = ph.op("gpsimd", lambda e: e.dma_start(
                    wpb[1][:, :].rearrange("p (c n) -> p c n", c=DC),
                    WkT_d[:, :].rearrange("(c p) n -> p c n", p=128)),
                    inc=[("dgw", 16)])

                def sbc(e):
                    for n in range(NS):
                        i = e.matmul(psb[:, n * 512:(n + 1) * 512], ones_row[:, :],
                                     s_bf[:, n * 512:(n + 1) * 512],
                                     start=True, stop=True)
                    return i
                ev_bc = ph.op("tensor", sbc, inc=[("pe", 1)], waits=[ev_sbf])
                ev_v0 = None
                ev_cl2 = {}
                ev_vv2 = {}
                for c in range(DC):
                    b = c % 2
                    w = []
                    if c >= 2:
                        w.append(ev_vv2[c - 2])
                    ev_cl2[c] = ph.op("vector", lambda e, c=c, b=b: e.tensor_scalar(
                        clipv[b][:, :], xnT[:, c * T:(c + 1) * T],
                        8.0, -8.0, ALU.min, ALU.max), inc=[("dve", 1)], waits=w)
                    ev_vv2[c] = ph.op("vector", lambda e, c=c, b=b: e.tensor_mul(
                        xnT[:, c * T:(c + 1) * T], clipv[b][:, :],
                        psb[:, :]), inc=[("dve", 1)], waits=[ev_cl2[c], ev_bc])
                    if c == 0:
                        ev_v0 = ev_vv2[c]
                ev_row0 = ph.op("scalar", lambda e: e.copy(
                    xnT[0:1, 0:T], dv0s_bf[:, :]),
                    inc=[("act", 1)], waits=[ev_v0, ev_d0])
                ph.run()
        vT = xnT
        if want("vT"):
            dump_sbuf("vT", vT[:, :], [128, DC * T])
        if stop == "vT":
            _finish(p)
            ctx_at1.close()
            return nc, dump_outs

        QG = sb("QG", [128, DC * T], BF16, ctx_at2)
        with ExitStack() as c2:
            pq = [ps("pq%d" % i, [128, T], F32, c2) for i in range(2)]
            ph = Phase(p, "qproj")
            ev_qg = {}
            for m in range(DC):
                b = m % 2
                w = [ev_lk]
                if m >= 2:
                    w.append(ev_qg[m - 2])
                else:
                    w += [ev_row0, ev_vv2[DC - 1], ev_vv2[DC - 2]]

                def qmm(e, m=m, b=b):
                    for c in range(DC):
                        for n in range(NS):
                            i = e.matmul(
                                pq[b][:, n * 512:(n + 1) * 512],
                                wpb[0][:, c * D + m * 128: c * D + (m + 1) * 128],
                                vT[:, c * T + n * 512: c * T + (n + 1) * 512],
                                start=(c == 0), stop=(c == DC - 1))
                    return i
                ev_mm = ph.op("tensor", qmm, inc=[("pe", 1)], waits=w)
                ev_qg[m] = ph.op("vector", lambda e, m=m, b=b: e.tensor_mul(
                    QG[:, m * T:(m + 1) * T], pq[b][:, :],
                    g_sb[:, m * T:(m + 1) * T]), inc=[("dve", 1)], waits=[ev_mm])
            ev_neg = ph.op("vector", lambda e: e.tensor_scalar_mul(
                QG[0:1, 0:T], QG[0:1, 0:T], -1.0), inc=[("dve", 1)],
                waits=[ev_qg[0]])
            ph.run()

            kT = sb("kT", [128, DC * T], BF16, ctx_at2)
            vv = sb("vv", [128, TC * H * 65], BF16, ctx_at2)
            ph = Phase(p, "kproj")
            ev_kev = {}
            for m in range(DC):
                b = m % 2
                w = [ev_lk]
                if m >= 2:
                    w.append(ev_kev[m - 2])
                else:
                    w.append(ev_qg[DC - 2 + m])
                    w.append(ev_neg)

                def kmm(e, m=m, b=b):
                    for c in range(DC):
                        for n in range(NS):
                            i = e.matmul(
                                pq[b][:, n * 512:(n + 1) * 512],
                                wpb[1][:, c * D + m * 128: c * D + (m + 1) * 128],
                                vT[:, c * T + n * 512: c * T + (n + 1) * 512],
                                start=(c == 0), stop=(c == DC - 1))
                    return i
                ev_mm = ph.op("tensor", kmm, inc=[("pe", 1)], waits=w)
                ev_kev[m] = ph.op("scalar", lambda e, m=m, b=b: e.copy(
                    kT[:, m * T:(m + 1) * T], pq[b][:, :]), inc=[("act", 1)],
                    waits=[ev_mm])
            ev_lv = ph.op("gpsimd", lambda e: e.dma_start(
                wpb[0][:, :].rearrange("p (c n) -> p c n", c=DC),
                WvT_d[:, :].rearrange("(c p) n -> p c n", p=128)),
                inc=[("dgw", 16)], waits=[ev_qg[DC - 1]])
            ph.run()

        cb = ExitStack()
        sqs2 = [sb("sqb2%d" % i, [128, T // 2], BF16, cb) for i in range(2)]
        r_m = sb("r_m", [1, 1], F32, cb)
        nb_bf = sb("nb_bf", [1, 1], BF16, cb)
        with ExitStack() as c2:
            pv = [ps("pv%d" % i, [128, D], F32, c2) for i in range(2)]
            pqn = ps("pqn", [1, T], F32, c2)
            ph = Phase(p, "vvproj")
            ev_one = ph.op("gpsimd", lambda e: e.memset(
                bass.AP(vv, 64, [[TC * H * 65, 128], [H * 65, TC], [65, H], [1, 1]]),
                1.0), inc=[("pl", 1)])
            ev_vev = {}
            ev_sq = {}
            ev_nmm = {}
            norm_srcs = []
            for s_, c, ev_ in ([(QG, c, ev_qg[c]) for c in range(DC)] +
                               [(kT, c, ev_kev[c]) for c in range(DC)]):
                for hh in (0, 1):
                    norm_srcs.append((s_, c, hh, ev_))
            for tt in range(TC):
                b = tt % 2
                w = [ev_lv]
                if tt >= 2:
                    w.append(ev_vev[tt - 2])

                def vmm(e, tt=tt, b=b):
                    for c in range(DC):
                        for n in range(D // 512):
                            i = e.matmul(
                                pv[b][:, n * 512:(n + 1) * 512],
                                vT[:, c * T + tt * 128: c * T + (tt + 1) * 128],
                                wpb[0][:, c * D + n * 512: c * D + (n + 1) * 512],
                                start=(c == 0), stop=(c == DC - 1))
                    return i
                ev_mm = ph.op("tensor", vmm, inc=[("pe", 1)], waits=w)
                ev_vev[tt] = ph.op("scalar", lambda e, tt=tt, b=b: e.copy(
                    bass.AP(vv, tt * H * 65, [[TC * H * 65, 128], [65, H], [1, 64]]),
                    bass.AP(pv[b], 0, [[D, 128], [64, H], [1, 64]])),
                    inc=[("act", 1)], waits=[ev_mm, ev_one])
                NK = len(norm_srcs)
                TH2 = T // 2

                def emit_nmm(k):
                    s_, c, hh, ev_ = norm_srcs[k]

                    def nmm(e, k=k, c=c, hh=hh):
                        for n in range(TH2 // 512):
                            i = e.matmul(
                                pqn[:, hh * TH2 + n * 512: hh * TH2 + (n + 1) * 512],
                                ones_col[:, :],
                                sqs2[k % 2][:, n * 512:(n + 1) * 512],
                                start=(c == 0 and s_ is QG),
                                stop=(c == DC - 1 and s_ is kT))
                        return i
                    ev_nmm[k] = ph.op("tensor", nmm, inc=[("pe", 1)],
                                      waits=[ev_sq[k]])

                for ksub in (2 * tt, 2 * tt + 1):
                    if ksub < NK:
                        src_t, c, hh, ev_src = norm_srcs[ksub]
                        w2 = [ev_src]
                        if (ksub - 2) in ev_nmm:
                            w2.append(ev_nmm[ksub - 2])
                        ev_sq[ksub] = ph.op(
                            "scalar",
                            lambda e, src_t=src_t, c=c, hh=hh, ksub=ksub:
                            e.activation(
                                sqs2[ksub % 2][:, :],
                                src_t[:, c * T + hh * TH2:
                                      c * T + (hh + 1) * TH2],
                                AF.Square), inc=[("act", 1)], waits=w2)
                        if ksub >= 1:
                            emit_nmm(ksub - 1)
            emit_nmm(NK - 1)
            ev = ev_nmm[NK - 1]
            ev2 = ph.op("vector", lambda e: e.reduce_max(
                r_m[:, :], pqn[:, :], axis=AX.X), inc=[("dve", 1)], waits=[ev])
            ev3 = ph.op("vector", lambda e: e.tensor_scalar(
                r_m[:, :], r_m[:, :], 1.5 / 16.0, 43.0, ALU.mult, ALU.min),
                inc=[("dve", 1)], waits=[ev2])
            ev4 = ph.op("vector", lambda e: e.tensor_scalar_mul(
                r_m[:, :], r_m[:, :], -1.0), inc=[("dve", 1)], waits=[ev3])
            ev5 = ph.op("scalar", lambda e: e.copy(nb_bf[:, :], r_m[:, :]),
                        inc=[("act", 1)], waits=[ev4])
            ph.run()
        ctx_at1.close()

        if want("QG"):
            dump_sbuf("QG", QG[:, :], [128, DC * T])
            dump_sbuf("kT", kT[:, :], [128, DC * T])
            dump_sbuf("vv", vv[:, :], [128, TC * H * 65])
        if stop == "QG":
            _finish(p)
            ctx_at2.close()
            return nc, dump_outs

        with ExitStack() as c2:
            pnb = ps("pnb", [128, 1], F32, c2)
            ph = Phase(p, "bbcast")
            ev6 = ph.op("tensor", lambda e: e.matmul(
                pnb[:, :], ones_row[:, :], nb_bf[:, :], start=True, stop=True),
                inc=[("pe", 1)], waits=[ev5])
            ev_nb = ph.op("scalar", lambda e: e.copy(negB[:, :], pnb[:, :]),
                          inc=[("act", 1)], waits=[ev6])
            ph.run()
        cb.close()

        if want("bb"):
            dump_sbuf("bb", negB[:, :], [128, 1])
        if stop == "bb":
            _finish(p)
            ctx_at2.close()
            return nc, dump_outs

        ctx_at3 = ExitStack()
        avout = sb("avout", [128, DC * T], BF16, ctx_at3, side="right")
        wo_sb = sb("wo_sb", [128, DC * D], BF16, ctx_at3, side="right")
        ph = Phase(p, "wo_load")
        ev_lwo = ph.op("gpsimd", lambda e: e.dma_start(
            wo_sb[:, :].rearrange("p (c n) -> p c n", c=DC),
            WoT_d[:, :].rearrange("(c p) n -> p c n", p=128)), inc=[("dgw", 16)])
        ph.run()

        with ExitStack() as ca:
            NB = 3
            NPT = 5
            PT = [sb("PT%d" % i, [128, 1024], BF16, ca) for i in range(NPT)]
            srow = [sb("srow%d" % i, [1, 512], F32, ca) for i in range(4)]
            sums16 = sb("sums16", [H, T], F32, ca)
            QQ = T // 512
            with ExitStack() as c2:
                pS = [ps("pS%d" % i, [128, 1024], F32, c2) for i in range(NB)]
                pav = [ps("pav%d" % i, [65, 512], F32, c2) for i in range(2)]
                ph = Phase(p, "attn_core")
                ev_exp = {}
                ev_avp = {}
                ev_evac = {}
                ev_srr = {}
                ev_sdma = {}
                it_g = [0]
                for hp in range(H // 2):
                    for qq in range(QQ):
                        hq = hp * QQ + qq

                        def emit_av(kc, hq=hq, hp=hp, qq=qq):
                            it = hq * KC + kc
                            w = [ev_exp[it]]
                            if kc == 0 and hq > 0:
                                w.append(ev_evac[(hq - 1, 0)])
                                w.append(ev_evac[(hq - 1, 1)])
                                w.append(ev_srr[(hq - 1, 0)])
                                w.append(ev_srr[(hq - 1, 1)])

                            def avmm(e, hp=hp, kc=kc, it=it):
                                for j in range(2):
                                    i = e.matmul(
                                        pav[j][:, :],
                                        vv[:, kc * H * 65 + (2 * hp + j) * 65:
                                           kc * H * 65 + (2 * hp + j) * 65 + 65],
                                        PT[it % NPT][:, j * 512:(j + 1) * 512],
                                        start=(kc == 0), stop=(kc == KC - 1))
                                return i
                            ev_avp[it] = ph.op("tensor", avmm, inc=[("pe", 1)],
                                               waits=w)

                        for kc in range(KC):
                            it = hq * KC + kc
                            b = kc % NB
                            w = []
                            if it >= NB:
                                w.append(ev_exp[it - NB])

                            def smm(e, hp=hp, qq=qq, kc=kc, b=b):
                                q0 = qq * 512
                                e.matmul(
                                    pS[b][:, 0:512],
                                    kT[0:64, hp * T + kc * 128:
                                       hp * T + (kc + 1) * 128],
                                    QG[0:64, hp * T + q0: hp * T + q0 + 512],
                                    start=True, stop=True)
                                i = e.matmul(
                                    pS[b][:, 512:1024],
                                    kT[64:128, hp * T + kc * 128:
                                       hp * T + (kc + 1) * 128],
                                    QG[64:128, hp * T + q0: hp * T + q0 + 512],
                                    start=True, stop=True)
                                return i
                            ev_s = ph.op("tensor", smm, inc=[("pe", 1)], waits=w)
                            wv_ = [ev_s, ev_nb]
                            if it >= NPT:
                                wv_.append(ev_avp[it - NPT])
                            ev_exp[it] = ph.op(
                                "scalar", lambda e, it=it, b=b: e.activation(
                                    PT[it % NPT][:, :], pS[b][:, :], AF.Exp,
                                    bias=negB[:, 0:1], scale=0.125),
                                inc=[("act", 1)], waits=wv_)
                            if kc >= 1:
                                emit_av(kc - 1)
                        emit_av(KC - 1)
                        for j in range(2):
                            half = j * 64
                            ev_evac[(hq, j)] = ph.op(
                                "vector",
                                lambda e, hp=hp, qq=qq, j=j, half=half: e.tensor_copy(
                                    avout[half:half + 64,
                                          hp * T + qq * 512: hp * T + qq * 512 + 512],
                                    pav[j][0:64, :]),
                                inc=[("dve", 1)],
                                waits=[ev_avp[hq * KC + KC - 1]])
                            si = 2 * (hq % 2) + j
                            w2 = [ev_avp[hq * KC + KC - 1]]
                            prev = ev_sdma.get((hq - 2, j))
                            if prev:
                                w2.append(prev)
                            ev_sr = ph.op("vector", lambda e, j=j, si=si: e.tensor_copy(
                                srow[si][:, :], pav[j][64:65, :]),
                                inc=[("dve", 1)], waits=w2)
                            ev_srr[(hq, j)] = ev_sr
                            ev_sdma[(hq, j)] = ph.op(
                                "sync", lambda e, hp=hp, qq=qq, j=j, si=si: e.dma_start(
                                    sums16[2 * hp + j: 2 * hp + j + 1,
                                           qq * 512:(qq + 1) * 512], srow[si][:, :]),
                                inc=[("dsr%d" % j, 16)], waits=[ev_sr])
                ph.run()
            rec16 = sb("rec16", [H, T], F32, ca)
            rec16b = sb("rec16b", [H, T], BF16, ca)
            ev_attn_done = []
            with ExitStack() as c2:
                pnorm = [ps("pnorm%d" % i, [128, T], F32, c2) for i in range(2)]
                ph = Phase(p, "avnorm")
                ev_r = ph.op("vector", lambda e: e.reciprocal(
                    rec16[:, :], sums16[:, :]), inc=[("dve", 1)],
                    waits=[p.now("dsr0"), p.now("dsr1")])
                ev_rb = ph.op("scalar", lambda e: e.copy(rec16b[:, :], rec16[:, :]),
                              inc=[("act", 1)], waits=[ev_r])
                nrm_ev = {}
                for hp in range(DC):
                    pb = hp % 2
                    w = [ev_rb, ev_evac[(hp * (T // 512) + T // 512 - 1, 1)]]
                    if hp >= 2:
                        w.append(nrm_ev[hp - 2])

                    def nbc(e, hp=hp, pb=pb):
                        for j in range(2):
                            hh = 2 * hp + j
                            for n in range(NS):
                                i = e.matmul(
                                    pnorm[pb][64 * j:64 * (j + 1),
                                              n * 512:(n + 1) * 512],
                                    oh16[:, hh * 64:(hh + 1) * 64],
                                    rec16b[:, n * 512:(n + 1) * 512],
                                    start=True, stop=True)
                        return i
                    ev_b = ph.op("tensor", nbc, inc=[("pe", 1)], waits=w)
                    nrm_ev[hp] = ph.op("vector", lambda e, hp=hp, pb=pb: e.tensor_mul(
                        avout[:, hp * T:(hp + 1) * T],
                        avout[:, hp * T:(hp + 1) * T], pnorm[pb][:, :]),
                        inc=[("dve", 1)], waits=[ev_b])
                    ev_attn_done.append(nrm_ev[hp])
                ph.run()
        ctx_at2.close()

        if want("avout"):
            dump_sbuf("avout", avout[:, :], [128, DC * T])
        if stop == "avout":
            _finish(p)
            ctx_at3.close()
            return nc, dump_outs

        ctx_xr2 = ExitStack()
        xr2 = sb("xr2", [128, DC * T], F32, ctx_xr2)
        xr2_sl = mk_sl(xr2)

        ph = Phase(p, "reload")
        ev_rel = ph.op("sync", lambda e: e.dma_start(
            xr2[:, :].rearrange("p (c t) -> p c t", c=DC),
            xsp_d[:, :].rearrange("(c p) t -> p c t", p=128)),
            inc=[("drl", 16)], waits=ev_attn_done)
        ph.run()

        with ExitStack() as cw:
            sp_sb = sb("sp_sb", [128, DC * 512], BF16, cw)
            sqw = [sb("sqw%d" % i, [128, 512], BF16, cw) for i in range(2)]
            t32w = [sb("t32w%d" % i, [128, 512], F32, cw) for i in range(2)]
            r_o0 = sb("r_o0", [1, 512], F32, cw)
            r_x0 = sb("r_x0", [1, 512], F32, cw)
            with ExitStack() as c2:
                pw = [ps("pw%d" % i, [128, 512], F32, c2) for i in range(2)]
                psq = ps("psq", [1, 512], F32, c2)
                ph = Phase(p, "wo_tail")
                ev_sp = {}
                ev_sqev = {}
                ev_smm = {}
                ev_add = {}
                ev_o1 = {}
                ev_x0c = {}
                for ts_ in range(NS):
                    for md in range(DC):
                        b = md % 2
                        w = [ev_lwo] + list(ev_attn_done)
                        if (ts_, md - 2) in ev_sp:
                            w.append(ev_sp[(ts_, md - 2)])
                        elif ts_ > 0:
                            w.append(ev_sp[(ts_ - 1, DC - 2 + md)])

                        def wmma(e, ts_=ts_, md=md, b=b):
                            for c in range(DC // 2):
                                i = e.matmul(
                                    pw[b][:, :],
                                    wo_sb[:, c * D + md * 128: c * D + (md + 1) * 128],
                                    avout[:, c * T + ts_ * 512: c * T + ts_ * 512 + 512],
                                    start=(c == 0), stop=False)
                            return i

                        def wmmb(e, ts_=ts_, md=md, b=b):
                            for c in range(DC // 2, DC):
                                i = e.matmul(
                                    pw[b][:, :],
                                    wo_sb[:, c * D + md * 128: c * D + (md + 1) * 128],
                                    avout[:, c * T + ts_ * 512: c * T + ts_ * 512 + 512],
                                    start=False, stop=(c == DC - 1))
                            return i
                        ph.op("tensor", wmma, inc=[("pe", 1)],
                              waits=[x for x in w if x not in ev_attn_done]
                              + ev_attn_done[:DC // 2])
                        ev_mm = ph.op("tensor", wmmb, inc=[("pe", 1)],
                                      waits=ev_attn_done[DC // 2:])
                        w2 = [ev_mm]
                        if ts_ > 0:
                            w2.append(ev_add[(ts_ - 1, md)])
                            w2.append(ev_sqev[(ts_ - 1, md)])
                        ev_t1 = ph.op("vector", lambda e, md=md, b=b: e.tensor_scalar(
                            t32w[b][:, :], pw[b][:, :], 0.1, 8.0, ALU.mult, ALU.min),
                            inc=[("dve", 1)], waits=w2)
                        ev_sp[(ts_, md)] = ph.op(
                            "vector", lambda e, md=md, b=b: e.tensor_scalar_max(
                                sp_sb[:, md * 512:(md + 1) * 512], t32w[b][:, :],
                                -8.0), inc=[("dve", 1)], waits=[ev_t1])
                        w3 = [ev_sp[(ts_, md)]]
                        if (ts_, md - 2) in ev_smm:
                            w3.append(ev_smm[(ts_, md - 2)])
                        elif ts_ > 0 and md < 2:
                            w3.append(ev_smm[(ts_ - 1, DC - 2 + md)])
                        ev_sqev[(ts_, md)] = ph.op(
                            "scalar", lambda e, md=md, b=b: e.activation(
                                sqw[b][:, :], sp_sb[:, md * 512:(md + 1) * 512],
                                AF.Square), inc=[("act", 1)], waits=w3)

                        def emit_sqmm(mk, ts_=ts_):
                            w4 = [ev_sqev[(ts_, mk)]]
                            if mk == 0 and ts_ > 0:
                                w4.append(ev_o1[ts_ - 1])

                            def sqmm(e, mk=mk):
                                lhs = ones_col0 if mk == 0 else ones_col
                                return e.matmul(psq[:, :], lhs[:, :],
                                                sqw[mk % 2][:, :],
                                                start=(mk == 0),
                                                stop=(mk == DC - 1))
                            ev_smm[(ts_, mk)] = ph.op("tensor", sqmm,
                                                      inc=[("pe", 1)], waits=w4)

                        if md >= 1:
                            emit_sqmm(md - 1)
                        if md == DC - 1:
                            emit_sqmm(DC - 1)
                    w5 = [ev_smm[(ts_, DC - 1)]]
                    if ts_ > 0:
                        w5.append(ev_x0c[ts_ - 1])
                    ev_o1[ts_] = ph.op("scalar", lambda e: e.activation(
                        r_o0[:, :], psq[:, :], AF.Ln, bias=c_1eps[0:1, 0:1]),
                        inc=[("act", 1)], waits=w5)
                    ev_o2 = ph.op("scalar", lambda e: e.activation(
                        r_o0[:, :], r_o0[:, :], AF.Exp, scale=0.5),
                        inc=[("act", 1)], waits=[ev_o1[ts_]])
                    w6 = [ev_o2, ev_rel]
                    if ts_ > 0:
                        w6.append(ev_x0c[ts_ - 1])
                    ev_x0 = ph.op("vector", lambda e, ts_=ts_: e.tensor_add(
                        r_x0[:, :], xr2[0:1, ts_ * 512: ts_ * 512 + 512], r_o0[:, :]),
                        inc=[("dve", 1)], waits=w6)
                    for c in range(DC):
                        ev_add[(ts_, c)] = ph.op(
                            "vector", lambda e, ts_=ts_, c=c: e.tensor_add(
                                xr2_sl(c, ts_ * 512, 512), xr2_sl(c, ts_ * 512, 512),
                                sp_sb[:, c * 512:(c + 1) * 512]),
                            inc=[("dve", 1)],
                            waits=[ev_sp[(ts_, c)], ev_x0, ev_rel])
                    ev_x0c[ts_] = ph.op("scalar", lambda e, ts_=ts_: e.copy(
                        xr2[0:1, ts_ * 512: ts_ * 512 + 512], r_x0[:, :]),
                        inc=[("act", 1)], waits=[ev_add[(ts_, 0)]])
                ph.run()
        ctx_at3.close()

        if want("attn"):
            dump_sbuf("attn", xr2[:, :], [128, DC * T])
        if stop == "attn":
            _finish(p)
            ctx_xr2.close()
            return nc, dump_outs

        f2_res = ffn_sublayer("f2", xr2_sl)

        ph = Phase(p, "out")
        for tc in range(T // 512):
            ph.op("sync", lambda e, tc=tc: e.dma_start(
                bass.AP(yT_d, tc * 512, [[T, 128], [128 * T, DC], [1, 512]]),
                bass.AP(xr2, tc * 512, [[DC * T, 128], [T, DC], [1, 512]])),
                inc=[("dou", 16)],
                waits=[f2_res[(tc, 0)], f2_res[(tc, 1)]])
        ph.run()
        _finish(p)
        ctx_xr2.close()

    return nc, dump_outs


def _finish(p):
    ph = Phase(p, "final_wait")
    sems = ["dxp", "dgp", "dgd", "dg0", "dg1", "dh0", "dh1", "dhw0", "dhw1",
            "dgw", "dsp", "drl", "dou", "dsr0", "dsr1", "dw10", "dw11"]
    ph.op("gpsimd", lambda e: e.nop(), waits=[p.now(s) for s in sems])
    ph.run()


def host_inputs(inputs, cfg=CFG):
    E, FC, DC = cfg.E, cfg.FC, cfg.DC
    x = np.asarray(inputs["L0_attn_x"])
    base = {
        "WrT": np.ascontiguousarray(np.asarray(inputs["L0_ffn_router_w"]).T),
        "rb": np.ascontiguousarray(
            np.asarray(inputs["L0_ffn_router_b"]).reshape(1, E)),
        "WeT": np.ascontiguousarray(
            np.asarray(inputs["L0_ffn_expert_w"]).transpose(0, 2, 1)),
        "Be": np.ascontiguousarray(np.asarray(inputs["L0_ffn_expert_b"])),
        "W1T": np.ascontiguousarray(np.asarray(inputs["L0_attn_W1"]).T),
        "b1T": np.ascontiguousarray(
            np.asarray(inputs["L0_attn_b1"]).reshape(FC, 128).T),
        "W2T": np.ascontiguousarray(np.asarray(inputs["L0_attn_W2"]).T),
        "b2T": np.ascontiguousarray(
            np.asarray(inputs["L0_attn_b2"]).reshape(DC, 128).T),
        "WqT": np.ascontiguousarray(np.asarray(inputs["L0_attn_Wq"]).T),
        "WkT": np.ascontiguousarray(np.asarray(inputs["L0_attn_Wk"]).T),
        "WvT": np.ascontiguousarray(np.asarray(inputs["L0_attn_Wv"]).T),
        "WoT": np.ascontiguousarray(np.asarray(inputs["L0_attn_Wo"]).T),
    }
    maps = []
    for c in range(x.shape[0]):
        m = dict(base)
        m["xT"] = np.ascontiguousarray(x[c].T)
        maps.append(m)
    return maps


def kernel(**inputs):
    nc, _ = build(CFG)
    in_maps = host_inputs(inputs, CFG)
    res = run_bass_kernel_spmd(nc, in_maps, core_ids=list(range(8)))
    out = np.stack([np.asarray(r["yT"]).T for r in res.results], axis=0)
    return np.ascontiguousarray(out.astype(np.float32))


# version 18
# speedup vs baseline: 1.0119x; 1.0119x over previous
import numpy as np
from contextlib import ExitStack

import concourse.bass as bass
import concourse.mybir as mybir
from concourse.bass_utils import run_bass_kernel_spmd

F32 = mybir.dt.float32
BF16 = mybir.dt.bfloat16
AF = mybir.ActivationFunctionType
ALU = mybir.AluOpType
AX = mybir.AxisListType

EPS = 1e-6
LN_EPS = 1e-5


class Cfg:
    def __init__(self, T=2048, D=1024, DFF=2048, H=16, E=8):
        assert D % 512 == 0 and DFF % 128 == 0 and T % 1024 == 0
        assert D // H == 64
        self.T, self.D, self.DFF, self.H, self.E = T, D, DFF, H, E
        self.DC = D // 128
        self.FC = DFF // 128
        self.TC = T // 128
        self.NSL = T // 512
        self.SL = 512
        self.TPS = self.SL // 128


CFG = Cfg()
NDUM = (2, 1)
ENGS = ("tensor", "vector", "scalar", "gpsimd", "sync")
_blk_ctr = [0]


class Prog:
    def __init__(self, nc, ctx):
        self.nc = nc
        self.ctx = ctx
        self.sems = {}
        self.cnt = {}

    def sem(self, name):
        if name not in self.sems:
            self.sems[name] = self.ctx.enter_context(self.nc.semaphore(name))
            self.cnt[name] = 0
        return self.sems[name]

    def now(self, name):
        self.sem(name)
        return (name, self.cnt[name])


class Phase:

    def __init__(self, p, name=""):
        self.p = p
        _blk_ctr[0] += 1
        self.name = "%s_%d" % (name, _blk_ctr[0])
        self.ops = {e: [] for e in ENGS}

    def op(self, eng, fn, inc=(), waits=()):
        p = self.p
        evs = []
        for sname, by in inc:
            p.sem(sname)
            p.cnt[sname] += by
            evs.append((sname, p.cnt[sname]))
        wd = {}
        for wv in waits:
            if wv is None:
                continue
            sname, val = wv
            wd[sname] = max(wd.get(sname, 0), val)
        self.ops[eng].append((sorted(wd.items()), fn, [(s, b) for s, b in inc]))
        return evs[0] if evs else None

    def run(self):
        p = self.p

        def mk_body(oplist):
            def body(eng):
                for waits, fn, incs in oplist:
                    for sname, val in waits:
                        if val > 0:
                            eng.wait_ge(p.sems[sname], val)
                    inst = fn(eng)
                    for sname, by in incs:
                        inst.then_inc(p.sems[sname], by)
            return body

        with p.nc.Block(self.name) as blk:
            for e in ENGS:
                if self.ops[e]:
                    getattr(blk, e)(mk_body(self.ops[e]))


def build(cfg=CFG, dumps=(), stop=None):
    T, D, DFF, H, E = cfg.T, cfg.D, cfg.DFF, cfg.H, cfg.E
    DC, FC, TC = cfg.DC, cfg.FC, cfg.TC
    NSL, SL, TPS = cfg.NSL, cfg.SL, cfg.TPS
    KC = T // 128
    NS = T // 512
    QH = T // 1024

    nc = bass.Bass("TRN2", target_bir_lowering=False)

    xT_d = nc.dram_tensor("xT", [D, T], F32, kind="ExternalInput")
    WrT_d = nc.dram_tensor("WrT", [D, E], F32, kind="ExternalInput")
    rb_d = nc.dram_tensor("rb", [1, E], F32, kind="ExternalInput")
    WeT_d = nc.dram_tensor("WeT", [E, D, D], F32, kind="ExternalInput")
    Be_d = nc.dram_tensor("Be", [E, D], F32, kind="ExternalInput")
    W1T_d = nc.dram_tensor("W1T", [D, DFF], F32, kind="ExternalInput")
    b1T_d = nc.dram_tensor("b1T", [128, FC], F32, kind="ExternalInput")
    W2T_d = nc.dram_tensor("W2T", [DFF, D], F32, kind="ExternalInput")
    b2T_d = nc.dram_tensor("b2T", [128, DC], F32, kind="ExternalInput")
    WqT_d = nc.dram_tensor("WqT", [D, D], F32, kind="ExternalInput")
    WkT_d = nc.dram_tensor("WkT", [D, D], F32, kind="ExternalInput")
    WvT_d = nc.dram_tensor("WvT", [D, D], F32, kind="ExternalInput")
    WoT_d = nc.dram_tensor("WoT", [D, D], F32, kind="ExternalInput")
    yT_d = nc.dram_tensor("yT", [D, T], F32, kind="ExternalOutput")

    WeC_d = nc.dram_tensor("WeC", [E, D, D], BF16, kind="Internal")
    xsp_d = nc.dram_tensor("xsp", [D, T], F32, kind="Internal")

    dump_outs = {}

    def want(tag):
        return tag in dumps

    with ExitStack() as ctx0:
        p = Prog(nc, ctx0)

        def sb(name, shape, dtype, c=None, side="left"):
            return (c or ctx0).enter_context(
                nc.sbuf_tensor(name, shape, dtype, side=side))

        def ps(name, shape, dtype, c=None, side="left"):
            return (c or ctx0).enter_context(
                nc.psum_tensor(name, shape, dtype, side=side))

        ident = sb("ident", [128, 128], BF16)
        ones_col = sb("ones_col", [128, 1], BF16)
        ones_col0 = sb("ones_col0", [128, 1], BF16)
        mean_col = sb("mean_col", [128, 1], BF16)
        ones_row = sb("ones_row", [1, 128], BF16)
        ones_rowP = sb("ones_rowP", [65, 128], BF16)
        sel_sb = sb("sel_sb", [8, E * 128], BF16)
        oh16 = sb("oh16", [H, H * 64], BF16)
        b1T = sb("b1T_s", [128, FC], F32)
        b2T = sb("b2T_s", [128, DC], F32)
        Be_bf = sb("Be_bf", [8, D], BF16)
        wr32 = sb("wr32", [128, DC * E], F32)
        wr_hi = sb("wr_hi", [128, DC * E], BF16)
        wr_lo = sb("wr_lo", [128, DC * E], BF16)
        rb32 = sb("rb32", [1, E], F32)
        rb_hi = sb("rb_hi", [1, E], BF16)
        rb_lo = sb("rb_lo", [1, E], BF16)
        negB = sb("negB", [128, 1], F32)
        c_lneps = sb("c_lneps", [128, 1], F32)
        c_1eps = sb("c_1eps", [128, 1], F32)
        c_neg1 = sb("c_neg1", [128, 1], F32)
        c_pos1 = sb("c_pos1", [128, 1], F32)

        ctx_xr = ExitStack()
        xr = sb("xr", [128, DC * T], F32, ctx_xr)

        def mk_sl(t):
            def sl(c, t0=0, tn=None):
                tn = T if tn is None else tn
                return t[:, c * T + t0: c * T + t0 + tn]
            return sl
        xr_sl = mk_sl(xr)

        ph = Phase(p, "prologue")
        ev_xT = ph.op("sync", lambda e: e.dma_start(
            xr[:, :].rearrange("p (c t) -> p c t", c=DC),
            xT_d[:, :].rearrange("(c p) t -> p c t", p=128)), inc=[("dxp", 16)])
        ph.op("sync", lambda e: e.dma_start(
            wr32[:, :].rearrange("p (c e) -> p c e", c=DC),
            WrT_d[:, :].rearrange("(c p) e -> p c e", p=128)), inc=[("dxp", 16)])
        ph.op("sync", lambda e: e.dma_start(rb32[:, :], rb_d[:, :]), inc=[("dxp", 16)])
        ph.op("sync", lambda e: e.dma_start(b1T[:, :], b1T_d[:, :]), inc=[("dxp", 16)])
        ph.op("sync", lambda e: e.dma_start(b2T[:, :], b2T_d[:, :]), inc=[("dxp", 16)])
        ev_be = ph.op("gpsimd", lambda e: e.dma_start(Be_bf[:, :], Be_d[:, :]), inc=[("dgp", 16)])
        ev = ph.op("gpsimd", lambda e: e.memset(ident[:, :], 0.0), inc=[("pl", 1)])
        ph.op("gpsimd", lambda e: e.affine_select(
            out=ident[:, :], in_=ident[:, :], compare_op=ALU.not_equal,
            fill=1.0, base=0, pattern=[[-1, 128]], channel_multiplier=1),
            inc=[("pl", 1)], waits=[ev])
        ph.op("gpsimd", lambda e: e.memset(ones_col[:, :], 1.0), inc=[("pl", 1)])
        ev = ph.op("gpsimd", lambda e: e.memset(ones_col0[:, :], 1.0), inc=[("pl", 1)])
        ph.op("gpsimd", lambda e: e.memset(ones_col0[0:1, :], 0.0),
              inc=[("pl", 1)], waits=[ev])
        ph.op("gpsimd", lambda e: e.memset(mean_col[:, :], 1.0 / D), inc=[("pl", 1)])
        ph.op("gpsimd", lambda e: e.memset(ones_row[:, :], 1.0), inc=[("pl", 1)])
        ph.op("gpsimd", lambda e: e.memset(ones_rowP[:, :], 1.0), inc=[("pl", 1)])
        ph.op("gpsimd", lambda e: e.memset(c_lneps[:, :], LN_EPS), inc=[("pl", 1)])
        ph.op("gpsimd", lambda e: e.memset(c_1eps[:, :], 1.0 + EPS), inc=[("pl", 1)])
        ph.op("gpsimd", lambda e: e.memset(c_neg1[:, :], -1.0), inc=[("pl", 1)])
        ph.op("gpsimd", lambda e: e.memset(c_pos1[:, :], 1.0), inc=[("pl", 1)])
        ev = ph.op("gpsimd", lambda e: e.memset(sel_sb[:, :], 0.0), inc=[("pl", 1)])
        ph.op("gpsimd", lambda e: e.affine_select(
            out=bass.AP(sel_sb, 0, [[E * 128, 8], [128, E], [1, 128]]),
            in_=bass.AP(sel_sb, 0, [[E * 128, 8], [128, E], [1, 128]]),
            compare_op=ALU.not_equal, fill=1.0, base=0,
            pattern=[[-1, E], [0, 128]], channel_multiplier=1),
            inc=[("pl", 1)], waits=[ev])
        ev = ph.op("gpsimd", lambda e: e.memset(oh16[:, :], 0.0), inc=[("pl", 1)])
        ph.op("gpsimd", lambda e: e.affine_select(
            out=bass.AP(oh16, 0, [[H * 64, H], [64, H], [1, 64]]),
            in_=bass.AP(oh16, 0, [[H * 64, H], [64, H], [1, 64]]),
            compare_op=ALU.not_equal, fill=1.0, base=0,
            pattern=[[-1, H], [0, 64]], channel_multiplier=1),
            inc=[("pl", 1)], waits=[ev])
        ph.run()

        ph = Phase(p, "loads_wait")
        wall = [p.now("dxp"), p.now("dgp"), p.now("pl")]
        for e_ in ENGS:
            ph.op(e_, lambda e: e.nop(), waits=wall)
        ph.run()

        ph = Phase(p, "router_split")
        ev_hi = ph.op("scalar", lambda e: e.copy(wr_hi[:, :], wr32[:, :]),
                      inc=[("act", 1)])
        ph.op("vector", lambda e: e.tensor_sub(wr_lo[:, :], wr32[:, :], wr_hi[:, :]),
              inc=[("dve", 1)], waits=[ev_hi])
        ev_hi = ph.op("scalar", lambda e: e.copy(rb_hi[:, :], rb32[:, :]),
                      inc=[("act", 1)])
        ph.op("vector", lambda e: e.tensor_sub(rb_lo[:, :], rb32[:, :], rb_hi[:, :]),
              inc=[("dve", 1)], waits=[ev_hi])
        ph.run()

        def dump_sbuf(tag, ap, shape):
            d = nc.dram_tensor("dump_" + tag, list(shape), F32, kind="ExternalOutput")
            dump_outs[tag] = d
            ph = Phase(p, "dump_" + tag)
            ph.op("gpsimd", lambda e: e.dma_start(d[:, :], ap), inc=[("dgd", 16)])
            ph.op("gpsimd", lambda e: e.nop(), waits=[p.now("dgd")])
            ph.run()

        if want("xr0"):
            dump_sbuf("xr0", xr[:, :], [128, DC * T])

        def layernorm(tag, xcur_sl, out_hi, out_lo, chunk_consumer=None,
                      after_slice=None, lo_war_fn=None):
            with ExitStack() as c1:
                xbf = [sb("xbf%s%d" % (tag, i), [128, SL], BF16, c1)
                       for i in range(2)]
                sqb = [sb("sqb%s%d" % (tag, i), [128, SL], BF16, c1)
                       for i in range(2)]
                m_row = sb("m_row" + tag, [1, T], F32, c1)
                v2_row = sb("v2_row" + tag, [1, T], F32, c1)
                ra = sb("ra_row" + tag, [1, T], F32, c1)
                m_bf = sb("m_bf" + tag, [1, T], BF16, c1)
                r_bf = sb("r_bf" + tag, [1, T], BF16, c1)
                t32 = [sb("t32%s%d" % (tag, i), [128, SL], F32, c1)
                       for i in range(3)]
                with ExitStack() as c2:
                    pm = [ps("pm%s%d" % (tag, i), [1, SL], F32, c2)
                          for i in range(2)]
                    pv = [ps("pv%s%d" % (tag, i), [1, SL], F32, c2)
                          for i in range(2)]
                    pmb = ps("pmb" + tag, [128, SL], F32, c2)
                    prb = ps("prb" + tag, [128, SL], F32, c2) \
                        if out_lo is None else None
                    ph = Phase(p, "ln_" + tag)
                    ev_mm = {}
                    ev_cs = {}
                    ev_hi = {}
                    ev_lo = {}
                    ev_ap = {}
                    pb_readers = {}
                    ev_mv = {}
                    for s in range(NSL):
                        t0 = s * SL
                        sp = s % 2
                        cols = slice(t0, t0 + SL)
                        for c in range(DC):
                            g = s * DC + c
                            b = g % 2
                            w = [ev_mm[g - 2]] if (g - 2) in ev_mm else []
                            ev_cast = ph.op("scalar", lambda e, c=c, b=b, t0=t0:
                                            e.copy(xbf[b][:, :],
                                                   xcur_sl(c, t0, SL)),
                                            inc=[("act", 1)], waits=w)
                            ev_sq = ph.op("scalar", lambda e, c=c, b=b, t0=t0:
                                          e.activation(sqb[b][:, :],
                                                       xcur_sl(c, t0, SL),
                                                       AF.Square),
                                          inc=[("act", 1)], waits=w)
                            ev_cs[g] = (ev_cast, ev_sq)
                            wm_ = list(ev_cs[g])
                            if c == 0 and (s - 2) in ev_mv:
                                wm_ += ev_mv[s - 2]

                            def mm(e, sp=sp, c=c, b=b):
                                e.matmul(pm[sp][:, :], mean_col[:, :],
                                         xbf[b][:, :],
                                         start=(c == 0), stop=(c == DC - 1))
                                return e.matmul(pv[sp][:, :], mean_col[:, :],
                                                sqb[b][:, :],
                                                start=(c == 0),
                                                stop=(c == DC - 1))
                            ev_mm[g] = ph.op("tensor", mm, inc=[("pe", 1)],
                                             waits=wm_)
                        ev = ev_mm[s * DC + DC - 1]
                        wst = [ev]
                        if (s - 2) in pb_readers:
                            wst.append(pb_readers[s - 2][-1])
                        ev_m = ph.op("scalar", lambda e, sp=sp, cols=cols: e.copy(
                            m_row[:, cols], pm[sp][:, :]), inc=[("act", 1)],
                            waits=wst)
                        ev_v2 = ph.op("scalar", lambda e, sp=sp, cols=cols: e.copy(
                            v2_row[:, cols], pv[sp][:, :]), inc=[("act", 1)],
                            waits=wst)
                        ev_mv[s] = [ev_m, ev_v2]
                        ev_msq = ph.op("vector", lambda e, cols=cols: e.tensor_mul(
                            ra[:, cols], m_row[:, cols], m_row[:, cols]),
                            inc=[("dve", 1)], waits=[ev_m])
                        ev_var = ph.op("vector", lambda e, cols=cols: e.tensor_sub(
                            v2_row[:, cols], v2_row[:, cols], ra[:, cols]),
                            inc=[("dve", 1)], waits=[ev_msq, ev_v2])
                        ev_ln = ph.op("scalar", lambda e, cols=cols: e.activation(
                            ra[:, cols], v2_row[:, cols], AF.Ln,
                            bias=c_lneps[0:1, 0:1]), inc=[("act", 1)],
                            waits=[ev_var])
                        ev_rs = ph.op("scalar", lambda e, cols=cols: e.activation(
                            ra[:, cols], ra[:, cols], AF.Exp, scale=-0.5),
                            inc=[("act", 1)], waits=[ev_ln])
                        ev_mbf = ph.op("scalar", lambda e, cols=cols: e.copy(
                            m_bf[:, cols], m_row[:, cols]), inc=[("act", 1)],
                            waits=[ev_m])
                        ev_rbf = ph.op("scalar", lambda e, cols=cols: e.copy(
                            r_bf[:, cols], ra[:, cols]), inc=[("act", 1)],
                            waits=[ev_rs])
                        wbc = [ev_mbf, ev_rbf] + pb_readers.get(s - 1, [])

                        def bc(e, cols=cols):
                            i = e.matmul(pmb[:, :], ones_row[:, :],
                                         m_bf[:, cols], start=True, stop=True)
                            if prb is not None:
                                i = e.matmul(prb[:, :], ones_row[:, :],
                                             r_bf[:, cols], start=True,
                                             stop=True)
                            return i
                        ev_bc = ph.op("tensor", bc, inc=[("pe", 1)], waits=wbc)
                        wlo = lo_war_fn(s) if lo_war_fn is not None else []
                        rdrs = []
                        for c in range(DC):
                            g = s * DC + c
                            b = g % 3
                            w = [ev_bc]
                            if (g - 3) in ev_ap:
                                w.append(ev_ap[g - 3])
                                w.append(ev_hi[g - 3])
                            ev_sub = ph.op("vector",
                                           lambda e, c=c, b=b, t0=t0: e.tensor_sub(
                                               t32[b][:, :], xcur_sl(c, t0, SL),
                                               pmb[:, :]),
                                           inc=[("dve", 1)], waits=w)
                            rdrs.append(ev_sub)
                            if out_lo is None:
                                ev_p = ph.op(
                                    "vector", lambda e, c=c, b=b, t0=t0: e.tensor_mul(
                                        out_hi[:, c * T + t0: c * T + t0 + SL],
                                        t32[b][:, :], prb[:, :]),
                                    inc=[("dve", 1)], waits=[ev_sub])
                                ev_hi[g] = ev_p
                                rdrs[-1] = ev_p
                                ev_ap[g] = ev_p
                            else:
                                ev_hi[g] = ph.op(
                                    "scalar", lambda e, c=c, b=b, t0=t0: e.copy(
                                        out_hi[:, c * T + t0: c * T + t0 + SL],
                                        t32[b][:, :]), inc=[("act", 1)],
                                    waits=[ev_sub])
                                ev_lo[g] = ph.op(
                                    "vector", lambda e, c=c, s=s, b=b, t0=t0:
                                    e.tensor_sub(
                                        out_lo(c, s), t32[b][:, :],
                                        out_hi[:, c * T + t0: c * T + t0 + SL]),
                                    inc=[("dve", 1)], waits=[ev_hi[g]] + wlo)
                                ev_ap[g] = ev_lo[g]
                            if chunk_consumer is not None:
                                chunk_consumer(ph, c, s, ev_hi[g], ev_lo.get(g))
                        pb_readers[s] = rdrs
                        if after_slice is not None:
                            after_slice(ph, s, ev_rbf, r_bf)
                    ph.run()
            return ev_hi, ev_ap

        ffn_state = {"cached": False}

        def ffn_sublayer(tag, xcur_sl):
            with ExitStack() as cf:
                xnhi = sb("xnhi" + tag, [128, DC * T], BF16, cf)
                xnlo_ctx = ExitStack()
                xnlo = [sb("xnlo%s%d" % (tag, i), [128, DC * SL], BF16,
                           xnlo_ctx, side="right") for i in range(2)]

                def lo_sl(c, s):
                    return xnlo[s % 2][:, c * SL:(c + 1) * SL]
                wm_tok = sb("wm_tok" + tag, [128, TC * E], BF16, cf)
                wm_tokr = sb("wm_tokr" + tag, [128, TC * E], BF16, cf)
                wmT = sb("wmT" + tag, [8, T], BF16, cf)
                wmTr = sb("wmTr" + tag, [8, T], BF16, cf)
                rT128 = sb("rT128" + tag, [128, TC], F32, cf)
                rmax_t = sb("rmax_t" + tag, [128, TC], F32, cf)
                sume_t = sb("sume_t" + tag, [128, TC], F32, cf)
                scal_s = sb("scal_s" + tag, [128, TC * E], F32, cf)
                scr_s = sb("scr_s" + tag, [128, TC * E], F32, cf)
                scr_m = sb("scr_m" + tag, [128, TC * E], F32, cf)
                maskb = sb("maskb" + tag, [128, E * T], BF16, cf)
                rt_ctx = ExitStack()
                rt_last = {}
                plg_box = {}
                ptr = ps("ptr" + tag, [128, SL], BF16, rt_ctx)
                pbc = ps("pbc" + tag, [128, SL], F32, rt_ctx)
                ev_wmT = {}
                ev_wmTr = {}
                maskb_ev = {}
                mk_prev = {}

                def router_consumer(ph, c, s, ev_hi_e, ev_lo_e):
                    if "plg" not in plg_box:
                        plg_box["plg"] = ps("plg" + tag, [128, TC * E], F32,
                                            rt_ctx, side="right")
                    plg = plg_box["plg"]
                    tcs = list(range(s * TPS, (s + 1) * TPS))

                    def rmm(e, c=c, s=s, tcs=tuple(tcs)):
                        i = None
                        for tc in tcs:
                            tt = (tc - s * TPS) * 128
                            for j, (lh, off, rh) in enumerate(
                                    ((xnhi, c * T + tc * 128, wr_hi),
                                     (xnhi, c * T + tc * 128, wr_lo),
                                     (xnlo[s % 2], c * SL + tt, wr_hi))):
                                i = e.matmul(
                                    plg[:, tc * E:(tc + 1) * E],
                                    lh[:, off: off + 128],
                                    rh[:, c * E:(c + 1) * E],
                                    start=(c == 0 and s == 0
                                           and tc == tcs[0] and j == 0),
                                    stop=False)
                            if c == DC - 1:
                                e.matmul(plg[:, tc * E:(tc + 1) * E], ones_row[:, :],
                                         rb_hi[:, :], start=False, stop=False)
                                i = e.matmul(plg[:, tc * E:(tc + 1) * E],
                                             ones_row[:, :], rb_lo[:, :],
                                             start=False,
                                             stop=(s == NSL - 1
                                                   and tc == tcs[-1]))
                        return i
                    rt_last[s] = ph.op("tensor", rmm, inc=[("pe", 1)],
                                       waits=[ev_hi_e, ev_lo_e])

                def bc3(t, s):
                    return bass.AP(t, s * TPS, [[TC, 128], [1, TPS], [0, E]])

                def ap3(t, s):
                    return bass.AP(t, s * TPS * E,
                                   [[TC * E, 128], [E, TPS], [1, E]])

                def after_slice(ph, s, ev_rbf, r_bf):
                    plg = plg_box["plg"]
                    ev_lg = rt_last[s]
                    cl, ch = s * TPS, (s + 1) * TPS
                    wrt = [ev_rbf]
                    if (s - 1) in ev_wmTr:
                        wrt.append(ev_wmTr[s - 1])

                    def trr(e, s=s, r_bf=r_bf):
                        for k in range(TPS):
                            i = e.transpose(ptr[:, 2 * k:2 * k + 1],
                                            r_bf[0:1,
                                                 s * SL + k * 128:
                                                 s * SL + (k + 1) * 128],
                                            ident[0:1, 0:1])
                        return i
                    ev_rt = ph.op("tensor", trr, inc=[("pe", 1)], waits=wrt)
                    ev_rtc = ph.op("scalar", lambda e, cl=cl: e.copy(
                        rT128[:, cl:cl + TPS],
                        bass.AP(ptr, 0, [[SL, 128], [2, TPS]])),
                        inc=[("act", 1)], waits=[ev_rt])
                    ev_sc = ph.op("vector", lambda e, s=s: e.tensor_tensor(
                        ap3(scal_s, s), ap3(plg, s), bc3(rT128, s),
                        ALU.mult), inc=[("dve", 1)], waits=[ev_lg, ev_rtc])
                    ev1 = ph.op("vector", lambda e, s=s, cl=cl, ch=ch:
                                e.tensor_reduce(rmax_t[:, cl:ch], ap3(scal_s, s),
                                                axis=AX.X, op=ALU.max),
                                inc=[("dve", 1)], waits=[ev_sc])
                    ev2 = ph.op("vector", lambda e, s=s: e.tensor_tensor(
                        ap3(scr_s, s), ap3(scal_s, s), bc3(rmax_t, s),
                        ALU.subtract), inc=[("dve", 1)], waits=[ev1])
                    ev3 = ph.op("vector", lambda e, s=s: e.tensor_tensor(
                        ap3(scr_m, s), ap3(scal_s, s), bc3(rmax_t, s),
                        ALU.is_equal), inc=[("dve", 1)], waits=[ev1])
                    ev4 = ph.op("scalar", lambda e, s=s: e.activation(
                        scr_s[:, s * TPS * E:(s + 1) * TPS * E],
                        scr_s[:, s * TPS * E:(s + 1) * TPS * E], AF.Exp),
                        inc=[("act", 1)], waits=[ev2])
                    ev5 = ph.op("vector", lambda e, s=s, cl=cl, ch=ch:
                                e.tensor_reduce(sume_t[:, cl:ch], ap3(scr_s, s),
                                                axis=AX.X, op=ALU.add),
                                inc=[("dve", 1)], waits=[ev4])
                    ev6 = ph.op("vector", lambda e, cl=cl, ch=ch: e.reciprocal(
                        sume_t[:, cl:ch], sume_t[:, cl:ch]),
                        inc=[("dve", 1)], waits=[ev5])
                    ev_wm = ph.op("vector", lambda e, s=s: e.scalar_tensor_tensor(
                        ap3(wm_tok, s), ap3(scr_m, s), 0.5, bc3(sume_t, s),
                        ALU.mult, ALU.mult),
                        inc=[("dve", 1)], waits=[ev3, ev6])
                    ev_wmr = ph.op("vector", lambda e, s=s: e.tensor_tensor(
                        ap3(wm_tokr, s), ap3(wm_tok, s), bc3(rT128, s),
                        ALU.mult), inc=[("dve", 1)], waits=[ev_wm])
                    def tr(e, s=s, src=wm_tok):
                        for k in range(TPS):
                            i = e.transpose(ptr[0:8, k * 128:(k + 1) * 128],
                                            src[:, (s * TPS + k) * E:
                                                (s * TPS + k + 1) * E],
                                            ident[:, :])
                        return i
                    ev_t = ph.op("tensor", tr, inc=[("pe", 1)],
                                 waits=[ev_wm, ev_rtc])
                    ev_wmT[s] = ph.op("scalar", lambda e, s=s: e.copy(
                        wmT[:, s * SL:(s + 1) * SL], ptr[0:8, :]),
                        inc=[("act", 1)], waits=[ev_t])

                    def tr2(e, s=s, src=wm_tokr):
                        for k in range(TPS):
                            i = e.transpose(ptr[0:8, k * 128:(k + 1) * 128],
                                            src[:, (s * TPS + k) * E:
                                                (s * TPS + k + 1) * E],
                                            ident[:, :])
                        return i
                    ev_t2 = ph.op("tensor", tr2, inc=[("pe", 1)],
                                  waits=[ev_wmr, ev_wmT[s]])
                    ev_wmTr[s] = ph.op("scalar", lambda e, s=s: e.copy(
                        wmTr[:, s * SL:(s + 1) * SL], ptr[0:8, :]),
                        inc=[("act", 1)], waits=[ev_t2])
                    for e_i in range(E):
                        idx = s * E + e_i
                        w = [ev_wmTr[s]]
                        if (idx - 1) in mk_prev:
                            w.append(mk_prev[idx - 1])

                        def bmm(e, e_i=e_i, s=s):
                            return e.matmul(pbc[:, :],
                                            sel_sb[:, e_i * 128:(e_i + 1) * 128],
                                            wmTr[:, s * SL:(s + 1) * SL],
                                            start=True, stop=True)
                        ev_mmb = ph.op("tensor", bmm, inc=[("pe", 1)], waits=w)
                        mk_prev[idx] = ph.op(
                            "scalar", lambda e, e_i=e_i, s=s: e.copy(
                                maskb[:, e_i * T + s * SL:
                                      e_i * T + (s + 1) * SL], pbc[:, :]),
                            inc=[("act", 1)], waits=[ev_mmb])
                        maskb_ev[(s, e_i)] = mk_prev[idx]

                ev_hi_ln, _ = layernorm(
                    tag, xcur_sl, xnhi, lo_sl,
                    chunk_consumer=router_consumer, after_slice=after_slice,
                    lo_war_fn=lambda s: [rt_last[s - 2]]
                    if (s - 2) in rt_last else [])
                rt_ctx.close()
                xnlo_ctx.close()
                if want("ln" + tag):
                    dump_sbuf("ln" + tag, xnhi[:, :], [128, DC * T])
                if want("wm" + tag):
                    dump_sbuf("wm" + tag, wm_tok[:, :], [128, TC * E])

                wbuf = [sb("wbuf%s%d" % (tag, i), [128, DC * D], BF16, cf)
                        for i in range(2)]
                xnm = [sb("xnm%s%d" % (tag, i), [128, DC * 512], BF16, cf)
                       for i in range(2)]
                with ExitStack() as c2:
                    py = ps("py" + tag, [128, DC * 512], F32, c2)
                    ph = Phase(p, "experts_" + tag)
                    cached = ffn_state["cached"]
                    load_ev = {}
                    wb_ev = {}
                    mm_ev = {}
                    res_ev = {}
                    TQn = T // 512
                    nli = TQn * E

                    def emit_load(li):
                        tc_, e_ = divmod(li, E)
                        w = []
                        if li >= 2:
                            w.append(mm_ev[li - 2])
                            if (li - 2) in wb_ev:
                                w.append(wb_ev[li - 2])
                        if not cached and tc_ == 0:
                            load_ev[li] = ph.op(
                                "gpsimd", lambda e, e_=e_, li=li: e.dma_start(
                                    wbuf[li % 2][:, :].rearrange(
                                        "p (c n) -> p c n", c=DC),
                                    WeT_d[e_].rearrange("(c p) n -> p c n", p=128)),
                                inc=[("dg%d" % (li % 2), 16)], waits=w)
                            wb_ev[li] = ph.op("sync", lambda e, e_=e_, li=li: e.dma_start(
                                WeC_d[e_].rearrange("(c p) n -> p c n", p=128),
                                wbuf[li % 2][:, :].rearrange(
                                    "p (c n) -> p c n", c=DC)),
                                inc=[("dhw%d" % (li % 2), 16)], waits=[load_ev[li]])
                        else:
                            load_ev[li] = ph.op(
                                "sync", lambda e, e_=e_, li=li: e.dma_start(
                                    wbuf[li % 2][:, :].rearrange(
                                        "p (c n) -> p c n", c=DC),
                                    WeC_d[e_].rearrange("(c p) n -> p c n", p=128)),
                                inc=[("dh%d" % (li % 2), 16)], waits=w)

                    emit_load(0)
                    emit_load(1)
                    for tc in range(TQn):
                        for e_i in range(E):
                            li = tc * E + e_i
                            w = [maskb_ev[(tc, e_i)],
                                 ev_hi_ln[(tc + 1) * DC - 1]]
                            if li >= 2:
                                w.append(mm_ev[li - 2])

                            def vmask(e, tc=tc, e_i=e_i, li=li):
                                for c in range(DC):
                                    i = e.tensor_mul(
                                        xnm[li % 2][:, c * 512:(c + 1) * 512],
                                        xnhi[:, c * T + tc * 512:
                                             c * T + tc * 512 + 512],
                                        maskb[:, e_i * T + tc * 512:
                                              e_i * T + tc * 512 + 512])
                                return i
                            ev_vm = ph.op("vector", vmask, inc=[("dve", 1)], waits=w)
                            w = [ev_vm, load_ev[li]]
                            if tc > 0 and e_i == 0:
                                w.append(res_ev[(tc - 1, 0)])

                            def emma(e, tc=tc, e_i=e_i, li=li):
                                for m in range(DC // 2):
                                    for c in range(DC):
                                        i = e.matmul(
                                            py[:, m * 512:(m + 1) * 512],
                                            wbuf[li % 2][:, c * D + m * 128:
                                                         c * D + (m + 1) * 128],
                                            xnm[li % 2][:, c * 512:(c + 1) * 512],
                                            start=(e_i == 0 and c == 0), stop=False)
                                return i

                            def emmb(e, tc=tc, e_i=e_i, li=li):
                                for m in range(DC // 2, DC):
                                    for c in range(DC):
                                        i = e.matmul(
                                            py[:, m * 512:(m + 1) * 512],
                                            wbuf[li % 2][:, c * D + m * 128:
                                                         c * D + (m + 1) * 128],
                                            xnm[li % 2][:, c * 512:(c + 1) * 512],
                                            start=(e_i == 0 and c == 0), stop=False)
                                return i
                            ph.op("tensor", emma, inc=[("pe", 1)], waits=w)
                            wb_ = [] if not (tc > 0 and e_i == 0) \
                                else [res_ev[(tc - 1, 1)]]
                            mm_ev[li] = ph.op("tensor", emmb, inc=[("pe", 1)],
                                              waits=wb_)
                            if li + 2 < nli:
                                emit_load(li + 2)

                        def bmm2a(e, tc=tc):
                            for m in range(DC // 2):
                                i = e.matmul(py[:, m * 512:(m + 1) * 512],
                                             Be_bf[:, m * 128:(m + 1) * 128],
                                             wmT[:, tc * 512: tc * 512 + 512],
                                             start=False, stop=True)
                            return i

                        def bmm2b(e, tc=tc):
                            for m in range(DC // 2, DC):
                                i = e.matmul(py[:, m * 512:(m + 1) * 512],
                                             Be_bf[:, m * 128:(m + 1) * 128],
                                             wmT[:, tc * 512: tc * 512 + 512],
                                             start=False, stop=True)
                            return i
                        ev_ba = ph.op("tensor", bmm2a, inc=[("pe", 1)],
                                      waits=[ev_wmT[tc]])
                        ev_bb = ph.op("tensor", bmm2b, inc=[("pe", 1)], waits=[])

                        def radda(e, tc=tc):
                            for m in range(DC // 2):
                                i = e.tensor_add(
                                    xcur_sl(m, tc * 512, 512),
                                    xcur_sl(m, tc * 512, 512),
                                    py[:, m * 512:(m + 1) * 512])
                            return i

                        def raddb(e, tc=tc):
                            for m in range(DC // 2, DC):
                                i = e.tensor_add(
                                    xcur_sl(m, tc * 512, 512),
                                    xcur_sl(m, tc * 512, 512),
                                    py[:, m * 512:(m + 1) * 512])
                            return i
                        res_ev[(tc, 0)] = ph.op("vector", radda, inc=[("dve", 1)],
                                                waits=[ev_ba])
                        res_ev[(tc, 1)] = ph.op("vector", raddb, inc=[("dve", 1)],
                                                waits=[ev_bb])
                    ph.run()
                return res_ev

        f1_res = ffn_sublayer("f1", xr_sl)
        ffn_state["cached"] = True
        if want("ffn1"):
            dump_sbuf("ffn1", xr[:, :], [128, DC * T])
        if stop == "ffn1":
            _finish(p)
            ctx_xr.close()
            return nc, dump_outs

        ctx_at1 = ExitStack()
        xnT = sb("xnT", [128, DC * T], BF16, ctx_at1, side="right")
        g_sb = sb("g_sb", [128, DC * T], BF16, ctx_at1, side="right")

        ph = Phase(p, "spill")
        ev_spill = ph.op("sync", lambda e: e.dma_start(
            xsp_d[:, :].rearrange("(c p) t -> p c t", p=128),
            xr[:, :].rearrange("p (c t) -> p c t", c=DC)), inc=[("dsp", 16)],
            waits=[f1_res[(T // 512 - 1, 0)], f1_res[(T // 512 - 1, 1)]])
        ph.run()

        vch_ctx = ExitStack()
        clipb = [sb("clipb%d" % i, [128, SL], BF16, vch_ctx, side="right")
                 for i in range(2)]
        sqvb = [sb("sqvb%d" % i, [128, SL], BF16, vch_ctx, side="right")
                for i in range(2)]
        pss_ctx = ExitStack()
        pss_box = {}
        ss_row = sb("ss_row", [1, T], F32, vch_ctx, side="right")
        vch_ev = {"cl": {}, "sq": {}, "mm": {}, "ev": {}}

        def vchain_consumer(ph, c, s, ev_hi_e, ev_lo_e):
            if "pss" not in pss_box:
                pss_box["pss"] = [ps("pss%d" % i, [1, SL], F32, pss_ctx,
                                     side="right") for i in range(2)]
            pss = pss_box["pss"][s % 2]
            t0 = s * SL
            gi = s * DC + c
            b = gi % 2
            w = [ev_hi_e]
            if gi >= 2:
                w.append(vch_ev["sq"][gi - 2])
            ev_cl = ph.op("vector", lambda e, c=c, t0=t0, b=b: e.tensor_scalar(
                clipb[b][:, :], xnT[:, c * T + t0: c * T + t0 + SL],
                8.0, -8.0, ALU.min, ALU.max), inc=[("dve", 1)], waits=w)
            vch_ev["cl"][gi] = ev_cl
            w2 = [ev_cl]
            if gi >= 2:
                w2.append(vch_ev["mm"][gi - 2])
            ev_sq = ph.op("scalar", lambda e, b=b: e.activation(
                sqvb[b][:, :], clipb[b][:, :], AF.Square),
                inc=[("act", 1)], waits=w2)
            vch_ev["sq"][gi] = ev_sq
            wmm = [ev_sq]
            if c == 0 and (s - 2) in vch_ev["ev"]:
                wmm.append(vch_ev["ev"][s - 2])

            def smm(e, c=c, b=b, pss=pss):
                lhs = ones_col0 if c == 0 else ones_col
                return e.matmul(pss[:, :], lhs[:, :], sqvb[b][:, :],
                                start=(c == 0), stop=(c == DC - 1))
            vch_ev["mm"][gi] = ph.op("tensor", smm, inc=[("pe", 1)], waits=wmm)
            if c == DC - 1:
                vch_ev["ev"][s] = ph.op(
                    "scalar", lambda e, t0=t0, pss=pss: e.copy(
                        ss_row[0:1, t0: t0 + SL], pss[:, :]),
                    inc=[("act", 1)], waits=[vch_ev["mm"][gi]])

        layernorm("a", xr_sl, xnT, None, chunk_consumer=vchain_consumer)
        ev_ssdone = [vch_ev["ev"][s] for s in range(NSL)]

        ph = Phase(p, "spill_wait")
        for e_ in ENGS:
            ph.op(e_, lambda e: e.nop(), waits=[ev_spill])
        ph.run()
        ctx_xr.close()

        if want("lna"):
            dump_sbuf("lna", xnT[:, :], [128, DC * T])
        if stop == "lna":
            _finish(p)
            ctx_at1.close()
            return nc, dump_outs

        with ExitStack() as cm:
            W1s = sb("W1s", [128, DC * DFF], BF16, cm)
            W2s = sb("W2s", [128, FC * D], BF16, cm)
            h1 = sb("h1", [128, FC * 512], BF16, cm)
            th = [sb("th%d" % i, [128, 512], F32, cm) for i in range(2)]
            ph = Phase(p, "metric")
            HC2 = DC // 2
            ev_w1h = []
            for hw in range(2):
                ev_w1h.append(ph.op("gpsimd", lambda e, hw=hw: e.dma_start(
                    W1s[:, hw * HC2 * DFF:(hw + 1) * HC2 * DFF].rearrange(
                        "p (c n) -> p c n", c=HC2),
                    W1T_d[hw * (D // 2):(hw + 1) * (D // 2), :].rearrange(
                        "(c p) n -> p c n", p=128)),
                    inc=[("dw1%d" % hw, 16)]))
            ev_w1 = ev_w1h[1]
            ev_w2 = ph.op("gpsimd", lambda e: e.dma_start(
                W2s[:, :].rearrange("p (c n) -> p c n", c=FC),
                W2T_d[:, :].rearrange("(c p) n -> p c n", p=128)), inc=[("dgw", 16)])
            ev_adj = ph.op("vector", lambda e: e.tensor_scalar_add(
                xnT[0:1, 0:T], xnT[0:1, 0:T], -1.0), inc=[("dve", 1)])
            with ExitStack() as c2:
                ph1 = [ps("ph1%d" % i, [128, 512], F32, c2) for i in range(2)]
                pg = [ps("pg%d" % i, [128, 512], F32, c2) for i in range(2)]
                ev_h1ev = {}
                ev_thev = {}
                ev_gw = {}
                for q in range(T // 512):
                    for m in range(FC):
                        b = m % 2
                        w = [ev_adj]
                        if (q, m - 2) in ev_h1ev:
                            w.append(ev_h1ev[(q, m - 2)])
                        elif q > 0:
                            w.append(ev_h1ev[(q - 1, FC - 2 + m)])
                        if q > 0 and m < 2:
                            w.append(ev_gw[(q - 1, DC - 1)])

                        def h1mma(e, q=q, m=m, b=b):
                            for c in range(HC2):
                                i = e.matmul(
                                    ph1[b][:, :],
                                    W1s[:, c * DFF + m * 128: c * DFF + (m + 1) * 128],
                                    xnT[:, c * T + q * 512: c * T + q * 512 + 512],
                                    start=(c == 0), stop=False)
                            return i

                        def h1mmb(e, q=q, m=m, b=b):
                            for c in range(HC2, DC):
                                i = e.matmul(
                                    ph1[b][:, :],
                                    W1s[:, c * DFF + m * 128: c * DFF + (m + 1) * 128],
                                    xnT[:, c * T + q * 512: c * T + q * 512 + 512],
                                    start=False, stop=(c == DC - 1))
                            return i
                        wa = [x for x in w if x is not ev_w2] + [ev_w1h[0]]
                        ph.op("tensor", h1mma, inc=[("pe", 1)], waits=wa)
                        ev_mm = ph.op("tensor", h1mmb, inc=[("pe", 1)],
                                      waits=[ev_w1h[1]])
                        ev_h1ev[(q, m)] = ph.op(
                            "scalar", lambda e, q=q, m=m, b=b: e.activation(
                                h1[:, m * 512:(m + 1) * 512], ph1[b][:, :], AF.Tanh,
                                bias=b1T[:, m:m + 1]), inc=[("act", 1)],
                            waits=[ev_mm])
                    for md in range(DC):
                        b = md % 2
                        w = [ev_w2, ev_h1ev[(q, FC - 1)]]
                        if (q, md - 2) in ev_thev:
                            w.append(ev_thev[(q, md - 2)])
                        elif q > 0:
                            w.append(ev_thev[(q - 1, DC - 2 + md)])

                        def gmm(e, q=q, md=md, b=b):
                            for c in range(FC):
                                i = e.matmul(
                                    pg[b][:, :],
                                    W2s[:, c * D + md * 128: c * D + (md + 1) * 128],
                                    h1[:, c * 512:(c + 1) * 512],
                                    start=(c == 0), stop=(c == FC - 1))
                            return i
                        ev_gw[(q, md)] = ph.op("tensor", gmm, inc=[("pe", 1)], waits=w)
                        ev_th = ph.op("scalar", lambda e, q=q, md=md, b=b: e.activation(
                            th[b][:, :], pg[b][:, :], AF.Tanh, bias=b2T[:, md:md + 1]),
                            inc=[("act", 1)], waits=[ev_gw[(q, md)]])
                        ev_thev[(q, md)] = ev_th
                        ev_ab = ph.op("scalar", lambda e, q=q, md=md, b=b: e.activation(
                            g_sb[:, md * T + q * 512: md * T + q * 512 + 512],
                            th[b][:, :], AF.Abs, scale=0.1, bias=c_pos1[:, 0:1]),
                            inc=[("act", 1)], waits=[ev_th])
                        if md == 0:
                            ev_thev[(q, md)] = ph.op(
                                "scalar", lambda e, q=q, b=b: e.activation(
                                    g_sb[0:1, q * 512: q * 512 + 512],
                                    th[b][0:1, :], AF.Abs, scale=0.1, bias=c_neg1[0:1, 0:1]),
                                inc=[("act", 1)], waits=[ev_ab])
                ph.run()
        if want("g"):
            dump_sbuf("g", g_sb[:, :], [128, DC * T])
        if stop == "g":
            _finish(p)
            ctx_at1.close()
            return nc, dump_outs

        ctx_at2 = ExitStack()

        with ExitStack() as cv:
            r_y0 = sb("r_y0", [1, T], F32, cv)
            r_t = sb("r_t", [1, T], F32, cv)
            r_a = sb("r_a", [1, T], F32, cv)
            r_d = sb("r_d", [1, T], F32, cv)
            r_dv0 = sb("r_dv0", [1, T], F32, cv)
            r_ss = sb("r_ss", [1, T], F32, cv)
            r_s = sb("r_s", [1, T], F32, cv)
            s_bf = sb("s_bf", [1, T], BF16, cv)
            dv0s_bf = sb("dv0s_bf", [1, T], BF16, cv)
            clipv = [sb("clipv%d" % i, [128, T], BF16, cv) for i in range(2)]
            ph = Phase(p, "vchain")
            ev_ss = ph.op("scalar", lambda e: e.copy(r_ss[:, :], ss_row[:, :]),
                          inc=[("act", 1)], waits=ev_ssdone)
            ev1 = ph.op("scalar", lambda e: e.activation(
                r_y0[:, :], r_ss[:, :], AF.Ln, bias=c_1eps[0:1, 0:1]),
                inc=[("act", 1)], waits=[ev_ss])
            ev2 = ph.op("scalar", lambda e: e.activation(
                r_y0[:, :], r_y0[:, :], AF.Exp, scale=0.5),
                inc=[("act", 1)], waits=[ev1])
            ev3 = ph.op("vector", lambda e: e.tensor_scalar_max(
                r_t[:, :], r_y0[:, :], 1.0 + EPS), inc=[("dve", 1)], waits=[ev2])
            ev4 = ph.op("scalar", lambda e: e.activation(
                r_a[:, :], r_t[:, :], AF.Square), inc=[("act", 1)], waits=[ev3])
            ev5 = ph.op("scalar", lambda e: e.activation(
                r_a[:, :], r_a[:, :], AF.Ln, bias=c_neg1[0:1, 0:1]),
                inc=[("act", 1)], waits=[ev4])
            ev6 = ph.op("scalar", lambda e: e.activation(
                r_a[:, :], r_a[:, :], AF.Exp, scale=0.5), inc=[("act", 1)],
                waits=[ev5])
            ev7 = ph.op("vector", lambda e: e.tensor_add(
                r_a[:, :], r_a[:, :], r_t[:, :]), inc=[("dve", 1)], waits=[ev6])
            ev8 = ph.op("scalar", lambda e: e.activation(
                r_a[:, :], r_a[:, :], AF.Ln), inc=[("act", 1)], waits=[ev7])
            ev9 = ph.op("vector", lambda e: e.tensor_scalar_max(
                r_d[:, :], r_a[:, :], 0.001), inc=[("dve", 1)], waits=[ev8])
            ev10 = ph.op("vector", lambda e: e.tensor_sub(
                r_dv0[:, :], r_y0[:, :], r_t[:, :]), inc=[("dve", 1)],
                waits=[ev3, ev9])
            ev11 = ph.op("scalar", lambda e: e.activation(
                r_a[:, :], r_dv0[:, :], AF.Square), inc=[("act", 1)],
                waits=[ev10, ev9])
            ev12 = ph.op("vector", lambda e: e.tensor_sub(
                r_a[:, :], r_ss[:, :], r_a[:, :]), inc=[("dve", 1)], waits=[ev11])
            ev13 = ph.op("vector", lambda e: e.tensor_scalar_max(
                r_a[:, :], r_a[:, :], EPS), inc=[("dve", 1)], waits=[ev12])
            ev14 = ph.op("scalar", lambda e: e.activation(
                r_a[:, :], r_a[:, :], AF.Ln), inc=[("act", 1)], waits=[ev13])
            ev15 = ph.op("scalar", lambda e: e.activation(
                r_a[:, :], r_a[:, :], AF.Exp, scale=0.5), inc=[("act", 1)],
                waits=[ev14])
            ev16 = ph.op("vector", lambda e: e.tensor_scalar_add(
                r_a[:, :], r_a[:, :], EPS), inc=[("dve", 1)], waits=[ev15])
            ev17 = ph.op("vector", lambda e: e.reciprocal(
                r_a[:, :], r_a[:, :]), inc=[("dve", 1)], waits=[ev16])
            ev18 = ph.op("vector", lambda e: e.tensor_mul(
                r_s[:, :], r_d[:, :], r_a[:, :]), inc=[("dve", 1)],
                waits=[ev17, ev9])
            ev_sbf = ph.op("scalar", lambda e: e.copy(s_bf[:, :], r_s[:, :]),
                           inc=[("act", 1)], waits=[ev18])
            ev19 = ph.op("vector", lambda e: e.tensor_mul(
                r_dv0[:, :], r_dv0[:, :], r_s[:, :]), inc=[("dve", 1)],
                waits=[ev18])
            ev_d0 = ph.op("scalar", lambda e: e.copy(dv0s_bf[:, :], r_dv0[:, :]),
                          inc=[("act", 1)], waits=[ev19])
            ph.run()
            pss_ctx.close()
            vch_ctx.close()
            wpb = [sb("wpb%d" % i, [128, DC * D], BF16, ctx_at1, side="right")
                   for i in range(2)]
            with ExitStack() as c2:
                psb = ps("psb", [128, T], F32, c2)
                ph = Phase(p, "vbuild")
                ev_lq = ph.op("gpsimd", lambda e: e.dma_start(
                    wpb[0][:, :].rearrange("p (c n) -> p c n", c=DC),
                    WqT_d[:, :].rearrange("(c p) n -> p c n", p=128)),
                    inc=[("dgw", 16)])
                ev_lk = ph.op("gpsimd", lambda e: e.dma_start(
                    wpb[1][:, :].rearrange("p (c n) -> p c n", c=DC),
                    WkT_d[:, :].rearrange("(c p) n -> p c n", p=128)),
                    inc=[("dgw", 16)])

                def sbc(e):
                    for n in range(NS):
                        i = e.matmul(psb[:, n * 512:(n + 1) * 512], ones_row[:, :],
                                     s_bf[:, n * 512:(n + 1) * 512],
                                     start=True, stop=True)
                    return i
                ev_bc = ph.op("tensor", sbc, inc=[("pe", 1)], waits=[ev_sbf])
                ev_v0 = None
                ev_cl2 = {}
                ev_vv2 = {}
                for c in range(DC):
                    b = c % 2
                    w = []
                    if c >= 2:
                        w.append(ev_vv2[c - 2])
                    ev_cl2[c] = ph.op("vector", lambda e, c=c, b=b: e.tensor_scalar(
                        clipv[b][:, :], xnT[:, c * T:(c + 1) * T],
                        8.0, -8.0, ALU.min, ALU.max), inc=[("dve", 1)], waits=w)
                    ev_vv2[c] = ph.op("vector", lambda e, c=c, b=b: e.tensor_mul(
                        xnT[:, c * T:(c + 1) * T], clipv[b][:, :],
                        psb[:, :]), inc=[("dve", 1)], waits=[ev_cl2[c], ev_bc])
                    if c == 0:
                        ev_v0 = ev_vv2[c]
                ev_row0 = ph.op("scalar", lambda e: e.copy(
                    xnT[0:1, 0:T], dv0s_bf[:, :]),
                    inc=[("act", 1)], waits=[ev_v0, ev_d0])
                ph.run()
        vT = xnT
        if want("vT"):
            dump_sbuf("vT", vT[:, :], [128, DC * T])
        if stop == "vT":
            _finish(p)
            ctx_at1.close()
            return nc, dump_outs

        QG = sb("QG", [128, DC * T], BF16, ctx_at2)
        with ExitStack() as c2:
            pq = [ps("pq%d" % i, [128, T], F32, c2) for i in range(2)]
            ph = Phase(p, "qproj")
            ev_qg = {}
            for m in range(DC):
                b = m % 2
                w = [ev_lk]
                if m >= 2:
                    w.append(ev_qg[m - 2])
                else:
                    w += [ev_row0, ev_vv2[DC - 1], ev_vv2[DC - 2]]

                def qmm(e, m=m, b=b):
                    for c in range(DC):
                        for n in range(NS):
                            i = e.matmul(
                                pq[b][:, n * 512:(n + 1) * 512],
                                wpb[0][:, c * D + m * 128: c * D + (m + 1) * 128],
                                vT[:, c * T + n * 512: c * T + (n + 1) * 512],
                                start=(c == 0), stop=(c == DC - 1))
                    return i
                ev_mm = ph.op("tensor", qmm, inc=[("pe", 1)], waits=w)
                ev_qg[m] = ph.op("vector", lambda e, m=m, b=b: e.tensor_mul(
                    QG[:, m * T:(m + 1) * T], pq[b][:, :],
                    g_sb[:, m * T:(m + 1) * T]), inc=[("dve", 1)], waits=[ev_mm])
            ev_neg = ph.op("vector", lambda e: e.tensor_scalar_mul(
                QG[0:1, 0:T], QG[0:1, 0:T], -1.0), inc=[("dve", 1)],
                waits=[ev_qg[0]])
            ph.run()

            kT = sb("kT", [128, DC * T], BF16, ctx_at2)
            vv = sb("vv", [128, TC * H * 65], BF16, ctx_at2)
            ph = Phase(p, "kproj")
            ev_kev = {}
            for m in range(DC):
                b = m % 2
                w = [ev_lk]
                if m >= 2:
                    w.append(ev_kev[m - 2])
                else:
                    w.append(ev_qg[DC - 2 + m])
                    w.append(ev_neg)

                def kmm(e, m=m, b=b):
                    for c in range(DC):
                        for n in range(NS):
                            i = e.matmul(
                                pq[b][:, n * 512:(n + 1) * 512],
                                wpb[1][:, c * D + m * 128: c * D + (m + 1) * 128],
                                vT[:, c * T + n * 512: c * T + (n + 1) * 512],
                                start=(c == 0), stop=(c == DC - 1))
                    return i
                ev_mm = ph.op("tensor", kmm, inc=[("pe", 1)], waits=w)
                ev_kev[m] = ph.op("scalar", lambda e, m=m, b=b: e.copy(
                    kT[:, m * T:(m + 1) * T], pq[b][:, :]), inc=[("act", 1)],
                    waits=[ev_mm])
            ev_lv = ph.op("gpsimd", lambda e: e.dma_start(
                wpb[0][:, :].rearrange("p (c n) -> p c n", c=DC),
                WvT_d[:, :].rearrange("(c p) n -> p c n", p=128)),
                inc=[("dgw", 16)], waits=[ev_qg[DC - 1]])
            ph.run()

        cb = ExitStack()
        sqs2 = [sb("sqb2%d" % i, [128, T // 2], BF16, cb) for i in range(2)]
        r_m = sb("r_m", [1, 1], F32, cb)
        nb_bf = sb("nb_bf", [1, 1], BF16, cb)
        with ExitStack() as c2:
            pv = [ps("pv%d" % i, [128, D], F32, c2) for i in range(2)]
            pqn = ps("pqn", [1, T], F32, c2)
            ph = Phase(p, "vvproj")
            ev_one = ph.op("gpsimd", lambda e: e.memset(
                bass.AP(vv, 64, [[TC * H * 65, 128], [H * 65, TC], [65, H], [1, 1]]),
                1.0), inc=[("pl", 1)])
            ev_vev = {}
            ev_sq = {}
            ev_nmm = {}
            norm_srcs = []
            for s_, c, ev_ in ([(QG, c, ev_qg[c]) for c in range(DC)] +
                               [(kT, c, ev_kev[c]) for c in range(DC)]):
                for hh in (0, 1):
                    norm_srcs.append((s_, c, hh, ev_))
            for tt in range(TC):
                b = tt % 2
                w = [ev_lv]
                if tt >= 2:
                    w.append(ev_vev[tt - 2])

                def vmm(e, tt=tt, b=b):
                    for c in range(DC):
                        for n in range(D // 512):
                            i = e.matmul(
                                pv[b][:, n * 512:(n + 1) * 512],
                                vT[:, c * T + tt * 128: c * T + (tt + 1) * 128],
                                wpb[0][:, c * D + n * 512: c * D + (n + 1) * 512],
                                start=(c == 0), stop=(c == DC - 1))
                    return i
                ev_mm = ph.op("tensor", vmm, inc=[("pe", 1)], waits=w)
                ev_vev[tt] = ph.op("scalar", lambda e, tt=tt, b=b: e.copy(
                    bass.AP(vv, tt * H * 65, [[TC * H * 65, 128], [65, H], [1, 64]]),
                    bass.AP(pv[b], 0, [[D, 128], [64, H], [1, 64]])),
                    inc=[("act", 1)], waits=[ev_mm, ev_one])
                NK = len(norm_srcs)
                TH2 = T // 2

                def emit_nmm(k):
                    s_, c, hh, ev_ = norm_srcs[k]

                    def nmm(e, k=k, c=c, hh=hh):
                        for n in range(TH2 // 512):
                            i = e.matmul(
                                pqn[:, hh * TH2 + n * 512: hh * TH2 + (n + 1) * 512],
                                ones_col[:, :],
                                sqs2[k % 2][:, n * 512:(n + 1) * 512],
                                start=(c == 0 and s_ is QG),
                                stop=(c == DC - 1 and s_ is kT))
                        return i
                    ev_nmm[k] = ph.op("tensor", nmm, inc=[("pe", 1)],
                                      waits=[ev_sq[k]])

                for ksub in (2 * tt, 2 * tt + 1):
                    if ksub < NK:
                        src_t, c, hh, ev_src = norm_srcs[ksub]
                        w2 = [ev_src]
                        if (ksub - 2) in ev_nmm:
                            w2.append(ev_nmm[ksub - 2])
                        ev_sq[ksub] = ph.op(
                            "scalar",
                            lambda e, src_t=src_t, c=c, hh=hh, ksub=ksub:
                            e.activation(
                                sqs2[ksub % 2][:, :],
                                src_t[:, c * T + hh * TH2:
                                      c * T + (hh + 1) * TH2],
                                AF.Square), inc=[("act", 1)], waits=w2)
                        if ksub >= 1:
                            emit_nmm(ksub - 1)
            emit_nmm(NK - 1)
            ev = ev_nmm[NK - 1]
            ev2 = ph.op("vector", lambda e: e.reduce_max(
                r_m[:, :], pqn[:, :], axis=AX.X), inc=[("dve", 1)], waits=[ev])
            ev3 = ph.op("vector", lambda e: e.tensor_scalar(
                r_m[:, :], r_m[:, :], 1.5 / 16.0, 43.0, ALU.mult, ALU.min),
                inc=[("dve", 1)], waits=[ev2])
            ev4 = ph.op("vector", lambda e: e.tensor_scalar_mul(
                r_m[:, :], r_m[:, :], -1.0), inc=[("dve", 1)], waits=[ev3])
            ev5 = ph.op("scalar", lambda e: e.copy(nb_bf[:, :], r_m[:, :]),
                        inc=[("act", 1)], waits=[ev4])
            ph.run()
        ctx_at1.close()

        if want("QG"):
            dump_sbuf("QG", QG[:, :], [128, DC * T])
            dump_sbuf("kT", kT[:, :], [128, DC * T])
            dump_sbuf("vv", vv[:, :], [128, TC * H * 65])
        if stop == "QG":
            _finish(p)
            ctx_at2.close()
            return nc, dump_outs

        with ExitStack() as c2:
            pnb = ps("pnb", [128, 1], F32, c2)
            ph = Phase(p, "bbcast")
            ev6 = ph.op("tensor", lambda e: e.matmul(
                pnb[:, :], ones_row[:, :], nb_bf[:, :], start=True, stop=True),
                inc=[("pe", 1)], waits=[ev5])
            ev_nb = ph.op("scalar", lambda e: e.copy(negB[:, :], pnb[:, :]),
                          inc=[("act", 1)], waits=[ev6])
            ph.run()
        cb.close()

        if want("bb"):
            dump_sbuf("bb", negB[:, :], [128, 1])
        if stop == "bb":
            _finish(p)
            ctx_at2.close()
            return nc, dump_outs

        ctx_at3 = ExitStack()
        avout = sb("avout", [128, DC * T], BF16, ctx_at3, side="right")
        wo_sb = sb("wo_sb", [128, DC * D], BF16, ctx_at3, side="right")
        ph = Phase(p, "wo_load")
        ev_lwo = ph.op("gpsimd", lambda e: e.dma_start(
            wo_sb[:, :].rearrange("p (c n) -> p c n", c=DC),
            WoT_d[:, :].rearrange("(c p) n -> p c n", p=128)), inc=[("dgw", 16)])
        ph.run()

        with ExitStack() as ca:
            NB = 3
            NPT = 5
            PT = [sb("PT%d" % i, [128, 1024], BF16, ca) for i in range(NPT)]
            srow = [sb("srow%d" % i, [1, 512], F32, ca) for i in range(4)]
            sums16 = sb("sums16", [H, T], F32, ca)
            QQ = T // 512
            with ExitStack() as c2:
                pS = [ps("pS%d" % i, [128, 1024], F32, c2) for i in range(NB)]
                pav = [ps("pav%d" % i, [65, 512], F32, c2) for i in range(2)]
                ph = Phase(p, "attn_core")
                ev_exp = {}
                ev_avp = {}
                ev_evac = {}
                ev_srr = {}
                ev_sdma = {}
                it_g = [0]
                for hp in range(H // 2):
                    for qq in range(QQ):
                        hq = hp * QQ + qq

                        def emit_av(kc, hq=hq, hp=hp, qq=qq):
                            it = hq * KC + kc
                            w = [ev_exp[it]]
                            if kc == 0 and hq > 0:
                                w.append(ev_evac[(hq - 1, 0)])
                                w.append(ev_evac[(hq - 1, 1)])
                                w.append(ev_srr[(hq - 1, 0)])
                                w.append(ev_srr[(hq - 1, 1)])

                            def avmm(e, hp=hp, kc=kc, it=it):
                                for j in range(2):
                                    i = e.matmul(
                                        pav[j][:, :],
                                        vv[:, kc * H * 65 + (2 * hp + j) * 65:
                                           kc * H * 65 + (2 * hp + j) * 65 + 65],
                                        PT[it % NPT][:, j * 512:(j + 1) * 512],
                                        start=(kc == 0), stop=(kc == KC - 1))
                                return i
                            ev_avp[it] = ph.op("tensor", avmm, inc=[("pe", 1)],
                                               waits=w)

                        for kc in range(KC):
                            it = hq * KC + kc
                            b = kc % NB
                            w = []
                            if it >= NB:
                                w.append(ev_exp[it - NB])

                            def smm(e, hp=hp, qq=qq, kc=kc, b=b):
                                q0 = qq * 512
                                e.matmul(
                                    pS[b][:, 0:512],
                                    kT[0:64, hp * T + kc * 128:
                                       hp * T + (kc + 1) * 128],
                                    QG[0:64, hp * T + q0: hp * T + q0 + 512],
                                    start=True, stop=True)
                                i = e.matmul(
                                    pS[b][:, 512:1024],
                                    kT[64:128, hp * T + kc * 128:
                                       hp * T + (kc + 1) * 128],
                                    QG[64:128, hp * T + q0: hp * T + q0 + 512],
                                    start=True, stop=True)
                                return i
                            ev_s = ph.op("tensor", smm, inc=[("pe", 1)], waits=w)
                            wv_ = [ev_s, ev_nb]
                            if it >= NPT:
                                wv_.append(ev_avp[it - NPT])
                            ev_exp[it] = ph.op(
                                "scalar", lambda e, it=it, b=b: e.activation(
                                    PT[it % NPT][:, :], pS[b][:, :], AF.Exp,
                                    bias=negB[:, 0:1], scale=0.125),
                                inc=[("act", 1)], waits=wv_)
                            if kc >= 1:
                                emit_av(kc - 1)
                        emit_av(KC - 1)
                        for j in range(2):
                            half = j * 64
                            ev_evac[(hq, j)] = ph.op(
                                "vector",
                                lambda e, hp=hp, qq=qq, j=j, half=half: e.tensor_copy(
                                    avout[half:half + 64,
                                          hp * T + qq * 512: hp * T + qq * 512 + 512],
                                    pav[j][0:64, :]),
                                inc=[("dve", 1)],
                                waits=[ev_avp[hq * KC + KC - 1]])
                            si = 2 * (hq % 2) + j
                            w2 = [ev_avp[hq * KC + KC - 1]]
                            prev = ev_sdma.get((hq - 2, j))
                            if prev:
                                w2.append(prev)
                            ev_sr = ph.op("vector", lambda e, j=j, si=si: e.tensor_copy(
                                srow[si][:, :], pav[j][64:65, :]),
                                inc=[("dve", 1)], waits=w2)
                            ev_srr[(hq, j)] = ev_sr
                            ev_sdma[(hq, j)] = ph.op(
                                "sync", lambda e, hp=hp, qq=qq, j=j, si=si: e.dma_start(
                                    sums16[2 * hp + j: 2 * hp + j + 1,
                                           qq * 512:(qq + 1) * 512], srow[si][:, :]),
                                inc=[("dsr%d" % j, 16)], waits=[ev_sr])
                ph.run()
            rec16 = sb("rec16", [H, T], F32, ca)
            rec16b = sb("rec16b", [H, T], BF16, ca)
            ev_attn_done = []
            with ExitStack() as c2:
                pnorm = [ps("pnorm%d" % i, [128, T], F32, c2) for i in range(2)]
                ph = Phase(p, "avnorm")
                ev_r = ph.op("vector", lambda e: e.reciprocal(
                    rec16[:, :], sums16[:, :]), inc=[("dve", 1)],
                    waits=[p.now("dsr0"), p.now("dsr1")])
                ev_rb = ph.op("scalar", lambda e: e.copy(rec16b[:, :], rec16[:, :]),
                              inc=[("act", 1)], waits=[ev_r])
                nrm_ev = {}
                for hp in range(DC):
                    pb = hp % 2
                    w = [ev_rb, ev_evac[(hp * (T // 512) + T // 512 - 1, 1)]]
                    if hp >= 2:
                        w.append(nrm_ev[hp - 2])

                    def nbc(e, hp=hp, pb=pb):
                        for j in range(2):
                            hh = 2 * hp + j
                            for n in range(NS):
                                i = e.matmul(
                                    pnorm[pb][64 * j:64 * (j + 1),
                                              n * 512:(n + 1) * 512],
                                    oh16[:, hh * 64:(hh + 1) * 64],
                                    rec16b[:, n * 512:(n + 1) * 512],
                                    start=True, stop=True)
                        return i
                    ev_b = ph.op("tensor", nbc, inc=[("pe", 1)], waits=w)
                    nrm_ev[hp] = ph.op("vector", lambda e, hp=hp, pb=pb: e.tensor_mul(
                        avout[:, hp * T:(hp + 1) * T],
                        avout[:, hp * T:(hp + 1) * T], pnorm[pb][:, :]),
                        inc=[("dve", 1)], waits=[ev_b])
                    ev_attn_done.append(nrm_ev[hp])
                ph.run()
        ctx_at2.close()

        if want("avout"):
            dump_sbuf("avout", avout[:, :], [128, DC * T])
        if stop == "avout":
            _finish(p)
            ctx_at3.close()
            return nc, dump_outs

        ctx_xr2 = ExitStack()
        xr2 = sb("xr2", [128, DC * T], F32, ctx_xr2)
        xr2_sl = mk_sl(xr2)

        ph = Phase(p, "reload")
        ev_rel = ph.op("sync", lambda e: e.dma_start(
            xr2[:, :].rearrange("p (c t) -> p c t", c=DC),
            xsp_d[:, :].rearrange("(c p) t -> p c t", p=128)),
            inc=[("drl", 16)], waits=ev_attn_done)
        ph.run()

        with ExitStack() as cw:
            sp_sb = sb("sp_sb", [128, DC * 512], BF16, cw)
            sqw = [sb("sqw%d" % i, [128, 512], BF16, cw) for i in range(2)]
            t32w = [sb("t32w%d" % i, [128, 512], F32, cw) for i in range(2)]
            r_o0 = sb("r_o0", [1, 512], F32, cw)
            r_x0 = sb("r_x0", [1, 512], F32, cw)
            with ExitStack() as c2:
                pw = [ps("pw%d" % i, [128, 512], F32, c2) for i in range(2)]
                psq = ps("psq", [1, 512], F32, c2)
                ph = Phase(p, "wo_tail")
                ev_sp = {}
                ev_sqev = {}
                ev_smm = {}
                ev_add = {}
                ev_o1 = {}
                ev_x0c = {}
                for ts_ in range(NS):
                    for md in range(DC):
                        b = md % 2
                        w = [ev_lwo] + list(ev_attn_done)
                        if (ts_, md - 2) in ev_sp:
                            w.append(ev_sp[(ts_, md - 2)])
                        elif ts_ > 0:
                            w.append(ev_sp[(ts_ - 1, DC - 2 + md)])

                        def wmma(e, ts_=ts_, md=md, b=b):
                            for c in range(DC // 2):
                                i = e.matmul(
                                    pw[b][:, :],
                                    wo_sb[:, c * D + md * 128: c * D + (md + 1) * 128],
                                    avout[:, c * T + ts_ * 512: c * T + ts_ * 512 + 512],
                                    start=(c == 0), stop=False)
                            return i

                        def wmmb(e, ts_=ts_, md=md, b=b):
                            for c in range(DC // 2, DC):
                                i = e.matmul(
                                    pw[b][:, :],
                                    wo_sb[:, c * D + md * 128: c * D + (md + 1) * 128],
                                    avout[:, c * T + ts_ * 512: c * T + ts_ * 512 + 512],
                                    start=False, stop=(c == DC - 1))
                            return i
                        ph.op("tensor", wmma, inc=[("pe", 1)],
                              waits=[x for x in w if x not in ev_attn_done]
                              + ev_attn_done[:DC // 2])
                        ev_mm = ph.op("tensor", wmmb, inc=[("pe", 1)],
                                      waits=ev_attn_done[DC // 2:])
                        w2 = [ev_mm]
                        if ts_ > 0:
                            w2.append(ev_add[(ts_ - 1, md)])
                            w2.append(ev_sqev[(ts_ - 1, md)])
                        ev_t1 = ph.op("vector", lambda e, md=md, b=b: e.tensor_scalar(
                            t32w[b][:, :], pw[b][:, :], 0.1, 8.0, ALU.mult, ALU.min),
                            inc=[("dve", 1)], waits=w2)
                        ev_sp[(ts_, md)] = ph.op(
                            "vector", lambda e, md=md, b=b: e.tensor_scalar_max(
                                sp_sb[:, md * 512:(md + 1) * 512], t32w[b][:, :],
                                -8.0), inc=[("dve", 1)], waits=[ev_t1])
                        w3 = [ev_sp[(ts_, md)]]
                        if (ts_, md - 2) in ev_smm:
                            w3.append(ev_smm[(ts_, md - 2)])
                        elif ts_ > 0 and md < 2:
                            w3.append(ev_smm[(ts_ - 1, DC - 2 + md)])
                        ev_sqev[(ts_, md)] = ph.op(
                            "scalar", lambda e, md=md, b=b: e.activation(
                                sqw[b][:, :], sp_sb[:, md * 512:(md + 1) * 512],
                                AF.Square), inc=[("act", 1)], waits=w3)

                        def emit_sqmm(mk, ts_=ts_):
                            w4 = [ev_sqev[(ts_, mk)]]
                            if mk == 0 and ts_ > 0:
                                w4.append(ev_o1[ts_ - 1])

                            def sqmm(e, mk=mk):
                                lhs = ones_col0 if mk == 0 else ones_col
                                return e.matmul(psq[:, :], lhs[:, :],
                                                sqw[mk % 2][:, :],
                                                start=(mk == 0),
                                                stop=(mk == DC - 1))
                            ev_smm[(ts_, mk)] = ph.op("tensor", sqmm,
                                                      inc=[("pe", 1)], waits=w4)

                        if md >= 1:
                            emit_sqmm(md - 1)
                        if md == DC - 1:
                            emit_sqmm(DC - 1)
                    w5 = [ev_smm[(ts_, DC - 1)]]
                    if ts_ > 0:
                        w5.append(ev_x0c[ts_ - 1])
                    ev_o1[ts_] = ph.op("scalar", lambda e: e.activation(
                        r_o0[:, :], psq[:, :], AF.Ln, bias=c_1eps[0:1, 0:1]),
                        inc=[("act", 1)], waits=w5)
                    ev_o2 = ph.op("scalar", lambda e: e.activation(
                        r_o0[:, :], r_o0[:, :], AF.Exp, scale=0.5),
                        inc=[("act", 1)], waits=[ev_o1[ts_]])
                    w6 = [ev_o2, ev_rel]
                    if ts_ > 0:
                        w6.append(ev_x0c[ts_ - 1])
                    ev_x0 = ph.op("vector", lambda e, ts_=ts_: e.tensor_add(
                        r_x0[:, :], xr2[0:1, ts_ * 512: ts_ * 512 + 512], r_o0[:, :]),
                        inc=[("dve", 1)], waits=w6)
                    for c in range(DC):
                        ev_add[(ts_, c)] = ph.op(
                            "vector", lambda e, ts_=ts_, c=c: e.tensor_add(
                                xr2_sl(c, ts_ * 512, 512), xr2_sl(c, ts_ * 512, 512),
                                sp_sb[:, c * 512:(c + 1) * 512]),
                            inc=[("dve", 1)],
                            waits=[ev_sp[(ts_, c)], ev_x0, ev_rel])
                    ev_x0c[ts_] = ph.op("scalar", lambda e, ts_=ts_: e.copy(
                        xr2[0:1, ts_ * 512: ts_ * 512 + 512], r_x0[:, :]),
                        inc=[("act", 1)], waits=[ev_add[(ts_, 0)]])
                ph.run()
        ctx_at3.close()

        if want("attn"):
            dump_sbuf("attn", xr2[:, :], [128, DC * T])
        if stop == "attn":
            _finish(p)
            ctx_xr2.close()
            return nc, dump_outs

        f2_res = ffn_sublayer("f2", xr2_sl)

        ph = Phase(p, "out")
        for tc in range(T // 512):
            ph.op("sync", lambda e, tc=tc: e.dma_start(
                bass.AP(yT_d, tc * 512, [[T, 128], [128 * T, DC], [1, 512]]),
                bass.AP(xr2, tc * 512, [[DC * T, 128], [T, DC], [1, 512]])),
                inc=[("dou", 16)],
                waits=[f2_res[(tc, 0)], f2_res[(tc, 1)]])
        ph.run()
        _finish(p)
        ctx_xr2.close()

    return nc, dump_outs


def _finish(p):
    ph = Phase(p, "final_wait")
    sems = ["dxp", "dgp", "dgd", "dg0", "dg1", "dh0", "dh1", "dhw0", "dhw1",
            "dgw", "dsp", "drl", "dou", "dsr0", "dsr1", "dw10", "dw11"]
    ph.op("gpsimd", lambda e: e.nop(), waits=[p.now(s) for s in sems])
    ph.run()


def host_inputs(inputs, cfg=CFG):
    E, FC, DC = cfg.E, cfg.FC, cfg.DC
    x = np.asarray(inputs["L0_attn_x"])
    base = {
        "WrT": np.ascontiguousarray(np.asarray(inputs["L0_ffn_router_w"]).T),
        "rb": np.ascontiguousarray(
            np.asarray(inputs["L0_ffn_router_b"]).reshape(1, E)),
        "WeT": np.ascontiguousarray(
            np.asarray(inputs["L0_ffn_expert_w"]).transpose(0, 2, 1)),
        "Be": np.ascontiguousarray(np.asarray(inputs["L0_ffn_expert_b"])),
        "W1T": np.ascontiguousarray(np.asarray(inputs["L0_attn_W1"]).T),
        "b1T": np.ascontiguousarray(
            np.asarray(inputs["L0_attn_b1"]).reshape(FC, 128).T),
        "W2T": np.ascontiguousarray(np.asarray(inputs["L0_attn_W2"]).T),
        "b2T": np.ascontiguousarray(
            np.asarray(inputs["L0_attn_b2"]).reshape(DC, 128).T),
        "WqT": np.ascontiguousarray(np.asarray(inputs["L0_attn_Wq"]).T),
        "WkT": np.ascontiguousarray(np.asarray(inputs["L0_attn_Wk"]).T),
        "WvT": np.ascontiguousarray(np.asarray(inputs["L0_attn_Wv"]).T),
        "WoT": np.ascontiguousarray(np.asarray(inputs["L0_attn_Wo"]).T),
    }
    maps = []
    for c in range(x.shape[0]):
        m = dict(base)
        m["xT"] = np.ascontiguousarray(x[c].T)
        maps.append(m)
    return maps


def kernel(**inputs):
    nc, _ = build(CFG)
    in_maps = host_inputs(inputs, CFG)
    res = run_bass_kernel_spmd(nc, in_maps, core_ids=list(range(8)))
    out = np.stack([np.asarray(r["yT"]).T for r in res.results], axis=0)
    return np.ascontiguousarray(out.astype(np.float32))


# version 19
# speedup vs baseline: 1.0713x; 1.0587x over previous
import numpy as np
from contextlib import ExitStack

import concourse.bass as bass
import concourse.mybir as mybir
from concourse.bass_utils import run_bass_kernel_spmd

F32 = mybir.dt.float32
BF16 = mybir.dt.bfloat16
AF = mybir.ActivationFunctionType
ALU = mybir.AluOpType
AX = mybir.AxisListType

EPS = 1e-6
LN_EPS = 1e-5


class Cfg:
    def __init__(self, T=2048, D=1024, DFF=2048, H=16, E=8):
        assert D % 512 == 0 and DFF % 128 == 0 and T % 1024 == 0
        assert D // H == 64
        self.T, self.D, self.DFF, self.H, self.E = T, D, DFF, H, E
        self.DC = D // 128
        self.FC = DFF // 128
        self.TC = T // 128


CFG = Cfg()
NDUM = (2, 1)
ENGS = ("tensor", "vector", "scalar", "gpsimd", "sync")
_blk_ctr = [0]


class Prog:
    def __init__(self, nc, ctx):
        self.nc = nc
        self.ctx = ctx
        self.sems = {}
        self.cnt = {}

    def sem(self, name):
        if name not in self.sems:
            self.sems[name] = self.ctx.enter_context(self.nc.semaphore(name))
            self.cnt[name] = 0
        return self.sems[name]

    def now(self, name):
        self.sem(name)
        return (name, self.cnt[name])


class Phase:

    def __init__(self, p, name=""):
        self.p = p
        _blk_ctr[0] += 1
        self.name = "%s_%d" % (name, _blk_ctr[0])
        self.ops = {e: [] for e in ENGS}

    def op(self, eng, fn, inc=(), waits=()):
        p = self.p
        evs = []
        for sname, by in inc:
            p.sem(sname)
            p.cnt[sname] += by
            evs.append((sname, p.cnt[sname]))
        wd = {}
        for wv in waits:
            if wv is None:
                continue
            sname, val = wv
            wd[sname] = max(wd.get(sname, 0), val)
        self.ops[eng].append((sorted(wd.items()), fn, [(s, b) for s, b in inc]))
        return evs[0] if evs else None

    def run(self):
        p = self.p

        def mk_body(oplist):
            def body(eng):
                for waits, fn, incs in oplist:
                    for sname, val in waits:
                        if val > 0:
                            eng.wait_ge(p.sems[sname], val)
                    inst = fn(eng)
                    for sname, by in incs:
                        inst.then_inc(p.sems[sname], by)
            return body

        with p.nc.Block(self.name) as blk:
            for e in ENGS:
                if self.ops[e]:
                    getattr(blk, e)(mk_body(self.ops[e]))


def build(cfg=CFG, dumps=(), stop=None):
    T, D, DFF, H, E = cfg.T, cfg.D, cfg.DFF, cfg.H, cfg.E
    DC, FC, TC = cfg.DC, cfg.FC, cfg.TC
    KC = T // 128
    NS = T // 512
    QH = T // 1024

    nc = bass.Bass("TRN2", target_bir_lowering=False)

    xT_d = nc.dram_tensor("xT", [D, T], F32, kind="ExternalInput")
    WrT_d = nc.dram_tensor("WrT", [D, E], F32, kind="ExternalInput")
    rb_d = nc.dram_tensor("rb", [1, E], F32, kind="ExternalInput")
    WeT_d = nc.dram_tensor("WeT", [E, D, D], F32, kind="ExternalInput")
    Be_d = nc.dram_tensor("Be", [E, D], F32, kind="ExternalInput")
    W1T_d = nc.dram_tensor("W1T", [D, DFF], F32, kind="ExternalInput")
    b1T_d = nc.dram_tensor("b1T", [128, FC], F32, kind="ExternalInput")
    W2T_d = nc.dram_tensor("W2T", [DFF, D], F32, kind="ExternalInput")
    b2T_d = nc.dram_tensor("b2T", [128, DC], F32, kind="ExternalInput")
    WqT_d = nc.dram_tensor("WqT", [D, D], F32, kind="ExternalInput")
    WkT_d = nc.dram_tensor("WkT", [D, D], F32, kind="ExternalInput")
    WvT_d = nc.dram_tensor("WvT", [D, D], F32, kind="ExternalInput")
    WoT_d = nc.dram_tensor("WoT", [D, D], F32, kind="ExternalInput")
    yT_d = nc.dram_tensor("yT", [D, T], F32, kind="ExternalOutput")

    WeC_d = nc.dram_tensor("WeC", [E, D, D], BF16, kind="Internal")
    xsp_d = nc.dram_tensor("xsp", [D, T], F32, kind="Internal")

    dump_outs = {}

    def want(tag):
        return tag in dumps

    with ExitStack() as ctx0:
        p = Prog(nc, ctx0)

        def sb(name, shape, dtype, c=None, side="left"):
            return (c or ctx0).enter_context(
                nc.sbuf_tensor(name, shape, dtype, side=side))

        def ps(name, shape, dtype, c=None, side="left"):
            return (c or ctx0).enter_context(
                nc.psum_tensor(name, shape, dtype, side=side))

        ident = sb("ident", [128, 128], BF16)
        ones_col = sb("ones_col", [128, 1], BF16)
        ones_col0 = sb("ones_col0", [128, 1], BF16)
        mean_col = sb("mean_col", [128, 1], BF16)
        ones_row = sb("ones_row", [1, 128], BF16)
        sel_sb = sb("sel_sb", [8, E * 128], BF16)
        oh16 = sb("oh16", [H, H * 64], BF16)
        b1T = sb("b1T_s", [128, FC], F32)
        b2T = sb("b2T_s", [128, DC], F32)
        Be_bf = sb("Be_bf", [8, D], BF16)
        wr32 = sb("wr32", [128, DC * E], F32)
        wr_hi = sb("wr_hi", [128, DC * E], BF16)
        wr_lo = sb("wr_lo", [128, DC * E], BF16)
        rb32 = sb("rb32", [1, E], F32)
        rb_hi = sb("rb_hi", [1, E], BF16)
        rb_lo = sb("rb_lo", [1, E], BF16)
        negB = sb("negB", [128, 1], F32)
        c_lneps = sb("c_lneps", [128, 1], F32)
        c_1eps = sb("c_1eps", [128, 1], F32)
        c_neg1 = sb("c_neg1", [128, 1], F32)
        c_pos1 = sb("c_pos1", [128, 1], F32)

        ctx_xr = ExitStack()
        xr = sb("xr", [128, DC * T], F32, ctx_xr)

        def mk_sl(t):
            def sl(c, t0=0, tn=None):
                tn = T if tn is None else tn
                return t[:, c * T + t0: c * T + t0 + tn]
            return sl
        xr_sl = mk_sl(xr)

        ph = Phase(p, "prologue")
        ev_xT = ph.op("sync", lambda e: e.dma_start(
            xr[:, :].rearrange("p (c t) -> p c t", c=DC),
            xT_d[:, :].rearrange("(c p) t -> p c t", p=128)), inc=[("dxp", 16)])
        ph.op("sync", lambda e: e.dma_start(
            wr32[:, :].rearrange("p (c e) -> p c e", c=DC),
            WrT_d[:, :].rearrange("(c p) e -> p c e", p=128)), inc=[("dxp", 16)])
        ph.op("sync", lambda e: e.dma_start(rb32[:, :], rb_d[:, :]), inc=[("dxp", 16)])
        ph.op("sync", lambda e: e.dma_start(b1T[:, :], b1T_d[:, :]), inc=[("dxp", 16)])
        ph.op("sync", lambda e: e.dma_start(b2T[:, :], b2T_d[:, :]), inc=[("dxp", 16)])
        ev_be = ph.op("gpsimd", lambda e: e.dma_start(Be_bf[:, :], Be_d[:, :]), inc=[("dgp", 16)])
        ev = ph.op("gpsimd", lambda e: e.memset(ident[:, :], 0.0), inc=[("pl", 1)])
        ph.op("gpsimd", lambda e: e.affine_select(
            out=ident[:, :], in_=ident[:, :], compare_op=ALU.not_equal,
            fill=1.0, base=0, pattern=[[-1, 128]], channel_multiplier=1),
            inc=[("pl", 1)], waits=[ev])
        ph.op("gpsimd", lambda e: e.memset(ones_col[:, :], 1.0), inc=[("pl", 1)])
        ev = ph.op("gpsimd", lambda e: e.memset(ones_col0[:, :], 1.0), inc=[("pl", 1)])
        ph.op("gpsimd", lambda e: e.memset(ones_col0[0:1, :], 0.0),
              inc=[("pl", 1)], waits=[ev])
        ph.op("gpsimd", lambda e: e.memset(mean_col[:, :], 1.0 / D), inc=[("pl", 1)])
        ph.op("gpsimd", lambda e: e.memset(ones_row[:, :], 1.0), inc=[("pl", 1)])
        ph.op("gpsimd", lambda e: e.memset(c_lneps[:, :], LN_EPS), inc=[("pl", 1)])
        ph.op("gpsimd", lambda e: e.memset(c_1eps[:, :], 1.0 + EPS), inc=[("pl", 1)])
        ph.op("gpsimd", lambda e: e.memset(c_neg1[:, :], -1.0), inc=[("pl", 1)])
        ph.op("gpsimd", lambda e: e.memset(c_pos1[:, :], 1.0), inc=[("pl", 1)])
        ev = ph.op("gpsimd", lambda e: e.memset(sel_sb[:, :], 0.0), inc=[("pl", 1)])
        ph.op("gpsimd", lambda e: e.affine_select(
            out=bass.AP(sel_sb, 0, [[E * 128, 8], [128, E], [1, 128]]),
            in_=bass.AP(sel_sb, 0, [[E * 128, 8], [128, E], [1, 128]]),
            compare_op=ALU.not_equal, fill=1.0, base=0,
            pattern=[[-1, E], [0, 128]], channel_multiplier=1),
            inc=[("pl", 1)], waits=[ev])
        ev = ph.op("gpsimd", lambda e: e.memset(oh16[:, :], 0.0), inc=[("pl", 1)])
        ph.op("gpsimd", lambda e: e.affine_select(
            out=bass.AP(oh16, 0, [[H * 64, H], [64, H], [1, 64]]),
            in_=bass.AP(oh16, 0, [[H * 64, H], [64, H], [1, 64]]),
            compare_op=ALU.not_equal, fill=1.0, base=0,
            pattern=[[-1, H], [0, 64]], channel_multiplier=1),
            inc=[("pl", 1)], waits=[ev])
        ph.run()

        ph = Phase(p, "loads_wait")
        wall = [p.now("dxp"), p.now("dgp"), p.now("pl")]
        for e_ in ENGS:
            ph.op(e_, lambda e: e.nop(), waits=wall)
        ph.run()

        ph = Phase(p, "router_split")
        ev_hi = ph.op("scalar", lambda e: e.copy(wr_hi[:, :], wr32[:, :]),
                      inc=[("act", 1)])
        ph.op("vector", lambda e: e.tensor_sub(wr_lo[:, :], wr32[:, :], wr_hi[:, :]),
              inc=[("dve", 1)], waits=[ev_hi])
        ev_hi = ph.op("scalar", lambda e: e.copy(rb_hi[:, :], rb32[:, :]),
                      inc=[("act", 1)])
        ph.op("vector", lambda e: e.tensor_sub(rb_lo[:, :], rb32[:, :], rb_hi[:, :]),
              inc=[("dve", 1)], waits=[ev_hi])
        ph.run()

        def dump_sbuf(tag, ap, shape):
            d = nc.dram_tensor("dump_" + tag, list(shape), F32, kind="ExternalOutput")
            dump_outs[tag] = d
            ph = Phase(p, "dump_" + tag)
            ph.op("gpsimd", lambda e: e.dma_start(d[:, :], ap), inc=[("dgd", 16)])
            ph.op("gpsimd", lambda e: e.nop(), waits=[p.now("dgd")])
            ph.run()

        if want("xr0"):
            dump_sbuf("xr0", xr[:, :], [128, DC * T])

        def layernorm(tag, xcur_sl, out_hi, out_lo, chunk_consumer=None):
            with ExitStack() as c1:
                xbf = [sb("xbf%s%d" % (tag, i), [128, T], BF16, c1) for i in range(2)]
                sqb = [sb("sqb%s%d" % (tag, i), [128, T], BF16, c1) for i in range(2)]
                m_row = sb("m_row" + tag, [1, T], F32, c1)
                v2_row = sb("v2_row" + tag, [1, T], F32, c1)
                ra = sb("ra_row" + tag, [1, T], F32, c1)
                m_bf = sb("m_bf" + tag, [1, T], BF16, c1)
                r_bf = sb("r_bf" + tag, [1, T], BF16, c1)
                t32 = [sb("t32%s%d" % (tag, i), [128, T // 2], F32, c1)
                       for i in range(2)]
                with ExitStack() as c2:
                    pm = ps("pm" + tag, [1, T], F32, c2)
                    pv = ps("pv" + tag, [1, T], F32, c2)
                    ph = Phase(p, "ln_" + tag)
                    ev_mm = {}
                    ev_cs = {}

                    def emit_mm(c):
                        b = c % 2

                        def mm(e, c=c, b=b):
                            for n in range(NS):
                                e.matmul(pm[:, n * 512:(n + 1) * 512], mean_col[:, :],
                                         xbf[b][:, n * 512:(n + 1) * 512],
                                         start=(c == 0), stop=(c == DC - 1))
                            for n in range(NS):
                                i = e.matmul(pv[:, n * 512:(n + 1) * 512],
                                             mean_col[:, :],
                                             sqb[b][:, n * 512:(n + 1) * 512],
                                             start=(c == 0), stop=(c == DC - 1))
                            return i
                        ev_mm[c] = ph.op("tensor", mm, inc=[("pe", 1)],
                                         waits=list(ev_cs[c]))

                    for c in range(DC):
                        b = c % 2
                        w = [ev_mm[c - 2]] if (c - 2) in ev_mm else []
                        ev_cast = ph.op("scalar", lambda e, c=c, b=b: e.copy(
                            xbf[b][:, :], xcur_sl(c)), inc=[("act", 1)], waits=w)
                        ev_sq = ph.op("scalar", lambda e, c=c, b=b: e.activation(
                            sqb[b][:, :], xcur_sl(c), AF.Square),
                            inc=[("act", 1)], waits=w)
                        ev_cs[c] = (ev_cast, ev_sq)
                        if c >= 1:
                            emit_mm(c - 1)
                    emit_mm(DC - 1)
                    ev = ev_mm[DC - 1]
                    ev_m = ph.op("scalar", lambda e: e.copy(m_row[:, :], pm[:, :]),
                                 inc=[("act", 1)], waits=[ev])
                    ev_v2 = ph.op("scalar", lambda e: e.copy(v2_row[:, :], pv[:, :]),
                                  inc=[("act", 1)], waits=[ev])
                    ev_msq = ph.op("vector", lambda e: e.tensor_mul(
                        ra[:, :], m_row[:, :], m_row[:, :]), inc=[("dve", 1)],
                        waits=[ev_m])
                    ev_var = ph.op("vector", lambda e: e.tensor_sub(
                        v2_row[:, :], v2_row[:, :], ra[:, :]), inc=[("dve", 1)],
                        waits=[ev_msq, ev_v2])
                    ev_ln = ph.op("scalar", lambda e: e.activation(
                        ra[:, :], v2_row[:, :], AF.Ln, bias=c_lneps[0:1, 0:1]),
                        inc=[("act", 1)], waits=[ev_var])
                    ev_rs = ph.op("scalar", lambda e: e.activation(
                        ra[:, :], ra[:, :], AF.Exp, scale=-0.5),
                        inc=[("act", 1)], waits=[ev_ln])
                    ev_mbf = ph.op("scalar", lambda e: e.copy(m_bf[:, :], m_row[:, :]),
                                   inc=[("act", 1)], waits=[ev_m])
                    ev_rbf = ph.op("scalar", lambda e: e.copy(r_bf[:, :], ra[:, :]),
                                   inc=[("act", 1)], waits=[ev_rs])
                    ph.run()
                with ExitStack() as c2:
                    TH = T // 2
                    pmb = ps("pmb" + tag, [128, TH], F32, c2)
                    prb = ps("prb" + tag, [128, TH], F32, c2)
                    ph = Phase(p, "lnap_" + tag)
                    ev_hi = {}
                    ev_lo = {}
                    ev_ap = {}
                    for half in range(2):
                        t0 = half * TH
                        wbc = [ev_mbf, ev_rbf]
                        if half > 0:
                            wbc += [ev_ap[(c, 0)] for c in range(DC)]

                        def bc(e, t0=t0):
                            for n in range(TH // 512):
                                e.matmul(pmb[:, n * 512:(n + 1) * 512], ones_row[:, :],
                                         m_bf[:, t0 + n * 512: t0 + (n + 1) * 512],
                                         start=True, stop=True)
                            for n in range(TH // 512):
                                i = e.matmul(prb[:, n * 512:(n + 1) * 512],
                                             ones_row[:, :],
                                             r_bf[:, t0 + n * 512: t0 + (n + 1) * 512],
                                             start=True, stop=True)
                            return i
                        ev_bc = ph.op("tensor", bc, inc=[("pe", 1)], waits=wbc)
                        for c in range(DC):
                            b = c % 2
                            w = [ev_bc]
                            for k in ((c - 2, half), (DC - 2 + c, half - 1)):
                                if k in ev_ap:
                                    w.append(ev_ap[k])
                                    w.append(ev_hi[k])
                            ev_sub = ph.op("vector", lambda e, c=c, b=b, t0=t0: e.tensor_sub(
                                t32[b][:, :], xcur_sl(c, t0, TH), pmb[:, :]),
                                inc=[("dve", 1)], waits=w)
                            if out_lo is None:
                                ev_p = ph.op(
                                    "vector", lambda e, c=c, b=b, t0=t0: e.tensor_mul(
                                        out_hi[:, c * T + t0: c * T + t0 + TH],
                                        t32[b][:, :], prb[:, :]),
                                    inc=[("dve", 1)], waits=[ev_sub])
                                ev_hi[(c, half)] = ev_p
                            else:
                                ev_p = ph.op(
                                    "vector", lambda e, c=c, b=b: e.tensor_mul(
                                        t32[b][:, :], t32[b][:, :], prb[:, :]),
                                    inc=[("dve", 1)], waits=[ev_sub])
                                ev_hi[(c, half)] = ph.op(
                                    "scalar", lambda e, c=c, b=b, t0=t0: e.copy(
                                        out_hi[:, c * T + t0: c * T + t0 + TH],
                                        t32[b][:, :]), inc=[("act", 1)],
                                    waits=[ev_p])
                            last = ev_p
                            if out_lo is not None:
                                ev_lo[(c, half)] = ph.op(
                                    "vector", lambda e, c=c, b=b, t0=t0: e.tensor_sub(
                                        out_lo[:, c * T + t0: c * T + t0 + TH],
                                        t32[b][:, :],
                                        out_hi[:, c * T + t0: c * T + t0 + TH]),
                                    inc=[("dve", 1)], waits=[ev_hi[(c, half)]])
                                last = ev_lo[(c, half)]
                            ev_ap[(c, half)] = last
                            if chunk_consumer is not None:
                                chunk_consumer(ph, c, half, ev_hi[(c, half)],
                                               ev_lo.get((c, half)))
                    ph.run()

        ffn_state = {"cached": False}

        def ffn_sublayer(tag, xcur_sl):
            with ExitStack() as cf:
                xnhi = sb("xnhi" + tag, [128, DC * T], BF16, cf)
                xnlo_ctx = ExitStack()
                xnlo = sb("xnlo" + tag, [128, DC * T], BF16, xnlo_ctx, side="right")
                wm_tok = sb("wm_tok" + tag, [128, TC * E], BF16, cf)
                wmT = sb("wmT" + tag, [8, T], BF16, cf)
                scr8 = sb("scr8" + tag, [128, E], F32, cf)
                msk8 = sb("msk8" + tag, [128, E], F32, cf)
                rmax = sb("rmax" + tag, [128, 1], F32, cf)
                nmax = sb("nmax" + tag, [128, 1], F32, cf)
                sume = sb("sume" + tag, [128, 1], F32, cf)
                wrec = sb("wrec" + tag, [128, 1], F32, cf)
                rt_ctx = ExitStack()
                rt_last = {}
                plg_box = {}

                def router_consumer(ph, c, half, ev_hi_e, ev_lo_e):
                    if "plg" not in plg_box:
                        plg_box["plg"] = ps("plg" + tag, [128, TC * E], F32,
                                            rt_ctx, side="right")
                    plg = plg_box["plg"]
                    tcs = list(range(half * (TC // 2), (half + 1) * (TC // 2)))

                    def rmm(e, c=c, half=half, tcs=tuple(tcs)):
                        i = None
                        for tc in tcs:
                            for j, (lh, rh) in enumerate(
                                    ((xnhi, wr_hi), (xnhi, wr_lo), (xnlo, wr_hi))):
                                i = e.matmul(
                                    plg[:, tc * E:(tc + 1) * E],
                                    lh[:, c * T + tc * 128: c * T + (tc + 1) * 128],
                                    rh[:, c * E:(c + 1) * E],
                                    start=(c == 0 and half == 0
                                           and tc == tcs[0] and j == 0),
                                    stop=False)
                            if c == DC - 1:
                                e.matmul(plg[:, tc * E:(tc + 1) * E], ones_row[:, :],
                                         rb_hi[:, :], start=False, stop=False)
                                i = e.matmul(plg[:, tc * E:(tc + 1) * E],
                                             ones_row[:, :], rb_lo[:, :],
                                             start=False,
                                             stop=(half == 1 and tc == tcs[-1]))
                        return i
                    rt_last["ev"] = ph.op("tensor", rmm, inc=[("pe", 1)],
                                          waits=[ev_hi_e, ev_lo_e])

                layernorm(tag, xcur_sl, xnhi, xnlo, chunk_consumer=router_consumer)
                if want("ln" + tag):
                    dump_sbuf("ln" + tag, xnhi[:, :], [128, DC * T])
                    dump_sbuf("lo" + tag, xnlo[:, :], [128, DC * T])

                plg = plg_box["plg"]
                with ExitStack() as c2x:
                    ph = Phase(p, "rstats_" + tag)
                    ev_lg = rt_last["ev"]
                    rmax_t = sb("rmax_t" + tag, [128, TC], F32, cf)
                    sume_t = sb("sume_t" + tag, [128, TC], F32, cf)
                    scr_s = sb("scr_s" + tag, [128, TC * E], F32, cf)
                    scr_m = sb("scr_m" + tag, [128, TC * E], F32, cf)
                    plg3 = bass.AP(plg, 0, [[TC * E, 128], [E, TC], [1, E]])

                    def bc3(t):
                        return bass.AP(t, 0, [[TC, 128], [1, TC], [0, E]])
                    ev1 = ph.op("vector", lambda e: e.tensor_reduce(
                        rmax_t[:, :], plg3, axis=AX.X, op=ALU.max),
                        inc=[("dve", 1)], waits=[ev_lg])
                    ev2 = ph.op("vector", lambda e: e.tensor_tensor(
                        bass.AP(scr_s, 0, [[TC * E, 128], [E, TC], [1, E]]),
                        plg3, bc3(rmax_t), ALU.subtract),
                        inc=[("dve", 1)], waits=[ev1])
                    ev3 = ph.op("vector", lambda e: e.tensor_tensor(
                        bass.AP(scr_m, 0, [[TC * E, 128], [E, TC], [1, E]]),
                        plg3, bc3(rmax_t), ALU.is_equal),
                        inc=[("dve", 1)], waits=[ev1])
                    ev4 = ph.op("scalar", lambda e: e.activation(
                        scr_s[:, :], scr_s[:, :], AF.Exp), inc=[("act", 1)],
                        waits=[ev2])
                    ev5 = ph.op("vector", lambda e: e.tensor_reduce(
                        sume_t[:, :],
                        bass.AP(scr_s, 0, [[TC * E, 128], [E, TC], [1, E]]),
                        axis=AX.X, op=ALU.add), inc=[("dve", 1)], waits=[ev4])
                    ev6 = ph.op("vector", lambda e: e.reciprocal(
                        sume_t[:, :], sume_t[:, :]), inc=[("dve", 1)], waits=[ev5])
                    ev_wm = ph.op("vector", lambda e: e.scalar_tensor_tensor(
                        bass.AP(wm_tok, 0, [[TC * E, 128], [E, TC], [1, E]]),
                        bass.AP(scr_m, 0, [[TC * E, 128], [E, TC], [1, E]]),
                        0.5, bc3(sume_t), ALU.mult, ALU.mult),
                        inc=[("dve", 1)], waits=[ev3, ev6])
                    ph.run()
                rt_ctx.close()
                if want("wm" + tag):
                    dump_sbuf("wm" + tag, wm_tok[:, :], [128, TC * E])
                with ExitStack() as c3:
                    pwt = ps("pwt" + tag, [8, T], BF16, c3)
                    ph = Phase(p, "wmT_" + tag)

                    def tr(e):
                        for tc in range(TC):
                            i = e.transpose(pwt[:, tc * 128:(tc + 1) * 128],
                                            wm_tok[:, tc * E:(tc + 1) * E],
                                            ident[:, :])
                        return i
                    ev = ph.op("tensor", tr, inc=[("pe", 1)], waits=[ev_wm])
                    ev_wmT = ph.op("scalar", lambda e: e.copy(wmT[:, :], pwt[:, :]),
                                   inc=[("act", 1)], waits=[ev])
                    ph.run()
                xnlo_ctx.close()

                maskb = sb("maskb" + tag, [128, E * T], BF16, cf)
                with ExitStack() as c2:
                    pbc = [ps("pbc%s%d" % (tag, i), [128, T], F32, c2)
                           for i in range(2)]
                    ph = Phase(p, "maskb_" + tag)
                    ev_ev = [None, None]
                    for e_i in range(E):
                        b = e_i % 2
                        w = [ev_wmT]
                        if ev_ev[b]:
                            w.append(ev_ev[b])

                        def bmm(e, e_i=e_i, b=b):
                            for n in range(NS):
                                i = e.matmul(pbc[b][:, n * 512:(n + 1) * 512],
                                             sel_sb[:, e_i * 128:(e_i + 1) * 128],
                                             wmT[:, n * 512:(n + 1) * 512],
                                             start=True, stop=True)
                            return i
                        ev_mm = ph.op("tensor", bmm, inc=[("pe", 1)], waits=w)
                        ev_ev[b] = ph.op("scalar", lambda e, e_i=e_i, b=b: e.copy(
                            maskb[:, e_i * T:(e_i + 1) * T], pbc[b][:, :]),
                            inc=[("act", 1)], waits=[ev_mm])
                    ev_maskb = ev_ev[(E - 1) % 2]
                    ev_maskb2 = ev_ev[E % 2]
                    ph.run()

                wbuf = [sb("wbuf%s%d" % (tag, i), [128, DC * D], BF16, cf)
                        for i in range(2)]
                xnm = [sb("xnm%s%d" % (tag, i), [128, DC * 512], BF16, cf)
                       for i in range(2)]
                with ExitStack() as c2:
                    py = ps("py" + tag, [128, DC * 512], F32, c2)
                    ph = Phase(p, "experts_" + tag)
                    cached = ffn_state["cached"]
                    load_ev = {}
                    wb_ev = {}
                    mm_ev = {}
                    res_ev = {}
                    TQn = T // 512
                    nli = TQn * E

                    def emit_load(li):
                        tc_, e_ = divmod(li, E)
                        w = []
                        if li >= 2:
                            w.append(mm_ev[li - 2])
                            if (li - 2) in wb_ev:
                                w.append(wb_ev[li - 2])
                        if not cached and tc_ == 0:
                            load_ev[li] = ph.op(
                                "gpsimd", lambda e, e_=e_, li=li: e.dma_start(
                                    wbuf[li % 2][:, :].rearrange(
                                        "p (c n) -> p c n", c=DC),
                                    WeT_d[e_].rearrange("(c p) n -> p c n", p=128)),
                                inc=[("dg%d" % (li % 2), 16)], waits=w)
                            wb_ev[li] = ph.op("sync", lambda e, e_=e_, li=li: e.dma_start(
                                WeC_d[e_].rearrange("(c p) n -> p c n", p=128),
                                wbuf[li % 2][:, :].rearrange(
                                    "p (c n) -> p c n", c=DC)),
                                inc=[("dhw%d" % (li % 2), 16)], waits=[load_ev[li]])
                        else:
                            load_ev[li] = ph.op(
                                "sync", lambda e, e_=e_, li=li: e.dma_start(
                                    wbuf[li % 2][:, :].rearrange(
                                        "p (c n) -> p c n", c=DC),
                                    WeC_d[e_].rearrange("(c p) n -> p c n", p=128)),
                                inc=[("dh%d" % (li % 2), 16)], waits=w)

                    emit_load(0)
                    emit_load(1)
                    for tc in range(TQn):
                        for e_i in range(E):
                            li = tc * E + e_i
                            w = [ev_maskb, ev_maskb2]
                            if li >= 2:
                                w.append(mm_ev[li - 2])

                            def vmask(e, tc=tc, e_i=e_i, li=li):
                                for c in range(DC):
                                    i = e.tensor_mul(
                                        xnm[li % 2][:, c * 512:(c + 1) * 512],
                                        xnhi[:, c * T + tc * 512:
                                             c * T + tc * 512 + 512],
                                        maskb[:, e_i * T + tc * 512:
                                              e_i * T + tc * 512 + 512])
                                return i
                            ev_vm = ph.op("vector", vmask, inc=[("dve", 1)], waits=w)
                            w = [ev_vm, load_ev[li]]
                            if tc > 0 and e_i == 0:
                                w.append(res_ev[(tc - 1, 0)])

                            def emma(e, tc=tc, e_i=e_i, li=li):
                                for m in range(DC // 2):
                                    for c in range(DC):
                                        i = e.matmul(
                                            py[:, m * 512:(m + 1) * 512],
                                            wbuf[li % 2][:, c * D + m * 128:
                                                         c * D + (m + 1) * 128],
                                            xnm[li % 2][:, c * 512:(c + 1) * 512],
                                            start=(e_i == 0 and c == 0), stop=False)
                                return i

                            def emmb(e, tc=tc, e_i=e_i, li=li):
                                for m in range(DC // 2, DC):
                                    for c in range(DC):
                                        i = e.matmul(
                                            py[:, m * 512:(m + 1) * 512],
                                            wbuf[li % 2][:, c * D + m * 128:
                                                         c * D + (m + 1) * 128],
                                            xnm[li % 2][:, c * 512:(c + 1) * 512],
                                            start=(e_i == 0 and c == 0), stop=False)
                                return i
                            ph.op("tensor", emma, inc=[("pe", 1)], waits=w)
                            wb_ = [] if not (tc > 0 and e_i == 0) \
                                else [res_ev[(tc - 1, 1)]]
                            mm_ev[li] = ph.op("tensor", emmb, inc=[("pe", 1)],
                                              waits=wb_)
                            if li + 2 < nli:
                                emit_load(li + 2)

                        def bmm2a(e, tc=tc):
                            for m in range(DC // 2):
                                i = e.matmul(py[:, m * 512:(m + 1) * 512],
                                             Be_bf[:, m * 128:(m + 1) * 128],
                                             wmT[:, tc * 512: tc * 512 + 512],
                                             start=False, stop=True)
                            return i

                        def bmm2b(e, tc=tc):
                            for m in range(DC // 2, DC):
                                i = e.matmul(py[:, m * 512:(m + 1) * 512],
                                             Be_bf[:, m * 128:(m + 1) * 128],
                                             wmT[:, tc * 512: tc * 512 + 512],
                                             start=False, stop=True)
                            return i
                        ev_ba = ph.op("tensor", bmm2a, inc=[("pe", 1)],
                                      waits=[ev_wmT])
                        ev_bb = ph.op("tensor", bmm2b, inc=[("pe", 1)], waits=[])

                        def radda(e, tc=tc):
                            for m in range(DC // 2):
                                i = e.tensor_add(
                                    xcur_sl(m, tc * 512, 512),
                                    xcur_sl(m, tc * 512, 512),
                                    py[:, m * 512:(m + 1) * 512])
                            return i

                        def raddb(e, tc=tc):
                            for m in range(DC // 2, DC):
                                i = e.tensor_add(
                                    xcur_sl(m, tc * 512, 512),
                                    xcur_sl(m, tc * 512, 512),
                                    py[:, m * 512:(m + 1) * 512])
                            return i
                        res_ev[(tc, 0)] = ph.op("vector", radda, inc=[("dve", 1)],
                                                waits=[ev_ba])
                        res_ev[(tc, 1)] = ph.op("vector", raddb, inc=[("dve", 1)],
                                                waits=[ev_bb])
                    ph.run()
                return res_ev

        f1_res = ffn_sublayer("f1", xr_sl)
        ffn_state["cached"] = True
        if want("ffn1"):
            dump_sbuf("ffn1", xr[:, :], [128, DC * T])
        if stop == "ffn1":
            _finish(p)
            ctx_xr.close()
            return nc, dump_outs

        ctx_at1 = ExitStack()
        xnT = sb("xnT", [128, DC * T], BF16, ctx_at1, side="right")
        g_sb = sb("g_sb", [128, DC * T], BF16, ctx_at1, side="right")

        ph = Phase(p, "spill")
        ev_spill = ph.op("sync", lambda e: e.dma_start(
            xsp_d[:, :].rearrange("(c p) t -> p c t", p=128),
            xr[:, :].rearrange("p (c t) -> p c t", c=DC)), inc=[("dsp", 16)],
            waits=[f1_res[(T // 512 - 1, 0)], f1_res[(T // 512 - 1, 1)]])
        ph.run()

        vch_ctx = ExitStack()
        clipb = [sb("clipb%d" % i, [128, T // 2], BF16, vch_ctx, side="right")
                 for i in range(2)]
        sqvb = [sb("sqvb%d" % i, [128, T // 2], BF16, vch_ctx, side="right")
                for i in range(2)]
        pss_ctx = ExitStack()
        pss_box = {}
        vch_ev = {"cl": {}, "sq": {}, "mm": {}}

        def vchain_consumer(ph, c, half, ev_hi_e, ev_lo_e):
            if "pss" not in pss_box:
                pss_box["pss"] = ps("pss", [1, T], F32, pss_ctx, side="right")
            pss = pss_box["pss"]
            TH = T // 2
            t0 = half * TH
            gi = half * DC + c
            b = gi % 2
            w = [ev_hi_e]
            if gi >= 2:
                w.append(vch_ev["sq"][gi - 2])
            ev_cl = ph.op("vector", lambda e, c=c, t0=t0, b=b: e.tensor_scalar(
                clipb[b][:, :], xnT[:, c * T + t0: c * T + t0 + TH],
                8.0, -8.0, ALU.min, ALU.max), inc=[("dve", 1)], waits=w)
            vch_ev["cl"][gi] = ev_cl
            w2 = [ev_cl]
            if gi >= 2:
                w2.append(vch_ev["mm"][gi - 2])
            ev_sq = ph.op("scalar", lambda e, b=b: e.activation(
                sqvb[b][:, :], clipb[b][:, :], AF.Square),
                inc=[("act", 1)], waits=w2)
            vch_ev["sq"][gi] = ev_sq

            def smm(e, c=c, t0=t0, b=b):
                lhs = ones_col0 if c == 0 else ones_col
                for n in range(T // 2 // 512):
                    i = e.matmul(pss[:, t0 + n * 512: t0 + (n + 1) * 512],
                                 lhs[:, :], sqvb[b][:, n * 512:(n + 1) * 512],
                                 start=(c == 0), stop=(c == DC - 1))
                return i
            vch_ev["mm"][gi] = ph.op("tensor", smm, inc=[("pe", 1)], waits=[ev_sq])

        layernorm("a", xr_sl, xnT, None, chunk_consumer=vchain_consumer)
        ev_ssdone = [vch_ev["mm"][DC - 1], vch_ev["mm"][2 * DC - 1]]

        ph = Phase(p, "spill_wait")
        for e_ in ENGS:
            ph.op(e_, lambda e: e.nop(), waits=[ev_spill])
        ph.run()
        ctx_xr.close()

        if want("lna"):
            dump_sbuf("lna", xnT[:, :], [128, DC * T])
        if stop == "lna":
            _finish(p)
            ctx_at1.close()
            return nc, dump_outs

        vch_ctx.close()
        cv2 = ExitStack()
        R = [sb("vcr%d" % i, [1, T], F32, cv2) for i in range(5)]
        s_bf = sb("s_bf", [1, T], BF16, cv2)
        dv0s_bf = sb("dv0s_bf", [1, T], BF16, cv2)
        vce = {}

        def emit_vchain(ph):
            pss = pss_box["pss"]
            ev_ss = ph.op("scalar", lambda e: e.copy(R[0][:, :], pss[:, :]),
                          inc=[("act", 1)], waits=ev_ssdone)
            yield
            ev1 = ph.op("scalar", lambda e: e.activation(
                R[1][:, :], R[0][:, :], AF.Ln, bias=c_1eps[0:1, 0:1]),
                inc=[("act", 1)], waits=[ev_ss])
            yield
            ev2 = ph.op("scalar", lambda e: e.activation(
                R[1][:, :], R[1][:, :], AF.Exp, scale=0.5),
                inc=[("act", 1)], waits=[ev1])
            yield
            ev3 = ph.op("vector", lambda e: e.tensor_scalar_max(
                R[2][:, :], R[1][:, :], 1.0 + EPS), inc=[("dve", 1)], waits=[ev2])
            yield
            ev4 = ph.op("scalar", lambda e: e.activation(
                R[3][:, :], R[2][:, :], AF.Square), inc=[("act", 1)], waits=[ev3])
            yield
            ev5 = ph.op("scalar", lambda e: e.activation(
                R[3][:, :], R[3][:, :], AF.Ln, bias=c_neg1[0:1, 0:1]),
                inc=[("act", 1)], waits=[ev4])
            yield
            ev6 = ph.op("scalar", lambda e: e.activation(
                R[3][:, :], R[3][:, :], AF.Exp, scale=0.5), inc=[("act", 1)],
                waits=[ev5])
            yield
            ev7 = ph.op("vector", lambda e: e.tensor_add(
                R[3][:, :], R[3][:, :], R[2][:, :]), inc=[("dve", 1)],
                waits=[ev6])
            yield
            ev8 = ph.op("scalar", lambda e: e.activation(
                R[3][:, :], R[3][:, :], AF.Ln), inc=[("act", 1)], waits=[ev7])
            yield
            ev9 = ph.op("vector", lambda e: e.tensor_scalar_max(
                R[4][:, :], R[3][:, :], 0.001), inc=[("dve", 1)], waits=[ev8])
            yield
            ev10 = ph.op("vector", lambda e: e.tensor_sub(
                R[3][:, :], R[1][:, :], R[2][:, :]), inc=[("dve", 1)],
                waits=[ev3, ev9])
            yield
            ev11 = ph.op("scalar", lambda e: e.activation(
                R[1][:, :], R[3][:, :], AF.Square), inc=[("act", 1)],
                waits=[ev10, ev9])
            yield
            ev12 = ph.op("vector", lambda e: e.tensor_sub(
                R[1][:, :], R[0][:, :], R[1][:, :]), inc=[("dve", 1)],
                waits=[ev11])
            yield
            ev13 = ph.op("vector", lambda e: e.tensor_scalar_max(
                R[1][:, :], R[1][:, :], EPS), inc=[("dve", 1)], waits=[ev12])
            yield
            ev14 = ph.op("scalar", lambda e: e.activation(
                R[1][:, :], R[1][:, :], AF.Ln), inc=[("act", 1)], waits=[ev13])
            yield
            ev15 = ph.op("scalar", lambda e: e.activation(
                R[1][:, :], R[1][:, :], AF.Exp, scale=0.5), inc=[("act", 1)],
                waits=[ev14])
            yield
            ev16 = ph.op("vector", lambda e: e.tensor_scalar_add(
                R[1][:, :], R[1][:, :], EPS), inc=[("dve", 1)], waits=[ev15])
            yield
            ev17 = ph.op("vector", lambda e: e.reciprocal(
                R[1][:, :], R[1][:, :]), inc=[("dve", 1)], waits=[ev16])
            yield
            ev18 = ph.op("vector", lambda e: e.tensor_mul(
                R[2][:, :], R[4][:, :], R[1][:, :]), inc=[("dve", 1)],
                waits=[ev17, ev10])
            yield
            vce["sbf"] = ph.op("scalar", lambda e: e.copy(
                s_bf[:, :], R[2][:, :]), inc=[("act", 1)], waits=[ev18])
            yield
            ev19 = ph.op("vector", lambda e: e.tensor_mul(
                R[3][:, :], R[3][:, :], R[2][:, :]), inc=[("dve", 1)],
                waits=[ev18, ev11])
            yield
            vce["d0"] = ph.op("scalar", lambda e: e.copy(
                dv0s_bf[:, :], R[3][:, :]), inc=[("act", 1)], waits=[ev19])
            yield

        with ExitStack() as cm:
            W1s = sb("W1s", [128, DC * DFF], BF16, cm)
            W2s = sb("W2s", [128, FC * D], BF16, cm)
            h1 = sb("h1", [128, FC * 512], BF16, cm)
            th = [sb("th%d" % i, [128, 512], F32, cm) for i in range(2)]
            ph = Phase(p, "metric")
            vgen = emit_vchain(ph)
            HC2 = DC // 2
            ev_w1h = []
            for hw in range(2):
                ev_w1h.append(ph.op("gpsimd", lambda e, hw=hw: e.dma_start(
                    W1s[:, hw * HC2 * DFF:(hw + 1) * HC2 * DFF].rearrange(
                        "p (c n) -> p c n", c=HC2),
                    W1T_d[hw * (D // 2):(hw + 1) * (D // 2), :].rearrange(
                        "(c p) n -> p c n", p=128)),
                    inc=[("dw1%d" % hw, 16)]))
            ev_w1 = ev_w1h[1]
            ev_w2 = ph.op("gpsimd", lambda e: e.dma_start(
                W2s[:, :].rearrange("p (c n) -> p c n", c=FC),
                W2T_d[:, :].rearrange("(c p) n -> p c n", p=128)), inc=[("dgw", 16)])
            ev_adj = ph.op("vector", lambda e: e.tensor_scalar_add(
                xnT[0:1, 0:T], xnT[0:1, 0:T], -1.0), inc=[("dve", 1)])
            with ExitStack() as c2:
                ph1 = [ps("ph1%d" % i, [128, 512], F32, c2) for i in range(2)]
                pg = [ps("pg%d" % i, [128, 512], F32, c2) for i in range(2)]
                ev_h1ev = {}
                ev_thev = {}
                ev_gw = {}
                for q in range(T // 512):
                    for m in range(FC):
                        b = m % 2
                        w = [ev_adj]
                        if (q, m - 2) in ev_h1ev:
                            w.append(ev_h1ev[(q, m - 2)])
                        elif q > 0:
                            w.append(ev_h1ev[(q - 1, FC - 2 + m)])
                        if q > 0 and m < 2:
                            w.append(ev_gw[(q - 1, DC - 1)])

                        def h1mma(e, q=q, m=m, b=b):
                            for c in range(HC2):
                                i = e.matmul(
                                    ph1[b][:, :],
                                    W1s[:, c * DFF + m * 128: c * DFF + (m + 1) * 128],
                                    xnT[:, c * T + q * 512: c * T + q * 512 + 512],
                                    start=(c == 0), stop=False)
                            return i

                        def h1mmb(e, q=q, m=m, b=b):
                            for c in range(HC2, DC):
                                i = e.matmul(
                                    ph1[b][:, :],
                                    W1s[:, c * DFF + m * 128: c * DFF + (m + 1) * 128],
                                    xnT[:, c * T + q * 512: c * T + q * 512 + 512],
                                    start=False, stop=(c == DC - 1))
                            return i
                        wa = [x for x in w if x is not ev_w2] + [ev_w1h[0]]
                        ph.op("tensor", h1mma, inc=[("pe", 1)], waits=wa)
                        ev_mm = ph.op("tensor", h1mmb, inc=[("pe", 1)],
                                      waits=[ev_w1h[1]])
                        ev_h1ev[(q, m)] = ph.op(
                            "scalar", lambda e, q=q, m=m, b=b: e.activation(
                                h1[:, m * 512:(m + 1) * 512], ph1[b][:, :], AF.Tanh,
                                bias=b1T[:, m:m + 1]), inc=[("act", 1)],
                            waits=[ev_mm])
                        next(vgen, None)
                    for md in range(DC):
                        b = md % 2
                        w = [ev_w2, ev_h1ev[(q, FC - 1)]]
                        if (q, md - 2) in ev_thev:
                            w.append(ev_thev[(q, md - 2)])
                        elif q > 0:
                            w.append(ev_thev[(q - 1, DC - 2 + md)])

                        def gmm(e, q=q, md=md, b=b):
                            for c in range(FC):
                                i = e.matmul(
                                    pg[b][:, :],
                                    W2s[:, c * D + md * 128: c * D + (md + 1) * 128],
                                    h1[:, c * 512:(c + 1) * 512],
                                    start=(c == 0), stop=(c == FC - 1))
                            return i
                        ev_gw[(q, md)] = ph.op("tensor", gmm, inc=[("pe", 1)], waits=w)
                        ev_th = ph.op("scalar", lambda e, q=q, md=md, b=b: e.activation(
                            th[b][:, :], pg[b][:, :], AF.Tanh, bias=b2T[:, md:md + 1]),
                            inc=[("act", 1)], waits=[ev_gw[(q, md)]])
                        ev_thev[(q, md)] = ev_th
                        ev_ab = ph.op("scalar", lambda e, q=q, md=md, b=b: e.activation(
                            g_sb[:, md * T + q * 512: md * T + q * 512 + 512],
                            th[b][:, :], AF.Abs, scale=0.1, bias=c_pos1[:, 0:1]),
                            inc=[("act", 1)], waits=[ev_th])
                        if md == 0:
                            ev_thev[(q, md)] = ph.op(
                                "scalar", lambda e, q=q, b=b: e.activation(
                                    g_sb[0:1, q * 512: q * 512 + 512],
                                    th[b][0:1, :], AF.Abs, scale=0.1, bias=c_neg1[0:1, 0:1]),
                                inc=[("act", 1)], waits=[ev_ab])
                        next(vgen, None)
                for _ in vgen:
                    pass
                ph.run()
        if want("g"):
            dump_sbuf("g", g_sb[:, :], [128, DC * T])
        if stop == "g":
            _finish(p)
            ctx_at1.close()
            return nc, dump_outs

        ctx_at2 = ExitStack()

        with ExitStack() as cv:
            clipv = [sb("clipv%d" % i, [128, T], BF16, cv) for i in range(2)]
            ev_sbf = vce["sbf"]
            ev_d0 = vce["d0"]
            pss_ctx.close()
            vch_ctx.close()
            wpb = [sb("wpb%d" % i, [128, DC * D], BF16, ctx_at1, side="right")
                   for i in range(2)]
            with ExitStack() as c2:
                psb = ps("psb", [128, T], F32, c2)
                ph = Phase(p, "vbuild")
                ev_lq = ph.op("gpsimd", lambda e: e.dma_start(
                    wpb[0][:, :].rearrange("p (c n) -> p c n", c=DC),
                    WqT_d[:, :].rearrange("(c p) n -> p c n", p=128)),
                    inc=[("dgw", 16)])
                ev_lk = ph.op("gpsimd", lambda e: e.dma_start(
                    wpb[1][:, :].rearrange("p (c n) -> p c n", c=DC),
                    WkT_d[:, :].rearrange("(c p) n -> p c n", p=128)),
                    inc=[("dgw", 16)])

                def sbc(e):
                    for n in range(NS):
                        i = e.matmul(psb[:, n * 512:(n + 1) * 512], ones_row[:, :],
                                     s_bf[:, n * 512:(n + 1) * 512],
                                     start=True, stop=True)
                    return i
                ev_bc = ph.op("tensor", sbc, inc=[("pe", 1)], waits=[ev_sbf])
                ev_v0 = None
                ev_cl2 = {}
                ev_vv2 = {}
                for c in range(DC):
                    b = c % 2
                    w = []
                    if c >= 2:
                        w.append(ev_vv2[c - 2])
                    ev_cl2[c] = ph.op("vector", lambda e, c=c, b=b: e.tensor_scalar(
                        clipv[b][:, :], xnT[:, c * T:(c + 1) * T],
                        8.0, -8.0, ALU.min, ALU.max), inc=[("dve", 1)], waits=w)
                    ev_vv2[c] = ph.op("vector", lambda e, c=c, b=b: e.tensor_mul(
                        xnT[:, c * T:(c + 1) * T], clipv[b][:, :],
                        psb[:, :]), inc=[("dve", 1)], waits=[ev_cl2[c], ev_bc])
                    if c == 0:
                        ev_v0 = ev_vv2[c]
                ev_row0 = ph.op("scalar", lambda e: e.copy(
                    xnT[0:1, 0:T], dv0s_bf[:, :]),
                    inc=[("act", 1)], waits=[ev_v0, ev_d0])
                ph.run()
        cv2.close()
        vT = xnT
        if want("vT"):
            dump_sbuf("vT", vT[:, :], [128, DC * T])
        if stop == "vT":
            _finish(p)
            ctx_at1.close()
            return nc, dump_outs

        QG = sb("QG", [128, DC * T], BF16, ctx_at2)
        with ExitStack() as c2:
            pq = [ps("pq%d" % i, [128, T], F32, c2) for i in range(2)]
            ph = Phase(p, "qproj")
            ev_qg = {}
            for m in range(DC):
                b = m % 2
                w = [ev_lk]
                if m >= 2:
                    w.append(ev_qg[m - 2])
                else:
                    w += [ev_row0, ev_vv2[DC - 1], ev_vv2[DC - 2]]

                def qmm(e, m=m, b=b):
                    for c in range(DC):
                        for n in range(NS):
                            i = e.matmul(
                                pq[b][:, n * 512:(n + 1) * 512],
                                wpb[0][:, c * D + m * 128: c * D + (m + 1) * 128],
                                vT[:, c * T + n * 512: c * T + (n + 1) * 512],
                                start=(c == 0), stop=(c == DC - 1))
                    return i
                ev_mm = ph.op("tensor", qmm, inc=[("pe", 1)], waits=w)
                ev_qg[m] = ph.op("vector", lambda e, m=m, b=b: e.tensor_mul(
                    QG[:, m * T:(m + 1) * T], pq[b][:, :],
                    g_sb[:, m * T:(m + 1) * T]), inc=[("dve", 1)], waits=[ev_mm])
            ev_neg = ph.op("vector", lambda e: e.tensor_scalar_mul(
                QG[0:1, 0:T], QG[0:1, 0:T], -1.0), inc=[("dve", 1)],
                waits=[ev_qg[0]])
            ph.run()

            kT = sb("kT", [128, DC * T], BF16, ctx_at2)
            vv = sb("vv", [128, TC * H * 65], BF16, ctx_at2)
            ph = Phase(p, "kproj")
            ev_kev = {}
            for m in range(DC):
                b = m % 2
                w = [ev_lk]
                if m >= 2:
                    w.append(ev_kev[m - 2])
                else:
                    w.append(ev_qg[DC - 2 + m])
                    w.append(ev_neg)

                def kmm(e, m=m, b=b):
                    for c in range(DC):
                        for n in range(NS):
                            i = e.matmul(
                                pq[b][:, n * 512:(n + 1) * 512],
                                wpb[1][:, c * D + m * 128: c * D + (m + 1) * 128],
                                vT[:, c * T + n * 512: c * T + (n + 1) * 512],
                                start=(c == 0), stop=(c == DC - 1))
                    return i
                ev_mm = ph.op("tensor", kmm, inc=[("pe", 1)], waits=w)
                ev_kev[m] = ph.op("scalar", lambda e, m=m, b=b: e.copy(
                    kT[:, m * T:(m + 1) * T], pq[b][:, :]), inc=[("act", 1)],
                    waits=[ev_mm])
            ev_lv = ph.op("gpsimd", lambda e: e.dma_start(
                wpb[0][:, :].rearrange("p (c n) -> p c n", c=DC),
                WvT_d[:, :].rearrange("(c p) n -> p c n", p=128)),
                inc=[("dgw", 16)], waits=[ev_qg[DC - 1]])
            ph.run()

        cb = ExitStack()
        sqs2 = [sb("sqb2%d" % i, [128, T // 2], BF16, cb) for i in range(2)]
        r_m = sb("r_m", [1, 1], F32, cb)
        nb_bf = sb("nb_bf", [1, 1], BF16, cb)
        with ExitStack() as c2:
            pv = [ps("pv%d" % i, [128, D], F32, c2) for i in range(2)]
            pqn = ps("pqn", [1, T], F32, c2)
            ph = Phase(p, "vvproj")
            ev_one = ph.op("gpsimd", lambda e: e.memset(
                bass.AP(vv, 64, [[TC * H * 65, 128], [H * 65, TC], [65, H], [1, 1]]),
                1.0), inc=[("pl", 1)])
            ev_vev = {}
            ev_sq = {}
            ev_nmm = {}
            norm_srcs = []
            for s_, c, ev_ in ([(QG, c, ev_qg[c]) for c in range(DC)] +
                               [(kT, c, ev_kev[c]) for c in range(DC)]):
                for hh in (0, 1):
                    norm_srcs.append((s_, c, hh, ev_))
            for tt in range(TC):
                b = tt % 2
                w = [ev_lv]
                if tt >= 2:
                    w.append(ev_vev[tt - 2])

                def vmm(e, tt=tt, b=b):
                    for c in range(DC):
                        for n in range(D // 512):
                            i = e.matmul(
                                pv[b][:, n * 512:(n + 1) * 512],
                                vT[:, c * T + tt * 128: c * T + (tt + 1) * 128],
                                wpb[0][:, c * D + n * 512: c * D + (n + 1) * 512],
                                start=(c == 0), stop=(c == DC - 1))
                    return i
                ev_mm = ph.op("tensor", vmm, inc=[("pe", 1)], waits=w)
                ev_vev[tt] = ph.op("scalar", lambda e, tt=tt, b=b: e.copy(
                    bass.AP(vv, tt * H * 65, [[TC * H * 65, 128], [65, H], [1, 64]]),
                    bass.AP(pv[b], 0, [[D, 128], [64, H], [1, 64]])),
                    inc=[("act", 1)], waits=[ev_mm, ev_one])
                NK = len(norm_srcs)
                TH2 = T // 2

                def emit_nmm(k):
                    s_, c, hh, ev_ = norm_srcs[k]

                    def nmm(e, k=k, c=c, hh=hh):
                        for n in range(TH2 // 512):
                            i = e.matmul(
                                pqn[:, hh * TH2 + n * 512: hh * TH2 + (n + 1) * 512],
                                ones_col[:, :],
                                sqs2[k % 2][:, n * 512:(n + 1) * 512],
                                start=(c == 0 and s_ is QG),
                                stop=(c == DC - 1 and s_ is kT))
                        return i
                    ev_nmm[k] = ph.op("tensor", nmm, inc=[("pe", 1)],
                                      waits=[ev_sq[k]])

                for ksub in (2 * tt, 2 * tt + 1):
                    if ksub < NK:
                        src_t, c, hh, ev_src = norm_srcs[ksub]
                        w2 = [ev_src]
                        if (ksub - 2) in ev_nmm:
                            w2.append(ev_nmm[ksub - 2])
                        ev_sq[ksub] = ph.op(
                            "scalar",
                            lambda e, src_t=src_t, c=c, hh=hh, ksub=ksub:
                            e.activation(
                                sqs2[ksub % 2][:, :],
                                src_t[:, c * T + hh * TH2:
                                      c * T + (hh + 1) * TH2],
                                AF.Square), inc=[("act", 1)], waits=w2)
                        if ksub >= 1:
                            emit_nmm(ksub - 1)
            emit_nmm(NK - 1)
            ev = ev_nmm[NK - 1]
            ev2 = ph.op("vector", lambda e: e.reduce_max(
                r_m[:, :], pqn[:, :], axis=AX.X), inc=[("dve", 1)], waits=[ev])
            ev3 = ph.op("vector", lambda e: e.tensor_scalar(
                r_m[:, :], r_m[:, :], 1.5 / 16.0, 43.0, ALU.mult, ALU.min),
                inc=[("dve", 1)], waits=[ev2])
            ev4 = ph.op("vector", lambda e: e.tensor_scalar_mul(
                r_m[:, :], r_m[:, :], -1.0), inc=[("dve", 1)], waits=[ev3])
            ev5 = ph.op("scalar", lambda e: e.copy(nb_bf[:, :], r_m[:, :]),
                        inc=[("act", 1)], waits=[ev4])
            ph.run()
        ctx_at1.close()

        if want("QG"):
            dump_sbuf("QG", QG[:, :], [128, DC * T])
            dump_sbuf("kT", kT[:, :], [128, DC * T])
            dump_sbuf("vv", vv[:, :], [128, TC * H * 65])
        if stop == "QG":
            _finish(p)
            ctx_at2.close()
            return nc, dump_outs

        with ExitStack() as c2:
            pnb = ps("pnb", [128, 1], F32, c2)
            ph = Phase(p, "bbcast")
            ev6 = ph.op("tensor", lambda e: e.matmul(
                pnb[:, :], ones_row[:, :], nb_bf[:, :], start=True, stop=True),
                inc=[("pe", 1)], waits=[ev5])
            ev_nb = ph.op("scalar", lambda e: e.copy(negB[:, :], pnb[:, :]),
                          inc=[("act", 1)], waits=[ev6])
            ph.run()
        cb.close()

        if want("bb"):
            dump_sbuf("bb", negB[:, :], [128, 1])
        if stop == "bb":
            _finish(p)
            ctx_at2.close()
            return nc, dump_outs

        ctx_at3 = ExitStack()
        avout = sb("avout", [128, DC * T], BF16, ctx_at3, side="right")
        wo_sb = sb("wo_sb", [128, DC * D], BF16, ctx_at3, side="right")
        ph = Phase(p, "wo_load")
        ev_lwo = ph.op("gpsimd", lambda e: e.dma_start(
            wo_sb[:, :].rearrange("p (c n) -> p c n", c=DC),
            WoT_d[:, :].rearrange("(c p) n -> p c n", p=128)), inc=[("dgw", 16)])
        ph.run()

        with ExitStack() as ca:
            NB = 3
            NPT = 5
            PT = [sb("PT%d" % i, [128, 1024], BF16, ca) for i in range(NPT)]
            srow = [sb("srow%d" % i, [1, 512], F32, ca) for i in range(4)]
            sums16 = sb("sums16", [H, T], F32, ca)
            QQ = T // 512
            with ExitStack() as c2:
                pS = [ps("pS%d" % i, [128, 1024], F32, c2) for i in range(NB)]
                pav = [ps("pav%d" % i, [65, 512], F32, c2) for i in range(2)]
                ph = Phase(p, "attn_core")
                ev_exp = {}
                ev_avp = {}
                ev_evac = {}
                ev_srr = {}
                ev_sdma = {}
                it_g = [0]
                for hp in range(H // 2):
                    for qq in range(QQ):
                        hq = hp * QQ + qq

                        def emit_av(kc, hq=hq, hp=hp, qq=qq):
                            it = hq * KC + kc
                            w = [ev_exp[it]]
                            if kc == 0 and hq > 0:
                                w.append(ev_evac[(hq - 1, 0)])
                                w.append(ev_evac[(hq - 1, 1)])
                                w.append(ev_srr[(hq - 1, 0)])
                                w.append(ev_srr[(hq - 1, 1)])

                            def avmm(e, hp=hp, kc=kc, it=it):
                                for j in range(2):
                                    i = e.matmul(
                                        pav[j][:, :],
                                        vv[:, kc * H * 65 + (2 * hp + j) * 65:
                                           kc * H * 65 + (2 * hp + j) * 65 + 65],
                                        PT[it % NPT][:, j * 512:(j + 1) * 512],
                                        start=(kc == 0), stop=(kc == KC - 1))
                                return i
                            ev_avp[it] = ph.op("tensor", avmm, inc=[("pe", 1)],
                                               waits=w)

                        for kc in range(KC):
                            it = hq * KC + kc
                            b = kc % NB
                            w = []
                            if it >= NB:
                                w.append(ev_exp[it - NB])

                            def smm(e, hp=hp, qq=qq, kc=kc, b=b):
                                q0 = qq * 512
                                e.matmul(
                                    pS[b][:, 0:512],
                                    kT[0:64, hp * T + kc * 128:
                                       hp * T + (kc + 1) * 128],
                                    QG[0:64, hp * T + q0: hp * T + q0 + 512],
                                    start=True, stop=True)
                                i = e.matmul(
                                    pS[b][:, 512:1024],
                                    kT[64:128, hp * T + kc * 128:
                                       hp * T + (kc + 1) * 128],
                                    QG[64:128, hp * T + q0: hp * T + q0 + 512],
                                    start=True, stop=True)
                                return i
                            ev_s = ph.op("tensor", smm, inc=[("pe", 1)], waits=w)
                            wv_ = [ev_s, ev_nb]
                            if it >= NPT:
                                wv_.append(ev_avp[it - NPT])
                            ev_exp[it] = ph.op(
                                "scalar", lambda e, it=it, b=b: e.activation(
                                    PT[it % NPT][:, :], pS[b][:, :], AF.Exp,
                                    bias=negB[:, 0:1], scale=0.125),
                                inc=[("act", 1)], waits=wv_)
                            if kc >= 1:
                                emit_av(kc - 1)
                        emit_av(KC - 1)
                        for j in range(2):
                            half = j * 64
                            ev_evac[(hq, j)] = ph.op(
                                "vector",
                                lambda e, hp=hp, qq=qq, j=j, half=half: e.tensor_copy(
                                    avout[half:half + 64,
                                          hp * T + qq * 512: hp * T + qq * 512 + 512],
                                    pav[j][0:64, :]),
                                inc=[("dve", 1)],
                                waits=[ev_avp[hq * KC + KC - 1]])
                            si = 2 * (hq % 2) + j
                            w2 = [ev_avp[hq * KC + KC - 1]]
                            prev = ev_sdma.get((hq - 2, j))
                            if prev:
                                w2.append(prev)
                            ev_sr = ph.op("vector", lambda e, j=j, si=si: e.tensor_copy(
                                srow[si][:, :], pav[j][64:65, :]),
                                inc=[("dve", 1)], waits=w2)
                            ev_srr[(hq, j)] = ev_sr
                            ev_sdma[(hq, j)] = ph.op(
                                "sync", lambda e, hp=hp, qq=qq, j=j, si=si: e.dma_start(
                                    sums16[2 * hp + j: 2 * hp + j + 1,
                                           qq * 512:(qq + 1) * 512], srow[si][:, :]),
                                inc=[("dsr%d" % j, 16)], waits=[ev_sr])
                ph.run()
            rec16 = sb("rec16", [H, T], F32, ca)
            rec16b = sb("rec16b", [H, T], BF16, ca)
            ev_attn_done = []
            with ExitStack() as c2:
                pnorm = [ps("pnorm%d" % i, [128, T], F32, c2) for i in range(2)]
                ph = Phase(p, "avnorm")
                ev_r = ph.op("vector", lambda e: e.reciprocal(
                    rec16[:, :], sums16[:, :]), inc=[("dve", 1)],
                    waits=[p.now("dsr0"), p.now("dsr1")])
                ev_rb = ph.op("scalar", lambda e: e.copy(rec16b[:, :], rec16[:, :]),
                              inc=[("act", 1)], waits=[ev_r])
                nrm_ev = {}
                ev_b_last = [None]
                for hp in range(DC):
                    pb = hp % 2
                    w = [ev_rb, ev_evac[(hp * (T // 512) + T // 512 - 1, 1)]]
                    if hp >= 2:
                        w.append(nrm_ev[hp - 2])

                    def nbc(e, hp=hp, pb=pb):
                        for j in range(2):
                            hh = 2 * hp + j
                            for n in range(NS):
                                i = e.matmul(
                                    pnorm[pb][64 * j:64 * (j + 1),
                                              n * 512:(n + 1) * 512],
                                    oh16[:, hh * 64:(hh + 1) * 64],
                                    rec16b[:, n * 512:(n + 1) * 512],
                                    start=True, stop=True)
                        return i
                    ev_b = ph.op("tensor", nbc, inc=[("pe", 1)], waits=w)
                    ev_b_last[0] = ev_b
                    nrm_ev[hp] = ph.op("vector", lambda e, hp=hp, pb=pb: e.tensor_mul(
                        avout[:, hp * T:(hp + 1) * T],
                        avout[:, hp * T:(hp + 1) * T], pnorm[pb][:, :]),
                        inc=[("dve", 1)], waits=[ev_b])
                    ev_attn_done.append(nrm_ev[hp])
                ph.run()
        ctx_at2.close()

        if want("avout"):
            dump_sbuf("avout", avout[:, :], [128, DC * T])
        if stop == "avout":
            _finish(p)
            ctx_at3.close()
            return nc, dump_outs

        ctx_xr2 = ExitStack()
        xr2 = sb("xr2", [128, DC * T], F32, ctx_xr2)
        xr2_sl = mk_sl(xr2)

        ph = Phase(p, "reload")
        ev_rel = ph.op("sync", lambda e: e.dma_start(
            xr2[:, :].rearrange("p (c t) -> p c t", c=DC),
            xsp_d[:, :].rearrange("(c p) t -> p c t", p=128)),
            inc=[("drl", 16)], waits=[ev_b_last[0]])
        ph.run()

        with ExitStack() as cw:
            sp_sb = sb("sp_sb", [128, DC * 512], BF16, cw)
            sqw = [sb("sqw%d" % i, [128, 512], BF16, cw) for i in range(2)]
            t32w = [sb("t32w%d" % i, [128, 512], F32, cw) for i in range(2)]
            r_o0 = sb("r_o0", [1, 512], F32, cw)
            r_x0 = sb("r_x0", [1, 512], F32, cw)
            with ExitStack() as c2:
                pw = [ps("pw%d" % i, [128, 512], F32, c2) for i in range(2)]
                psq = ps("psq", [1, 512], F32, c2)
                ph = Phase(p, "wo_tail")
                ev_sp = {}
                ev_sqev = {}
                ev_smm = {}
                ev_add = {}
                ev_o1 = {}
                ev_x0c = {}
                for ts_ in range(NS):
                    for md in range(DC):
                        b = md % 2
                        w = [ev_lwo] + list(ev_attn_done)
                        if (ts_, md - 2) in ev_sp:
                            w.append(ev_sp[(ts_, md - 2)])
                        elif ts_ > 0:
                            w.append(ev_sp[(ts_ - 1, DC - 2 + md)])

                        def wmma(e, ts_=ts_, md=md, b=b):
                            for c in range(DC // 2):
                                i = e.matmul(
                                    pw[b][:, :],
                                    wo_sb[:, c * D + md * 128: c * D + (md + 1) * 128],
                                    avout[:, c * T + ts_ * 512: c * T + ts_ * 512 + 512],
                                    start=(c == 0), stop=False)
                            return i

                        def wmmb(e, ts_=ts_, md=md, b=b):
                            for c in range(DC // 2, DC):
                                i = e.matmul(
                                    pw[b][:, :],
                                    wo_sb[:, c * D + md * 128: c * D + (md + 1) * 128],
                                    avout[:, c * T + ts_ * 512: c * T + ts_ * 512 + 512],
                                    start=False, stop=(c == DC - 1))
                            return i
                        ph.op("tensor", wmma, inc=[("pe", 1)],
                              waits=[x for x in w if x not in ev_attn_done]
                              + ev_attn_done[:DC // 2])
                        ev_mm = ph.op("tensor", wmmb, inc=[("pe", 1)],
                                      waits=ev_attn_done[DC // 2:])
                        w2 = [ev_mm]
                        if ts_ > 0:
                            w2.append(ev_add[(ts_ - 1, md)])
                            w2.append(ev_sqev[(ts_ - 1, md)])
                        ev_t1 = ph.op("vector", lambda e, md=md, b=b: e.tensor_scalar(
                            t32w[b][:, :], pw[b][:, :], 0.1, 8.0, ALU.mult, ALU.min),
                            inc=[("dve", 1)], waits=w2)
                        ev_sp[(ts_, md)] = ph.op(
                            "vector", lambda e, md=md, b=b: e.tensor_scalar_max(
                                sp_sb[:, md * 512:(md + 1) * 512], t32w[b][:, :],
                                -8.0), inc=[("dve", 1)], waits=[ev_t1])
                        w3 = [ev_sp[(ts_, md)]]
                        if (ts_, md - 2) in ev_smm:
                            w3.append(ev_smm[(ts_, md - 2)])
                        elif ts_ > 0 and md < 2:
                            w3.append(ev_smm[(ts_ - 1, DC - 2 + md)])
                        ev_sqev[(ts_, md)] = ph.op(
                            "scalar", lambda e, md=md, b=b: e.activation(
                                sqw[b][:, :], sp_sb[:, md * 512:(md + 1) * 512],
                                AF.Square), inc=[("act", 1)], waits=w3)

                        def emit_sqmm(mk, ts_=ts_):
                            w4 = [ev_sqev[(ts_, mk)]]
                            if mk == 0 and ts_ > 0:
                                w4.append(ev_o1[ts_ - 1])

                            def sqmm(e, mk=mk):
                                lhs = ones_col0 if mk == 0 else ones_col
                                return e.matmul(psq[:, :], lhs[:, :],
                                                sqw[mk % 2][:, :],
                                                start=(mk == 0),
                                                stop=(mk == DC - 1))
                            ev_smm[(ts_, mk)] = ph.op("tensor", sqmm,
                                                      inc=[("pe", 1)], waits=w4)

                        if md >= 1:
                            emit_sqmm(md - 1)
                        if md == DC - 1:
                            emit_sqmm(DC - 1)
                    w5 = [ev_smm[(ts_, DC - 1)]]
                    if ts_ > 0:
                        w5.append(ev_x0c[ts_ - 1])
                    ev_o1[ts_] = ph.op("scalar", lambda e: e.activation(
                        r_o0[:, :], psq[:, :], AF.Ln, bias=c_1eps[0:1, 0:1]),
                        inc=[("act", 1)], waits=w5)
                    ev_o2 = ph.op("scalar", lambda e: e.activation(
                        r_o0[:, :], r_o0[:, :], AF.Exp, scale=0.5),
                        inc=[("act", 1)], waits=[ev_o1[ts_]])
                    w6 = [ev_o2, ev_rel]
                    if ts_ > 0:
                        w6.append(ev_x0c[ts_ - 1])
                    ev_x0 = ph.op("vector", lambda e, ts_=ts_: e.tensor_add(
                        r_x0[:, :], xr2[0:1, ts_ * 512: ts_ * 512 + 512], r_o0[:, :]),
                        inc=[("dve", 1)], waits=w6)
                    for c in range(DC):
                        ev_add[(ts_, c)] = ph.op(
                            "vector", lambda e, ts_=ts_, c=c: e.tensor_add(
                                xr2_sl(c, ts_ * 512, 512), xr2_sl(c, ts_ * 512, 512),
                                sp_sb[:, c * 512:(c + 1) * 512]),
                            inc=[("dve", 1)],
                            waits=[ev_sp[(ts_, c)], ev_x0, ev_rel])
                    ev_x0c[ts_] = ph.op("scalar", lambda e, ts_=ts_: e.copy(
                        xr2[0:1, ts_ * 512: ts_ * 512 + 512], r_x0[:, :]),
                        inc=[("act", 1)], waits=[ev_add[(ts_, 0)]])
                ph.run()
        ctx_at3.close()

        if want("attn"):
            dump_sbuf("attn", xr2[:, :], [128, DC * T])
        if stop == "attn":
            _finish(p)
            ctx_xr2.close()
            return nc, dump_outs

        f2_res = ffn_sublayer("f2", xr2_sl)

        ph = Phase(p, "out")
        for tc in range(T // 512):
            ph.op("sync", lambda e, tc=tc: e.dma_start(
                bass.AP(yT_d, tc * 512, [[T, 128], [128 * T, DC], [1, 512]]),
                bass.AP(xr2, tc * 512, [[DC * T, 128], [T, DC], [1, 512]])),
                inc=[("dou", 16)],
                waits=[f2_res[(tc, 0)], f2_res[(tc, 1)]])
        ph.run()
        _finish(p)
        ctx_xr2.close()

    return nc, dump_outs


def _finish(p):
    ph = Phase(p, "final_wait")
    sems = ["dxp", "dgp", "dgd", "dg0", "dg1", "dh0", "dh1", "dhw0", "dhw1",
            "dgw", "dsp", "drl", "dou", "dsr0", "dsr1", "dw10", "dw11"]
    ph.op("gpsimd", lambda e: e.nop(), waits=[p.now(s) for s in sems])
    ph.run()


def host_inputs(inputs, cfg=CFG):
    E, FC, DC = cfg.E, cfg.FC, cfg.DC
    x = np.asarray(inputs["L0_attn_x"])
    base = {
        "WrT": np.ascontiguousarray(np.asarray(inputs["L0_ffn_router_w"]).T),
        "rb": np.ascontiguousarray(
            np.asarray(inputs["L0_ffn_router_b"]).reshape(1, E)),
        "WeT": np.ascontiguousarray(
            np.asarray(inputs["L0_ffn_expert_w"]).transpose(0, 2, 1)),
        "Be": np.ascontiguousarray(np.asarray(inputs["L0_ffn_expert_b"])),
        "W1T": np.ascontiguousarray(np.asarray(inputs["L0_attn_W1"]).T),
        "b1T": np.ascontiguousarray(
            np.asarray(inputs["L0_attn_b1"]).reshape(FC, 128).T),
        "W2T": np.ascontiguousarray(np.asarray(inputs["L0_attn_W2"]).T),
        "b2T": np.ascontiguousarray(
            np.asarray(inputs["L0_attn_b2"]).reshape(DC, 128).T),
        "WqT": np.ascontiguousarray(np.asarray(inputs["L0_attn_Wq"]).T),
        "WkT": np.ascontiguousarray(np.asarray(inputs["L0_attn_Wk"]).T),
        "WvT": np.ascontiguousarray(np.asarray(inputs["L0_attn_Wv"]).T),
        "WoT": np.ascontiguousarray(np.asarray(inputs["L0_attn_Wo"]).T),
    }
    maps = []
    for c in range(x.shape[0]):
        m = dict(base)
        m["xT"] = np.ascontiguousarray(x[c].T)
        maps.append(m)
    return maps


def kernel(**inputs):
    nc, _ = build(CFG)
    in_maps = host_inputs(inputs, CFG)
    res = run_bass_kernel_spmd(nc, in_maps, core_ids=list(range(8)))
    out = np.stack([np.asarray(r["yT"]).T for r in res.results], axis=0)
    return np.ascontiguousarray(out.astype(np.float32))


# version 20
# speedup vs baseline: 1.1019x; 1.0285x over previous
import numpy as np
from contextlib import ExitStack

import concourse.bass as bass
import concourse.mybir as mybir
from concourse.bass_utils import run_bass_kernel_spmd

F32 = mybir.dt.float32
BF16 = mybir.dt.bfloat16
AF = mybir.ActivationFunctionType
ALU = mybir.AluOpType
AX = mybir.AxisListType

EPS = 1e-6
LN_EPS = 1e-5


class Cfg:
    def __init__(self, T=2048, D=1024, DFF=2048, H=16, E=8):
        assert D % 512 == 0 and DFF % 128 == 0 and T % 1024 == 0
        assert D // H == 64
        self.T, self.D, self.DFF, self.H, self.E = T, D, DFF, H, E
        self.DC = D // 128
        self.FC = DFF // 128
        self.TC = T // 128


CFG = Cfg()
NDUM = (2, 1)
ENGS = ("tensor", "vector", "scalar", "gpsimd", "sync")
_blk_ctr = [0]


class Prog:
    def __init__(self, nc, ctx):
        self.nc = nc
        self.ctx = ctx
        self.sems = {}
        self.cnt = {}

    def sem(self, name):
        if name not in self.sems:
            self.sems[name] = self.ctx.enter_context(self.nc.semaphore(name))
            self.cnt[name] = 0
        return self.sems[name]

    def now(self, name):
        self.sem(name)
        return (name, self.cnt[name])


class Phase:

    def __init__(self, p, name=""):
        self.p = p
        _blk_ctr[0] += 1
        self.name = "%s_%d" % (name, _blk_ctr[0])
        self.ops = {e: [] for e in ENGS}

    def op(self, eng, fn, inc=(), waits=()):
        p = self.p
        evs = []
        for sname, by in inc:
            p.sem(sname)
            p.cnt[sname] += by
            evs.append((sname, p.cnt[sname]))
        wd = {}
        for wv in waits:
            if wv is None:
                continue
            sname, val = wv
            wd[sname] = max(wd.get(sname, 0), val)
        self.ops[eng].append((sorted(wd.items()), fn, [(s, b) for s, b in inc]))
        return evs[0] if evs else None

    def run(self):
        p = self.p

        def mk_body(oplist):
            def body(eng):
                for waits, fn, incs in oplist:
                    for sname, val in waits:
                        if val > 0:
                            eng.wait_ge(p.sems[sname], val)
                    inst = fn(eng)
                    for sname, by in incs:
                        inst.then_inc(p.sems[sname], by)
            return body

        with p.nc.Block(self.name) as blk:
            for e in ENGS:
                if self.ops[e]:
                    getattr(blk, e)(mk_body(self.ops[e]))


def build(cfg=CFG, dumps=(), stop=None):
    T, D, DFF, H, E = cfg.T, cfg.D, cfg.DFF, cfg.H, cfg.E
    DC, FC, TC = cfg.DC, cfg.FC, cfg.TC
    KC = T // 128
    NS = T // 512
    QH = T // 1024

    nc = bass.Bass("TRN2", target_bir_lowering=False)

    xT_d = nc.dram_tensor("xT", [D, T], F32, kind="ExternalInput")
    WrT_d = nc.dram_tensor("WrT", [D, E], F32, kind="ExternalInput")
    rb_d = nc.dram_tensor("rb", [1, E], F32, kind="ExternalInput")
    WeT_d = nc.dram_tensor("WeT", [E, D, D], F32, kind="ExternalInput")
    Be_d = nc.dram_tensor("Be", [E, D], F32, kind="ExternalInput")
    W1T_d = nc.dram_tensor("W1T", [D, DFF], F32, kind="ExternalInput")
    b1T_d = nc.dram_tensor("b1T", [128, FC], F32, kind="ExternalInput")
    W2T_d = nc.dram_tensor("W2T", [DFF, D], F32, kind="ExternalInput")
    b2T_d = nc.dram_tensor("b2T", [128, DC], F32, kind="ExternalInput")
    WqT_d = nc.dram_tensor("WqT", [D, D], F32, kind="ExternalInput")
    WkT_d = nc.dram_tensor("WkT", [D, D], F32, kind="ExternalInput")
    WvT_d = nc.dram_tensor("WvT", [D, D], F32, kind="ExternalInput")
    WoT_d = nc.dram_tensor("WoT", [D, D], F32, kind="ExternalInput")
    yT_d = nc.dram_tensor("yT", [D, T], F32, kind="ExternalOutput")

    WeC_d = nc.dram_tensor("WeC", [E, D, D], BF16, kind="Internal")
    xsp_d = nc.dram_tensor("xsp", [D, T], F32, kind="Internal")

    dump_outs = {}

    def want(tag):
        return tag in dumps

    with ExitStack() as ctx0:
        p = Prog(nc, ctx0)

        def sb(name, shape, dtype, c=None, side="left"):
            return (c or ctx0).enter_context(
                nc.sbuf_tensor(name, shape, dtype, side=side))

        def ps(name, shape, dtype, c=None, side="left"):
            return (c or ctx0).enter_context(
                nc.psum_tensor(name, shape, dtype, side=side))

        ident = sb("ident", [128, 128], BF16)
        ones_col = sb("ones_col", [128, 1], BF16)
        ones_col0 = sb("ones_col0", [128, 1], BF16)
        mean_col = sb("mean_col", [128, 1], BF16)
        ones_row = sb("ones_row", [1, 128], BF16)
        sel_sb = sb("sel_sb", [8, E * 128], BF16)
        oh16 = sb("oh16", [H, H * 64], BF16)
        b1T = sb("b1T_s", [128, FC], F32)
        b2T = sb("b2T_s", [128, DC], F32)
        Be_bf = sb("Be_bf", [8, D], BF16)
        wr32 = sb("wr32", [128, DC * E], F32)
        wr_hi = sb("wr_hi", [128, DC * E], BF16)
        wr_lo = sb("wr_lo", [128, DC * E], BF16)
        rb32 = sb("rb32", [1, E], F32)
        rb_hi = sb("rb_hi", [1, E], BF16)
        rb_lo = sb("rb_lo", [1, E], BF16)
        negB = sb("negB", [128, 1], F32)
        c_lneps = sb("c_lneps", [128, 1], F32)
        c_1eps = sb("c_1eps", [128, 1], F32)
        c_neg1 = sb("c_neg1", [128, 1], F32)
        c_pos1 = sb("c_pos1", [128, 1], F32)

        ctx_xr = ExitStack()
        xr = sb("xr", [128, DC * T], F32, ctx_xr)

        def mk_sl(t):
            def sl(c, t0=0, tn=None):
                tn = T if tn is None else tn
                return t[:, c * T + t0: c * T + t0 + tn]
            return sl
        xr_sl = mk_sl(xr)

        ph = Phase(p, "prologue")
        ev_xT = ph.op("sync", lambda e: e.dma_start(
            xr[:, :].rearrange("p (c t) -> p c t", c=DC),
            xT_d[:, :].rearrange("(c p) t -> p c t", p=128)), inc=[("dxp", 16)])
        ph.op("sync", lambda e: e.dma_start(
            wr32[:, :].rearrange("p (c e) -> p c e", c=DC),
            WrT_d[:, :].rearrange("(c p) e -> p c e", p=128)), inc=[("dxp", 16)])
        ph.op("sync", lambda e: e.dma_start(rb32[:, :], rb_d[:, :]), inc=[("dxp", 16)])
        ph.op("sync", lambda e: e.dma_start(b1T[:, :], b1T_d[:, :]), inc=[("dxp", 16)])
        ph.op("sync", lambda e: e.dma_start(b2T[:, :], b2T_d[:, :]), inc=[("dxp", 16)])
        ev_be = ph.op("gpsimd", lambda e: e.dma_start(Be_bf[:, :], Be_d[:, :]), inc=[("dgp", 16)])
        ev = ph.op("gpsimd", lambda e: e.memset(ident[:, :], 0.0), inc=[("pl", 1)])
        ph.op("gpsimd", lambda e: e.affine_select(
            out=ident[:, :], in_=ident[:, :], compare_op=ALU.not_equal,
            fill=1.0, base=0, pattern=[[-1, 128]], channel_multiplier=1),
            inc=[("pl", 1)], waits=[ev])
        ph.op("gpsimd", lambda e: e.memset(ones_col[:, :], 1.0), inc=[("pl", 1)])
        ev = ph.op("gpsimd", lambda e: e.memset(ones_col0[:, :], 1.0), inc=[("pl", 1)])
        ph.op("gpsimd", lambda e: e.memset(ones_col0[0:1, :], 0.0),
              inc=[("pl", 1)], waits=[ev])
        ph.op("gpsimd", lambda e: e.memset(mean_col[:, :], 1.0 / D), inc=[("pl", 1)])
        ph.op("gpsimd", lambda e: e.memset(ones_row[:, :], 1.0), inc=[("pl", 1)])
        ph.op("gpsimd", lambda e: e.memset(c_lneps[:, :], LN_EPS), inc=[("pl", 1)])
        ph.op("gpsimd", lambda e: e.memset(c_1eps[:, :], 1.0 + EPS), inc=[("pl", 1)])
        ph.op("gpsimd", lambda e: e.memset(c_neg1[:, :], -1.0), inc=[("pl", 1)])
        ph.op("gpsimd", lambda e: e.memset(c_pos1[:, :], 1.0), inc=[("pl", 1)])
        ev = ph.op("gpsimd", lambda e: e.memset(sel_sb[:, :], 0.0), inc=[("pl", 1)])
        ph.op("gpsimd", lambda e: e.affine_select(
            out=bass.AP(sel_sb, 0, [[E * 128, 8], [128, E], [1, 128]]),
            in_=bass.AP(sel_sb, 0, [[E * 128, 8], [128, E], [1, 128]]),
            compare_op=ALU.not_equal, fill=1.0, base=0,
            pattern=[[-1, E], [0, 128]], channel_multiplier=1),
            inc=[("pl", 1)], waits=[ev])
        ev = ph.op("gpsimd", lambda e: e.memset(oh16[:, :], 0.0), inc=[("pl", 1)])
        ph.op("gpsimd", lambda e: e.affine_select(
            out=bass.AP(oh16, 0, [[H * 64, H], [64, H], [1, 64]]),
            in_=bass.AP(oh16, 0, [[H * 64, H], [64, H], [1, 64]]),
            compare_op=ALU.not_equal, fill=1.0, base=0,
            pattern=[[-1, H], [0, 64]], channel_multiplier=1),
            inc=[("pl", 1)], waits=[ev])
        ph.run()

        ph = Phase(p, "loads_wait")
        wall = [p.now("dxp"), p.now("dgp"), p.now("pl")]
        for e_ in ENGS:
            ph.op(e_, lambda e: e.nop(), waits=wall)
        ph.run()

        ph = Phase(p, "router_split")
        ev_hi = ph.op("scalar", lambda e: e.copy(wr_hi[:, :], wr32[:, :]),
                      inc=[("act", 1)])
        ph.op("vector", lambda e: e.tensor_sub(wr_lo[:, :], wr32[:, :], wr_hi[:, :]),
              inc=[("dve", 1)], waits=[ev_hi])
        ev_hi = ph.op("scalar", lambda e: e.copy(rb_hi[:, :], rb32[:, :]),
                      inc=[("act", 1)])
        ph.op("vector", lambda e: e.tensor_sub(rb_lo[:, :], rb32[:, :], rb_hi[:, :]),
              inc=[("dve", 1)], waits=[ev_hi])
        ph.run()

        def dump_sbuf(tag, ap, shape):
            d = nc.dram_tensor("dump_" + tag, list(shape), F32, kind="ExternalOutput")
            dump_outs[tag] = d
            ph = Phase(p, "dump_" + tag)
            ph.op("gpsimd", lambda e: e.dma_start(d[:, :], ap), inc=[("dgd", 16)])
            ph.op("gpsimd", lambda e: e.nop(), waits=[p.now("dgd")])
            ph.run()

        if want("xr0"):
            dump_sbuf("xr0", xr[:, :], [128, DC * T])

        def layernorm(tag, xcur_sl, out_hi, out_lo, chunk_consumer=None,
                      ph_ext=None):
            with ExitStack() as c1:
                xbf = [sb("xbf%s%d" % (tag, i), [128, T], BF16, c1) for i in range(2)]
                sqb = [sb("sqb%s%d" % (tag, i), [128, T], BF16, c1) for i in range(2)]
                m_row = sb("m_row" + tag, [1, T], F32, c1)
                v2_row = sb("v2_row" + tag, [1, T], F32, c1)
                ra = sb("ra_row" + tag, [1, T], F32, c1)
                m_bf = sb("m_bf" + tag, [1, T], BF16, c1)
                r_bf = sb("r_bf" + tag, [1, T], BF16, c1)
                t32 = [sb("t32%s%d" % (tag, i), [128, T // 2], F32, c1)
                       for i in range(2)]
                with ExitStack() as c2:
                    pm = ps("pm" + tag, [1, T], F32, c2)
                    pv = ps("pv" + tag, [1, T], F32, c2)
                    ph = ph_ext if ph_ext is not None else Phase(p, "ln_" + tag)
                    ev_mm = {}
                    ev_cs = {}

                    def emit_mm(c):
                        b = c % 2

                        def mm(e, c=c, b=b):
                            for n in range(NS):
                                e.matmul(pm[:, n * 512:(n + 1) * 512], mean_col[:, :],
                                         xbf[b][:, n * 512:(n + 1) * 512],
                                         start=(c == 0), stop=(c == DC - 1))
                            for n in range(NS):
                                i = e.matmul(pv[:, n * 512:(n + 1) * 512],
                                             mean_col[:, :],
                                             sqb[b][:, n * 512:(n + 1) * 512],
                                             start=(c == 0), stop=(c == DC - 1))
                            return i
                        ev_mm[c] = ph.op("tensor", mm, inc=[("pe", 1)],
                                         waits=list(ev_cs[c]))

                    for c in range(DC):
                        b = c % 2
                        w = [ev_mm[c - 2]] if (c - 2) in ev_mm else []
                        ev_cast = ph.op("scalar", lambda e, c=c, b=b: e.copy(
                            xbf[b][:, :], xcur_sl(c)), inc=[("act", 1)], waits=w)
                        ev_sq = ph.op("scalar", lambda e, c=c, b=b: e.activation(
                            sqb[b][:, :], xcur_sl(c), AF.Square),
                            inc=[("act", 1)], waits=w)
                        ev_cs[c] = (ev_cast, ev_sq)
                        if c >= 1:
                            emit_mm(c - 1)
                    emit_mm(DC - 1)
                    ev = ev_mm[DC - 1]
                    ev_m = ph.op("scalar", lambda e: e.copy(m_row[:, :], pm[:, :]),
                                 inc=[("act", 1)], waits=[ev])
                    ev_v2 = ph.op("scalar", lambda e: e.copy(v2_row[:, :], pv[:, :]),
                                  inc=[("act", 1)], waits=[ev])
                    ev_msq = ph.op("vector", lambda e: e.tensor_mul(
                        ra[:, :], m_row[:, :], m_row[:, :]), inc=[("dve", 1)],
                        waits=[ev_m])
                    ev_var = ph.op("vector", lambda e: e.tensor_sub(
                        v2_row[:, :], v2_row[:, :], ra[:, :]), inc=[("dve", 1)],
                        waits=[ev_msq, ev_v2])
                    ev_ln = ph.op("scalar", lambda e: e.activation(
                        ra[:, :], v2_row[:, :], AF.Ln, bias=c_lneps[0:1, 0:1]),
                        inc=[("act", 1)], waits=[ev_var])
                    ev_rs = ph.op("scalar", lambda e: e.activation(
                        ra[:, :], ra[:, :], AF.Exp, scale=-0.5),
                        inc=[("act", 1)], waits=[ev_ln])
                    ev_mbf = ph.op("scalar", lambda e: e.copy(m_bf[:, :], m_row[:, :]),
                                   inc=[("act", 1)], waits=[ev_m])
                    ev_rbf = ph.op("scalar", lambda e: e.copy(r_bf[:, :], ra[:, :]),
                                   inc=[("act", 1)], waits=[ev_rs])
                with ExitStack() as c2:
                    TH = T // 2
                    pmb = ps("pmb" + tag, [128, TH], F32, c2)
                    prb = ps("prb" + tag, [128, TH], F32, c2)
                    ev_hi = {}
                    ev_lo = {}
                    ev_ap = {}
                    for half in range(2):
                        t0 = half * TH
                        wbc = [ev_mbf, ev_rbf]
                        if half > 0:
                            wbc += [ev_ap[(c, 0)] for c in range(DC)]

                        def bc(e, t0=t0):
                            for n in range(TH // 512):
                                e.matmul(pmb[:, n * 512:(n + 1) * 512], ones_row[:, :],
                                         m_bf[:, t0 + n * 512: t0 + (n + 1) * 512],
                                         start=True, stop=True)
                            for n in range(TH // 512):
                                i = e.matmul(prb[:, n * 512:(n + 1) * 512],
                                             ones_row[:, :],
                                             r_bf[:, t0 + n * 512: t0 + (n + 1) * 512],
                                             start=True, stop=True)
                            return i
                        ev_bc = ph.op("tensor", bc, inc=[("pe", 1)], waits=wbc)
                        for c in range(DC):
                            b = c % 2
                            w = [ev_bc]
                            for k in ((c - 2, half), (DC - 2 + c, half - 1)):
                                if k in ev_ap:
                                    w.append(ev_ap[k])
                                    w.append(ev_hi[k])
                            ev_sub = ph.op("vector", lambda e, c=c, b=b, t0=t0: e.tensor_sub(
                                t32[b][:, :], xcur_sl(c, t0, TH), pmb[:, :]),
                                inc=[("dve", 1)], waits=w)
                            if out_lo is None:
                                ev_p = ph.op(
                                    "vector", lambda e, c=c, b=b, t0=t0: e.tensor_mul(
                                        out_hi[:, c * T + t0: c * T + t0 + TH],
                                        t32[b][:, :], prb[:, :]),
                                    inc=[("dve", 1)], waits=[ev_sub])
                                ev_hi[(c, half)] = ev_p
                            else:
                                ev_p = ph.op(
                                    "vector", lambda e, c=c, b=b: e.tensor_mul(
                                        t32[b][:, :], t32[b][:, :], prb[:, :]),
                                    inc=[("dve", 1)], waits=[ev_sub])
                                ev_hi[(c, half)] = ph.op(
                                    "scalar", lambda e, c=c, b=b, t0=t0: e.copy(
                                        out_hi[:, c * T + t0: c * T + t0 + TH],
                                        t32[b][:, :]), inc=[("act", 1)],
                                    waits=[ev_p])
                            last = ev_p
                            if out_lo is not None:
                                ev_lo[(c, half)] = ph.op(
                                    "vector", lambda e, c=c, b=b, t0=t0: e.tensor_sub(
                                        out_lo[:, c * T + t0: c * T + t0 + TH],
                                        t32[b][:, :],
                                        out_hi[:, c * T + t0: c * T + t0 + TH]),
                                    inc=[("dve", 1)], waits=[ev_hi[(c, half)]])
                                last = ev_lo[(c, half)]
                            ev_ap[(c, half)] = last
                            if chunk_consumer is not None:
                                chunk_consumer(ph, c, half, ev_hi[(c, half)],
                                               ev_lo.get((c, half)))
                    if ph_ext is None:
                        ph.run()

        ffn_state = {"cached": False}

        def ffn_sublayer(tag, xcur_sl):
            with ExitStack() as cf:
                xnhi = sb("xnhi" + tag, [128, DC * T], BF16, cf)
                xnlo_ctx = ExitStack()
                xnlo = sb("xnlo" + tag, [128, DC * T], BF16, xnlo_ctx, side="right")
                wm_tok = sb("wm_tok" + tag, [128, TC * E], BF16, cf)
                wmT = sb("wmT" + tag, [8, T], BF16, cf)
                scr8 = sb("scr8" + tag, [128, E], F32, cf)
                msk8 = sb("msk8" + tag, [128, E], F32, cf)
                rmax = sb("rmax" + tag, [128, 1], F32, cf)
                nmax = sb("nmax" + tag, [128, 1], F32, cf)
                sume = sb("sume" + tag, [128, 1], F32, cf)
                wrec = sb("wrec" + tag, [128, 1], F32, cf)
                rt_ctx = ExitStack()
                rt_last = {}
                plg_box = {}

                def router_consumer(ph, c, half, ev_hi_e, ev_lo_e):
                    if "plg" not in plg_box:
                        plg_box["plg"] = ps("plg" + tag, [128, TC * E], F32,
                                            rt_ctx, side="right")
                    plg = plg_box["plg"]
                    tcs = list(range(half * (TC // 2), (half + 1) * (TC // 2)))

                    def rmm(e, c=c, half=half, tcs=tuple(tcs)):
                        i = None
                        for tc in tcs:
                            for j, (lh, rh) in enumerate(
                                    ((xnhi, wr_hi), (xnhi, wr_lo), (xnlo, wr_hi))):
                                i = e.matmul(
                                    plg[:, tc * E:(tc + 1) * E],
                                    lh[:, c * T + tc * 128: c * T + (tc + 1) * 128],
                                    rh[:, c * E:(c + 1) * E],
                                    start=(c == 0 and half == 0
                                           and tc == tcs[0] and j == 0),
                                    stop=False)
                            if c == DC - 1:
                                e.matmul(plg[:, tc * E:(tc + 1) * E], ones_row[:, :],
                                         rb_hi[:, :], start=False, stop=False)
                                i = e.matmul(plg[:, tc * E:(tc + 1) * E],
                                             ones_row[:, :], rb_lo[:, :],
                                             start=False,
                                             stop=(half == 1 and tc == tcs[-1]))
                        return i
                    rt_last["ev"] = ph.op("tensor", rmm, inc=[("pe", 1)],
                                          waits=[ev_hi_e, ev_lo_e])

                phm = Phase(p, "ffn_ln_" + tag)
                layernorm(tag, xcur_sl, xnhi, xnlo,
                          chunk_consumer=router_consumer, ph_ext=phm)

                plg = plg_box["plg"]
                with ExitStack() as c2x:
                    ph = phm
                    ev_lg = rt_last["ev"]
                    rmax_t = sb("rmax_t" + tag, [128, TC], F32, cf)
                    sume_t = sb("sume_t" + tag, [128, TC], F32, cf)
                    scr_s = sb("scr_s" + tag, [128, TC * E], F32, cf)
                    scr_m = sb("scr_m" + tag, [128, TC * E], F32, cf)
                    plg3 = bass.AP(plg, 0, [[TC * E, 128], [E, TC], [1, E]])

                    def bc3(t):
                        return bass.AP(t, 0, [[TC, 128], [1, TC], [0, E]])
                    ev1 = ph.op("vector", lambda e: e.tensor_reduce(
                        rmax_t[:, :], plg3, axis=AX.X, op=ALU.max),
                        inc=[("dve", 1)], waits=[ev_lg])
                    ev2 = ph.op("vector", lambda e: e.tensor_tensor(
                        bass.AP(scr_s, 0, [[TC * E, 128], [E, TC], [1, E]]),
                        plg3, bc3(rmax_t), ALU.subtract),
                        inc=[("dve", 1)], waits=[ev1])
                    ev3 = ph.op("vector", lambda e: e.tensor_tensor(
                        bass.AP(scr_m, 0, [[TC * E, 128], [E, TC], [1, E]]),
                        plg3, bc3(rmax_t), ALU.is_equal),
                        inc=[("dve", 1)], waits=[ev1])
                    ev4 = ph.op("scalar", lambda e: e.activation(
                        scr_s[:, :], scr_s[:, :], AF.Exp), inc=[("act", 1)],
                        waits=[ev2])
                    ev5 = ph.op("vector", lambda e: e.tensor_reduce(
                        sume_t[:, :],
                        bass.AP(scr_s, 0, [[TC * E, 128], [E, TC], [1, E]]),
                        axis=AX.X, op=ALU.add), inc=[("dve", 1)], waits=[ev4])
                    ev6 = ph.op("vector", lambda e: e.reciprocal(
                        sume_t[:, :], sume_t[:, :]), inc=[("dve", 1)], waits=[ev5])
                    ev_wm = ph.op("vector", lambda e: e.scalar_tensor_tensor(
                        bass.AP(wm_tok, 0, [[TC * E, 128], [E, TC], [1, E]]),
                        bass.AP(scr_m, 0, [[TC * E, 128], [E, TC], [1, E]]),
                        0.5, bc3(sume_t), ALU.mult, ALU.mult),
                        inc=[("dve", 1)], waits=[ev3, ev6])
                rt_ctx.close()
                with ExitStack() as c3:
                    pwt = ps("pwt" + tag, [8, T], BF16, c3)

                    def tr(e):
                        for tc in range(TC):
                            i = e.transpose(pwt[:, tc * 128:(tc + 1) * 128],
                                            wm_tok[:, tc * E:(tc + 1) * E],
                                            ident[:, :])
                        return i
                    ev = ph.op("tensor", tr, inc=[("pe", 1)], waits=[ev_wm])
                    ev_wmT = ph.op("scalar", lambda e: e.copy(wmT[:, :], pwt[:, :]),
                                   inc=[("act", 1)], waits=[ev])
                xnlo_ctx.close()

                maskb = sb("maskb" + tag, [128, E * T], BF16, cf)
                with ExitStack() as c2:
                    pbc = [ps("pbc%s%d" % (tag, i), [128, T], F32, c2)
                           for i in range(2)]
                    ev_ev = [None, None]
                    for e_i in range(E):
                        b = e_i % 2
                        w = [ev_wmT]
                        if ev_ev[b]:
                            w.append(ev_ev[b])

                        def bmm(e, e_i=e_i, b=b):
                            for n in range(NS):
                                i = e.matmul(pbc[b][:, n * 512:(n + 1) * 512],
                                             sel_sb[:, e_i * 128:(e_i + 1) * 128],
                                             wmT[:, n * 512:(n + 1) * 512],
                                             start=True, stop=True)
                            return i
                        ev_mm = ph.op("tensor", bmm, inc=[("pe", 1)], waits=w)
                        ev_ev[b] = ph.op("scalar", lambda e, e_i=e_i, b=b: e.copy(
                            maskb[:, e_i * T:(e_i + 1) * T], pbc[b][:, :]),
                            inc=[("act", 1)], waits=[ev_mm])
                    ev_maskb = ev_ev[(E - 1) % 2]
                    ev_maskb2 = ev_ev[E % 2]
                    phm.run()

                wbuf = [sb("wbuf%s%d" % (tag, i), [128, DC * D], BF16, cf)
                        for i in range(2)]
                xnm = [sb("xnm%s%d" % (tag, i), [128, DC * 512], BF16, cf)
                       for i in range(2)]
                with ExitStack() as c2:
                    py = ps("py" + tag, [128, DC * 512], F32, c2)
                    ph = Phase(p, "experts_" + tag)
                    cached = ffn_state["cached"]
                    load_ev = {}
                    wb_ev = {}
                    mm_ev = {}
                    res_ev = {}
                    TQn = T // 512
                    nli = TQn * E

                    def emit_load(li):
                        tc_, e_ = divmod(li, E)
                        w = []
                        if li >= 2:
                            w.append(mm_ev[li - 2])
                            if (li - 2) in wb_ev:
                                w.append(wb_ev[li - 2])
                        if not cached and tc_ == 0:
                            load_ev[li] = ph.op(
                                "gpsimd", lambda e, e_=e_, li=li: e.dma_start(
                                    wbuf[li % 2][:, :].rearrange(
                                        "p (c n) -> p c n", c=DC),
                                    WeT_d[e_].rearrange("(c p) n -> p c n", p=128)),
                                inc=[("dg%d" % (li % 2), 16)], waits=w)
                            wb_ev[li] = ph.op("sync", lambda e, e_=e_, li=li: e.dma_start(
                                WeC_d[e_].rearrange("(c p) n -> p c n", p=128),
                                wbuf[li % 2][:, :].rearrange(
                                    "p (c n) -> p c n", c=DC)),
                                inc=[("dhw%d" % (li % 2), 16)], waits=[load_ev[li]])
                        else:
                            load_ev[li] = ph.op(
                                "sync", lambda e, e_=e_, li=li: e.dma_start(
                                    wbuf[li % 2][:, :].rearrange(
                                        "p (c n) -> p c n", c=DC),
                                    WeC_d[e_].rearrange("(c p) n -> p c n", p=128)),
                                inc=[("dh%d" % (li % 2), 16)], waits=w)

                    emit_load(0)
                    emit_load(1)
                    for tc in range(TQn):
                        for e_i in range(E):
                            li = tc * E + e_i
                            w = [ev_maskb, ev_maskb2]
                            if li >= 2:
                                w.append(mm_ev[li - 2])

                            def vmask(e, tc=tc, e_i=e_i, li=li):
                                for c in range(DC):
                                    i = e.tensor_mul(
                                        xnm[li % 2][:, c * 512:(c + 1) * 512],
                                        xnhi[:, c * T + tc * 512:
                                             c * T + tc * 512 + 512],
                                        maskb[:, e_i * T + tc * 512:
                                              e_i * T + tc * 512 + 512])
                                return i
                            ev_vm = ph.op("vector", vmask, inc=[("dve", 1)], waits=w)
                            w = [ev_vm, load_ev[li]]
                            if tc > 0 and e_i == 0:
                                w.append(res_ev[(tc - 1, 0)])

                            def emma(e, tc=tc, e_i=e_i, li=li):
                                for m in range(DC // 2):
                                    for c in range(DC):
                                        i = e.matmul(
                                            py[:, m * 512:(m + 1) * 512],
                                            wbuf[li % 2][:, c * D + m * 128:
                                                         c * D + (m + 1) * 128],
                                            xnm[li % 2][:, c * 512:(c + 1) * 512],
                                            start=(e_i == 0 and c == 0), stop=False)
                                return i

                            def emmb(e, tc=tc, e_i=e_i, li=li):
                                for m in range(DC // 2, DC):
                                    for c in range(DC):
                                        i = e.matmul(
                                            py[:, m * 512:(m + 1) * 512],
                                            wbuf[li % 2][:, c * D + m * 128:
                                                         c * D + (m + 1) * 128],
                                            xnm[li % 2][:, c * 512:(c + 1) * 512],
                                            start=(e_i == 0 and c == 0), stop=False)
                                return i
                            ph.op("tensor", emma, inc=[("pe", 1)], waits=w)
                            wb_ = [] if not (tc > 0 and e_i == 0) \
                                else [res_ev[(tc - 1, 1)]]
                            mm_ev[li] = ph.op("tensor", emmb, inc=[("pe", 1)],
                                              waits=wb_)
                            if li + 2 < nli:
                                emit_load(li + 2)

                        def bmm2a(e, tc=tc):
                            for m in range(DC // 2):
                                i = e.matmul(py[:, m * 512:(m + 1) * 512],
                                             Be_bf[:, m * 128:(m + 1) * 128],
                                             wmT[:, tc * 512: tc * 512 + 512],
                                             start=False, stop=True)
                            return i

                        def bmm2b(e, tc=tc):
                            for m in range(DC // 2, DC):
                                i = e.matmul(py[:, m * 512:(m + 1) * 512],
                                             Be_bf[:, m * 128:(m + 1) * 128],
                                             wmT[:, tc * 512: tc * 512 + 512],
                                             start=False, stop=True)
                            return i
                        ev_ba = ph.op("tensor", bmm2a, inc=[("pe", 1)],
                                      waits=[ev_wmT])
                        ev_bb = ph.op("tensor", bmm2b, inc=[("pe", 1)], waits=[])

                        def radda(e, tc=tc):
                            for m in range(DC // 2):
                                i = e.tensor_add(
                                    xcur_sl(m, tc * 512, 512),
                                    xcur_sl(m, tc * 512, 512),
                                    py[:, m * 512:(m + 1) * 512])
                            return i

                        def raddb(e, tc=tc):
                            for m in range(DC // 2, DC):
                                i = e.tensor_add(
                                    xcur_sl(m, tc * 512, 512),
                                    xcur_sl(m, tc * 512, 512),
                                    py[:, m * 512:(m + 1) * 512])
                            return i
                        res_ev[(tc, 0)] = ph.op("vector", radda, inc=[("dve", 1)],
                                                waits=[ev_ba])
                        res_ev[(tc, 1)] = ph.op("vector", raddb, inc=[("dve", 1)],
                                                waits=[ev_bb])
                    ph.run()
                return res_ev

        f1_res = ffn_sublayer("f1", xr_sl)
        ffn_state["cached"] = True
        if want("ffn1"):
            dump_sbuf("ffn1", xr[:, :], [128, DC * T])
        if stop == "ffn1":
            _finish(p)
            ctx_xr.close()
            return nc, dump_outs

        ctx_at1 = ExitStack()
        xnT = sb("xnT", [128, DC * T], BF16, ctx_at1, side="right")
        g_sb = sb("g_sb", [128, DC * T], BF16, ctx_at1, side="right")

        ph = Phase(p, "spill")
        ev_spill = ph.op("sync", lambda e: e.dma_start(
            xsp_d[:, :].rearrange("(c p) t -> p c t", p=128),
            xr[:, :].rearrange("p (c t) -> p c t", c=DC)), inc=[("dsp", 16)],
            waits=[f1_res[(T // 512 - 1, 0)], f1_res[(T // 512 - 1, 1)]])
        ph.run()

        vch_ctx = ExitStack()
        clipb = [sb("clipb%d" % i, [128, T // 2], BF16, vch_ctx, side="right")
                 for i in range(2)]
        sqvb = [sb("sqvb%d" % i, [128, T // 2], BF16, vch_ctx, side="right")
                for i in range(2)]
        pss_ctx = ExitStack()
        pss_box = {}
        vch_ev = {"cl": {}, "sq": {}, "mm": {}}

        def vchain_consumer(ph, c, half, ev_hi_e, ev_lo_e):
            if "pss" not in pss_box:
                pss_box["pss"] = ps("pss", [1, T], F32, pss_ctx, side="right")
            pss = pss_box["pss"]
            TH = T // 2
            t0 = half * TH
            gi = half * DC + c
            b = gi % 2
            w = [ev_hi_e]
            if gi >= 2:
                w.append(vch_ev["sq"][gi - 2])
            ev_cl = ph.op("vector", lambda e, c=c, t0=t0, b=b: e.tensor_scalar(
                clipb[b][:, :], xnT[:, c * T + t0: c * T + t0 + TH],
                8.0, -8.0, ALU.min, ALU.max), inc=[("dve", 1)], waits=w)
            vch_ev["cl"][gi] = ev_cl
            w2 = [ev_cl]
            if gi >= 2:
                w2.append(vch_ev["mm"][gi - 2])
            ev_sq = ph.op("scalar", lambda e, b=b: e.activation(
                sqvb[b][:, :], clipb[b][:, :], AF.Square),
                inc=[("act", 1)], waits=w2)
            vch_ev["sq"][gi] = ev_sq

            def smm(e, c=c, t0=t0, b=b):
                lhs = ones_col0 if c == 0 else ones_col
                for n in range(T // 2 // 512):
                    i = e.matmul(pss[:, t0 + n * 512: t0 + (n + 1) * 512],
                                 lhs[:, :], sqvb[b][:, n * 512:(n + 1) * 512],
                                 start=(c == 0), stop=(c == DC - 1))
                return i
            vch_ev["mm"][gi] = ph.op("tensor", smm, inc=[("pe", 1)], waits=[ev_sq])

        layernorm("a", xr_sl, xnT, None, chunk_consumer=vchain_consumer)
        ev_ssdone = [vch_ev["mm"][DC - 1], vch_ev["mm"][2 * DC - 1]]

        ph = Phase(p, "spill_wait")
        for e_ in ENGS:
            ph.op(e_, lambda e: e.nop(), waits=[ev_spill])
        ph.run()
        ctx_xr.close()

        if want("lna"):
            dump_sbuf("lna", xnT[:, :], [128, DC * T])
        if stop == "lna":
            _finish(p)
            ctx_at1.close()
            return nc, dump_outs

        vch_ctx.close()
        cv2 = ExitStack()
        R = [sb("vcr%d" % i, [1, T], F32, cv2) for i in range(5)]
        s_bf = sb("s_bf", [1, T], BF16, cv2)
        dv0s_bf = sb("dv0s_bf", [1, T], BF16, cv2)
        vce = {}

        def emit_vchain(ph):
            pss = pss_box["pss"]
            ev_ss = ph.op("scalar", lambda e: e.copy(R[0][:, :], pss[:, :]),
                          inc=[("act", 1)], waits=ev_ssdone)
            yield
            ev1 = ph.op("scalar", lambda e: e.activation(
                R[1][:, :], R[0][:, :], AF.Ln, bias=c_1eps[0:1, 0:1]),
                inc=[("act", 1)], waits=[ev_ss])
            yield
            ev2 = ph.op("scalar", lambda e: e.activation(
                R[1][:, :], R[1][:, :], AF.Exp, scale=0.5),
                inc=[("act", 1)], waits=[ev1])
            yield
            ev3 = ph.op("vector", lambda e: e.tensor_scalar_max(
                R[2][:, :], R[1][:, :], 1.0 + EPS), inc=[("dve", 1)], waits=[ev2])
            yield
            ev4 = ph.op("scalar", lambda e: e.activation(
                R[3][:, :], R[2][:, :], AF.Square), inc=[("act", 1)], waits=[ev3])
            yield
            ev5 = ph.op("scalar", lambda e: e.activation(
                R[3][:, :], R[3][:, :], AF.Ln, bias=c_neg1[0:1, 0:1]),
                inc=[("act", 1)], waits=[ev4])
            yield
            ev6 = ph.op("scalar", lambda e: e.activation(
                R[3][:, :], R[3][:, :], AF.Exp, scale=0.5), inc=[("act", 1)],
                waits=[ev5])
            yield
            ev7 = ph.op("vector", lambda e: e.tensor_add(
                R[3][:, :], R[3][:, :], R[2][:, :]), inc=[("dve", 1)],
                waits=[ev6])
            yield
            ev8 = ph.op("scalar", lambda e: e.activation(
                R[3][:, :], R[3][:, :], AF.Ln), inc=[("act", 1)], waits=[ev7])
            yield
            ev9 = ph.op("vector", lambda e: e.tensor_scalar_max(
                R[4][:, :], R[3][:, :], 0.001), inc=[("dve", 1)], waits=[ev8])
            yield
            ev10 = ph.op("vector", lambda e: e.tensor_sub(
                R[3][:, :], R[1][:, :], R[2][:, :]), inc=[("dve", 1)],
                waits=[ev3, ev9])
            yield
            ev11 = ph.op("scalar", lambda e: e.activation(
                R[1][:, :], R[3][:, :], AF.Square), inc=[("act", 1)],
                waits=[ev10, ev9])
            yield
            ev12 = ph.op("vector", lambda e: e.tensor_sub(
                R[1][:, :], R[0][:, :], R[1][:, :]), inc=[("dve", 1)],
                waits=[ev11])
            yield
            ev13 = ph.op("vector", lambda e: e.tensor_scalar_max(
                R[1][:, :], R[1][:, :], EPS), inc=[("dve", 1)], waits=[ev12])
            yield
            ev14 = ph.op("scalar", lambda e: e.activation(
                R[1][:, :], R[1][:, :], AF.Ln), inc=[("act", 1)], waits=[ev13])
            yield
            ev15 = ph.op("scalar", lambda e: e.activation(
                R[1][:, :], R[1][:, :], AF.Exp, scale=0.5), inc=[("act", 1)],
                waits=[ev14])
            yield
            ev16 = ph.op("vector", lambda e: e.tensor_scalar_add(
                R[1][:, :], R[1][:, :], EPS), inc=[("dve", 1)], waits=[ev15])
            yield
            ev17 = ph.op("vector", lambda e: e.reciprocal(
                R[1][:, :], R[1][:, :]), inc=[("dve", 1)], waits=[ev16])
            yield
            ev18 = ph.op("vector", lambda e: e.tensor_mul(
                R[2][:, :], R[4][:, :], R[1][:, :]), inc=[("dve", 1)],
                waits=[ev17, ev10])
            yield
            vce["sbf"] = ph.op("scalar", lambda e: e.copy(
                s_bf[:, :], R[2][:, :]), inc=[("act", 1)], waits=[ev18])
            yield
            ev19 = ph.op("vector", lambda e: e.tensor_mul(
                R[3][:, :], R[3][:, :], R[2][:, :]), inc=[("dve", 1)],
                waits=[ev18, ev11])
            yield
            vce["d0"] = ph.op("scalar", lambda e: e.copy(
                dv0s_bf[:, :], R[3][:, :]), inc=[("act", 1)], waits=[ev19])
            yield

        with ExitStack() as cm:
            W1s = sb("W1s", [128, DC * DFF], BF16, cm)
            W2s = sb("W2s", [128, FC * D], BF16, cm)
            h1 = sb("h1", [128, FC * 512], BF16, cm)
            th = [sb("th%d" % i, [128, 512], F32, cm) for i in range(2)]
            ph = Phase(p, "metric")
            vgen = emit_vchain(ph)
            HC2 = DC // 2
            ev_w1h = []
            for hw in range(2):
                ev_w1h.append(ph.op("gpsimd", lambda e, hw=hw: e.dma_start(
                    W1s[:, hw * HC2 * DFF:(hw + 1) * HC2 * DFF].rearrange(
                        "p (c n) -> p c n", c=HC2),
                    W1T_d[hw * (D // 2):(hw + 1) * (D // 2), :].rearrange(
                        "(c p) n -> p c n", p=128)),
                    inc=[("dw1%d" % hw, 16)]))
            ev_w1 = ev_w1h[1]
            ev_w2 = ph.op("gpsimd", lambda e: e.dma_start(
                W2s[:, :].rearrange("p (c n) -> p c n", c=FC),
                W2T_d[:, :].rearrange("(c p) n -> p c n", p=128)), inc=[("dgw", 16)])
            ev_adj = ph.op("vector", lambda e: e.tensor_scalar_add(
                xnT[0:1, 0:T], xnT[0:1, 0:T], -1.0), inc=[("dve", 1)])
            with ExitStack() as c2:
                ph1 = [ps("ph1%d" % i, [128, 512], F32, c2) for i in range(2)]
                pg = [ps("pg%d" % i, [128, 512], F32, c2) for i in range(2)]
                ev_h1ev = {}
                ev_thev = {}
                ev_gw = {}
                for q in range(T // 512):
                    for m in range(FC):
                        b = m % 2
                        w = [ev_adj]
                        if (q, m - 2) in ev_h1ev:
                            w.append(ev_h1ev[(q, m - 2)])
                        elif q > 0:
                            w.append(ev_h1ev[(q - 1, FC - 2 + m)])
                        if q > 0 and m < 2:
                            w.append(ev_gw[(q - 1, DC - 1)])

                        def h1mma(e, q=q, m=m, b=b):
                            for c in range(HC2):
                                i = e.matmul(
                                    ph1[b][:, :],
                                    W1s[:, c * DFF + m * 128: c * DFF + (m + 1) * 128],
                                    xnT[:, c * T + q * 512: c * T + q * 512 + 512],
                                    start=(c == 0), stop=False)
                            return i

                        def h1mmb(e, q=q, m=m, b=b):
                            for c in range(HC2, DC):
                                i = e.matmul(
                                    ph1[b][:, :],
                                    W1s[:, c * DFF + m * 128: c * DFF + (m + 1) * 128],
                                    xnT[:, c * T + q * 512: c * T + q * 512 + 512],
                                    start=False, stop=(c == DC - 1))
                            return i
                        wa = [x for x in w if x is not ev_w2] + [ev_w1h[0]]
                        ph.op("tensor", h1mma, inc=[("pe", 1)], waits=wa)
                        ev_mm = ph.op("tensor", h1mmb, inc=[("pe", 1)],
                                      waits=[ev_w1h[1]])
                        ev_h1ev[(q, m)] = ph.op(
                            "scalar", lambda e, q=q, m=m, b=b: e.activation(
                                h1[:, m * 512:(m + 1) * 512], ph1[b][:, :], AF.Tanh,
                                bias=b1T[:, m:m + 1]), inc=[("act", 1)],
                            waits=[ev_mm])
                        next(vgen, None)
                    for md in range(DC):
                        b = md % 2
                        w = [ev_w2, ev_h1ev[(q, FC - 1)]]
                        if (q, md - 2) in ev_thev:
                            w.append(ev_thev[(q, md - 2)])
                        elif q > 0:
                            w.append(ev_thev[(q - 1, DC - 2 + md)])

                        def gmm(e, q=q, md=md, b=b):
                            for c in range(FC):
                                i = e.matmul(
                                    pg[b][:, :],
                                    W2s[:, c * D + md * 128: c * D + (md + 1) * 128],
                                    h1[:, c * 512:(c + 1) * 512],
                                    start=(c == 0), stop=(c == FC - 1))
                            return i
                        ev_gw[(q, md)] = ph.op("tensor", gmm, inc=[("pe", 1)], waits=w)
                        ev_th = ph.op("scalar", lambda e, q=q, md=md, b=b: e.activation(
                            th[b][:, :], pg[b][:, :], AF.Tanh, bias=b2T[:, md:md + 1]),
                            inc=[("act", 1)], waits=[ev_gw[(q, md)]])
                        ev_thev[(q, md)] = ev_th
                        ev_ab = ph.op("scalar", lambda e, q=q, md=md, b=b: e.activation(
                            g_sb[:, md * T + q * 512: md * T + q * 512 + 512],
                            th[b][:, :], AF.Abs, scale=0.1, bias=c_pos1[:, 0:1]),
                            inc=[("act", 1)], waits=[ev_th])
                        if md == 0:
                            ev_thev[(q, md)] = ph.op(
                                "scalar", lambda e, q=q, b=b: e.activation(
                                    g_sb[0:1, q * 512: q * 512 + 512],
                                    th[b][0:1, :], AF.Abs, scale=0.1, bias=c_neg1[0:1, 0:1]),
                                inc=[("act", 1)], waits=[ev_ab])
                        next(vgen, None)
                for _ in vgen:
                    pass
                ph.run()
        if want("g"):
            dump_sbuf("g", g_sb[:, :], [128, DC * T])
        if stop == "g":
            _finish(p)
            ctx_at1.close()
            return nc, dump_outs

        ctx_at2 = ExitStack()

        with ExitStack() as cv:
            clipv = [sb("clipv%d" % i, [128, T], BF16, cv) for i in range(2)]
            ev_sbf = vce["sbf"]
            ev_d0 = vce["d0"]
            pss_ctx.close()
            vch_ctx.close()
            wpb = [sb("wpb%d" % i, [128, DC * D], BF16, ctx_at1, side="right")
                   for i in range(2)]
            with ExitStack() as c2:
                psb = ps("psb", [128, T], F32, c2)
                ph = Phase(p, "vbuild")
                ev_lq = ph.op("gpsimd", lambda e: e.dma_start(
                    wpb[0][:, :].rearrange("p (c n) -> p c n", c=DC),
                    WqT_d[:, :].rearrange("(c p) n -> p c n", p=128)),
                    inc=[("dgw", 16)])
                ev_lk = ph.op("gpsimd", lambda e: e.dma_start(
                    wpb[1][:, :].rearrange("p (c n) -> p c n", c=DC),
                    WkT_d[:, :].rearrange("(c p) n -> p c n", p=128)),
                    inc=[("dgw", 16)])

                def sbc(e):
                    for n in range(NS):
                        i = e.matmul(psb[:, n * 512:(n + 1) * 512], ones_row[:, :],
                                     s_bf[:, n * 512:(n + 1) * 512],
                                     start=True, stop=True)
                    return i
                ev_bc = ph.op("tensor", sbc, inc=[("pe", 1)], waits=[ev_sbf])
                ev_v0 = None
                ev_cl2 = {}
                ev_vv2 = {}
                for c in range(DC):
                    b = c % 2
                    w = []
                    if c >= 2:
                        w.append(ev_vv2[c - 2])
                    ev_cl2[c] = ph.op("vector", lambda e, c=c, b=b: e.tensor_scalar(
                        clipv[b][:, :], xnT[:, c * T:(c + 1) * T],
                        8.0, -8.0, ALU.min, ALU.max), inc=[("dve", 1)], waits=w)
                    ev_vv2[c] = ph.op("vector", lambda e, c=c, b=b: e.tensor_mul(
                        xnT[:, c * T:(c + 1) * T], clipv[b][:, :],
                        psb[:, :]), inc=[("dve", 1)], waits=[ev_cl2[c], ev_bc])
                    if c == 0:
                        ev_v0 = ev_vv2[c]
                ev_row0 = ph.op("scalar", lambda e: e.copy(
                    xnT[0:1, 0:T], dv0s_bf[:, :]),
                    inc=[("act", 1)], waits=[ev_v0, ev_d0])
                ph.run()
        cv2.close()
        vT = xnT
        if want("vT"):
            dump_sbuf("vT", vT[:, :], [128, DC * T])
        if stop == "vT":
            _finish(p)
            ctx_at1.close()
            return nc, dump_outs

        QG = sb("QG", [128, DC * T], BF16, ctx_at2)
        with ExitStack() as c2:
            pq = [ps("pq%d" % i, [128, T], F32, c2) for i in range(2)]
            ph = Phase(p, "qproj")
            ev_qg = {}
            for m in range(DC):
                b = m % 2
                w = [ev_lk]
                if m >= 2:
                    w.append(ev_qg[m - 2])
                else:
                    w += [ev_row0, ev_vv2[DC - 1], ev_vv2[DC - 2]]

                def qmm(e, m=m, b=b):
                    for c in range(DC):
                        for n in range(NS):
                            i = e.matmul(
                                pq[b][:, n * 512:(n + 1) * 512],
                                wpb[0][:, c * D + m * 128: c * D + (m + 1) * 128],
                                vT[:, c * T + n * 512: c * T + (n + 1) * 512],
                                start=(c == 0), stop=(c == DC - 1))
                    return i
                ev_mm = ph.op("tensor", qmm, inc=[("pe", 1)], waits=w)
                ev_qg[m] = ph.op("vector", lambda e, m=m, b=b: e.tensor_mul(
                    QG[:, m * T:(m + 1) * T], pq[b][:, :],
                    g_sb[:, m * T:(m + 1) * T]), inc=[("dve", 1)], waits=[ev_mm])
            ev_neg = ph.op("vector", lambda e: e.tensor_scalar_mul(
                QG[0:1, 0:T], QG[0:1, 0:T], -1.0), inc=[("dve", 1)],
                waits=[ev_qg[0]])
            ph.run()

            kT = sb("kT", [128, DC * T], BF16, ctx_at2)
            vv = sb("vv", [128, TC * H * 65], BF16, ctx_at2)
            ph = Phase(p, "kproj")
            ev_kev = {}
            for m in range(DC):
                b = m % 2
                w = [ev_lk]
                if m >= 2:
                    w.append(ev_kev[m - 2])
                else:
                    w.append(ev_qg[DC - 2 + m])
                    w.append(ev_neg)

                def kmm(e, m=m, b=b):
                    for c in range(DC):
                        for n in range(NS):
                            i = e.matmul(
                                pq[b][:, n * 512:(n + 1) * 512],
                                wpb[1][:, c * D + m * 128: c * D + (m + 1) * 128],
                                vT[:, c * T + n * 512: c * T + (n + 1) * 512],
                                start=(c == 0), stop=(c == DC - 1))
                    return i
                ev_mm = ph.op("tensor", kmm, inc=[("pe", 1)], waits=w)
                ev_kev[m] = ph.op("scalar", lambda e, m=m, b=b: e.copy(
                    kT[:, m * T:(m + 1) * T], pq[b][:, :]), inc=[("act", 1)],
                    waits=[ev_mm])
            ev_lv = ph.op("gpsimd", lambda e: e.dma_start(
                wpb[0][:, :].rearrange("p (c n) -> p c n", c=DC),
                WvT_d[:, :].rearrange("(c p) n -> p c n", p=128)),
                inc=[("dgw", 16)], waits=[ev_qg[DC - 1]])
            ph.run()

        cb = ExitStack()
        sqs2 = [sb("sqb2%d" % i, [128, T // 2], BF16, cb) for i in range(2)]
        r_m = sb("r_m", [1, 1], F32, cb)
        nb_bf = sb("nb_bf", [1, 1], BF16, cb)
        with ExitStack() as c2:
            pv = [ps("pv%d" % i, [128, D], F32, c2) for i in range(2)]
            pqn = ps("pqn", [1, T], F32, c2)
            ph = Phase(p, "vvproj")
            ev_one = ph.op("gpsimd", lambda e: e.memset(
                bass.AP(vv, 64, [[TC * H * 65, 128], [H * 65, TC], [65, H], [1, 1]]),
                1.0), inc=[("pl", 1)])
            ev_vev = {}
            ev_sq = {}
            ev_nmm = {}
            norm_srcs = []
            for s_, c, ev_ in ([(QG, c, ev_qg[c]) for c in range(DC)] +
                               [(kT, c, ev_kev[c]) for c in range(DC)]):
                for hh in (0, 1):
                    norm_srcs.append((s_, c, hh, ev_))
            for tt in range(TC):
                b = tt % 2
                w = [ev_lv]
                if tt >= 2:
                    w.append(ev_vev[tt - 2])

                def vmm(e, tt=tt, b=b):
                    for c in range(DC):
                        for n in range(D // 512):
                            i = e.matmul(
                                pv[b][:, n * 512:(n + 1) * 512],
                                vT[:, c * T + tt * 128: c * T + (tt + 1) * 128],
                                wpb[0][:, c * D + n * 512: c * D + (n + 1) * 512],
                                start=(c == 0), stop=(c == DC - 1))
                    return i
                ev_mm = ph.op("tensor", vmm, inc=[("pe", 1)], waits=w)
                ev_vev[tt] = ph.op("scalar", lambda e, tt=tt, b=b: e.copy(
                    bass.AP(vv, tt * H * 65, [[TC * H * 65, 128], [65, H], [1, 64]]),
                    bass.AP(pv[b], 0, [[D, 128], [64, H], [1, 64]])),
                    inc=[("act", 1)], waits=[ev_mm, ev_one])
                NK = len(norm_srcs)
                TH2 = T // 2

                def emit_nmm(k):
                    s_, c, hh, ev_ = norm_srcs[k]

                    def nmm(e, k=k, c=c, hh=hh):
                        for n in range(TH2 // 512):
                            i = e.matmul(
                                pqn[:, hh * TH2 + n * 512: hh * TH2 + (n + 1) * 512],
                                ones_col[:, :],
                                sqs2[k % 2][:, n * 512:(n + 1) * 512],
                                start=(c == 0 and s_ is QG),
                                stop=(c == DC - 1 and s_ is kT))
                        return i
                    ev_nmm[k] = ph.op("tensor", nmm, inc=[("pe", 1)],
                                      waits=[ev_sq[k]])

                for ksub in (2 * tt, 2 * tt + 1):
                    if ksub < NK:
                        src_t, c, hh, ev_src = norm_srcs[ksub]
                        w2 = [ev_src]
                        if (ksub - 2) in ev_nmm:
                            w2.append(ev_nmm[ksub - 2])
                        ev_sq[ksub] = ph.op(
                            "scalar",
                            lambda e, src_t=src_t, c=c, hh=hh, ksub=ksub:
                            e.activation(
                                sqs2[ksub % 2][:, :],
                                src_t[:, c * T + hh * TH2:
                                      c * T + (hh + 1) * TH2],
                                AF.Square), inc=[("act", 1)], waits=w2)
                        if ksub >= 1:
                            emit_nmm(ksub - 1)
            emit_nmm(NK - 1)
            ev = ev_nmm[NK - 1]
            ev2 = ph.op("vector", lambda e: e.reduce_max(
                r_m[:, :], pqn[:, :], axis=AX.X), inc=[("dve", 1)], waits=[ev])
            ev3 = ph.op("vector", lambda e: e.tensor_scalar(
                r_m[:, :], r_m[:, :], 1.5 / 16.0, 43.0, ALU.mult, ALU.min),
                inc=[("dve", 1)], waits=[ev2])
            ev4 = ph.op("vector", lambda e: e.tensor_scalar_mul(
                r_m[:, :], r_m[:, :], -1.0), inc=[("dve", 1)], waits=[ev3])
            ev5 = ph.op("scalar", lambda e: e.copy(nb_bf[:, :], r_m[:, :]),
                        inc=[("act", 1)], waits=[ev4])
            ph.run()
        ctx_at1.close()

        if want("QG"):
            dump_sbuf("QG", QG[:, :], [128, DC * T])
            dump_sbuf("kT", kT[:, :], [128, DC * T])
            dump_sbuf("vv", vv[:, :], [128, TC * H * 65])
        if stop == "QG":
            _finish(p)
            ctx_at2.close()
            return nc, dump_outs

        with ExitStack() as c2:
            pnb = ps("pnb", [128, 1], F32, c2)
            ph = Phase(p, "bbcast")
            ev6 = ph.op("tensor", lambda e: e.matmul(
                pnb[:, :], ones_row[:, :], nb_bf[:, :], start=True, stop=True),
                inc=[("pe", 1)], waits=[ev5])
            ev_nb = ph.op("scalar", lambda e: e.copy(negB[:, :], pnb[:, :]),
                          inc=[("act", 1)], waits=[ev6])
            ph.run()
        cb.close()

        if want("bb"):
            dump_sbuf("bb", negB[:, :], [128, 1])
        if stop == "bb":
            _finish(p)
            ctx_at2.close()
            return nc, dump_outs

        ctx_at3 = ExitStack()
        avout = sb("avout", [128, DC * T], BF16, ctx_at3, side="right")
        wo_sb = sb("wo_sb", [128, DC * D], BF16, ctx_at3, side="right")
        ph = Phase(p, "wo_load")
        ev_lwo = ph.op("gpsimd", lambda e: e.dma_start(
            wo_sb[:, :].rearrange("p (c n) -> p c n", c=DC),
            WoT_d[:, :].rearrange("(c p) n -> p c n", p=128)), inc=[("dgw", 16)])
        ph.run()

        with ExitStack() as ca:
            NB = 2
            NPT = 5
            PT = [sb("PT%d" % i, [128, 1024], BF16, ca) for i in range(NPT)]
            srow = [sb("srow%d" % i, [1, 512], F32, ca) for i in range(4)]
            sums16 = sb("sums16", [H, T], F32, ca)
            QQ = T // 512
            with ExitStack() as c2:
                pS = [ps("pS%d" % i, [128, 1024], F32, c2) for i in range(NB)]
                pav = [[ps("pav%d_%d" % (par, j), [65, 512], F32, c2)
                        for j in range(2)] for par in range(2)]
                ph = Phase(p, "attn_core")
                ev_exp = {}
                ev_avp = {}
                ev_evac = {}
                ev_srr = {}
                ev_sdma = {}
                it_g = [0]
                for hp in range(H // 2):
                    for qq in range(QQ):
                        hq = hp * QQ + qq

                        def emit_av(kc, hq=hq, hp=hp, qq=qq):
                            it = hq * KC + kc
                            w = [ev_exp[it]]
                            if kc == 0 and hq > 1:
                                w.append(ev_evac[(hq - 2, 0)])
                                w.append(ev_evac[(hq - 2, 1)])
                                w.append(ev_srr[(hq - 2, 0)])
                                w.append(ev_srr[(hq - 2, 1)])

                            def avmm(e, hp=hp, kc=kc, it=it, hq=hq):
                                for j in range(2):
                                    i = e.matmul(
                                        pav[hq % 2][j][:, :],
                                        vv[:, kc * H * 65 + (2 * hp + j) * 65:
                                           kc * H * 65 + (2 * hp + j) * 65 + 65],
                                        PT[it % NPT][:, j * 512:(j + 1) * 512],
                                        start=(kc == 0), stop=(kc == KC - 1))
                                return i
                            ev_avp[it] = ph.op("tensor", avmm, inc=[("pe", 1)],
                                               waits=w)

                        for kc in range(KC):
                            it = hq * KC + kc
                            b = kc % NB
                            w = []
                            if it >= NB:
                                w.append(ev_exp[it - NB])

                            def smm(e, hp=hp, qq=qq, kc=kc, b=b):
                                q0 = qq * 512
                                e.matmul(
                                    pS[b][:, 0:512],
                                    kT[0:64, hp * T + kc * 128:
                                       hp * T + (kc + 1) * 128],
                                    QG[0:64, hp * T + q0: hp * T + q0 + 512],
                                    start=True, stop=True)
                                i = e.matmul(
                                    pS[b][:, 512:1024],
                                    kT[64:128, hp * T + kc * 128:
                                       hp * T + (kc + 1) * 128],
                                    QG[64:128, hp * T + q0: hp * T + q0 + 512],
                                    start=True, stop=True)
                                return i
                            ev_s = ph.op("tensor", smm, inc=[("pe", 1)], waits=w)
                            wv_ = [ev_s, ev_nb]
                            if it >= NPT:
                                wv_.append(ev_avp[it - NPT])
                            ev_exp[it] = ph.op(
                                "scalar", lambda e, it=it, b=b: e.activation(
                                    PT[it % NPT][:, :], pS[b][:, :], AF.Exp,
                                    bias=negB[:, 0:1], scale=0.125),
                                inc=[("act", 1)], waits=wv_)
                            if kc >= 1:
                                emit_av(kc - 1)
                        emit_av(KC - 1)
                        for j in range(2):
                            half = j * 64
                            ev_evac[(hq, j)] = ph.op(
                                "vector",
                                lambda e, hp=hp, qq=qq, j=j, half=half, hq=hq:
                                e.tensor_copy(
                                    avout[half:half + 64,
                                          hp * T + qq * 512: hp * T + qq * 512 + 512],
                                    pav[hq % 2][j][0:64, :]),
                                inc=[("dve", 1)],
                                waits=[ev_avp[hq * KC + KC - 1]])
                            si = 2 * (hq % 2) + j
                            w2 = [ev_avp[hq * KC + KC - 1]]
                            prev = ev_sdma.get((hq - 2, j))
                            if prev:
                                w2.append(prev)
                            ev_sr = ph.op("vector",
                                          lambda e, j=j, si=si, hq=hq: e.tensor_copy(
                                              srow[si][:, :],
                                              pav[hq % 2][j][64:65, :]),
                                          inc=[("dve", 1)], waits=w2)
                            ev_srr[(hq, j)] = ev_sr
                            ev_sdma[(hq, j)] = ph.op(
                                "sync", lambda e, hp=hp, qq=qq, j=j, si=si: e.dma_start(
                                    sums16[2 * hp + j: 2 * hp + j + 1,
                                           qq * 512:(qq + 1) * 512], srow[si][:, :]),
                                inc=[("dsr%d" % j, 16)], waits=[ev_sr])
                ph.run()
            rec16 = sb("rec16", [H, T], F32, ca)
            rec16b = sb("rec16b", [H, T], BF16, ca)
            ev_attn_done = []
            with ExitStack() as c2:
                pnorm = [ps("pnorm%d" % i, [128, T], F32, c2) for i in range(2)]
                ph = Phase(p, "avnorm")
                ev_r = ph.op("vector", lambda e: e.reciprocal(
                    rec16[:, :], sums16[:, :]), inc=[("dve", 1)],
                    waits=[p.now("dsr0"), p.now("dsr1")])
                ev_rb = ph.op("scalar", lambda e: e.copy(rec16b[:, :], rec16[:, :]),
                              inc=[("act", 1)], waits=[ev_r])
                nrm_ev = {}
                ev_b_last = [None]
                for hp in range(DC):
                    pb = hp % 2
                    w = [ev_rb, ev_evac[(hp * (T // 512) + T // 512 - 1, 1)]]
                    if hp >= 2:
                        w.append(nrm_ev[hp - 2])

                    def nbc(e, hp=hp, pb=pb):
                        for j in range(2):
                            hh = 2 * hp + j
                            for n in range(NS):
                                i = e.matmul(
                                    pnorm[pb][64 * j:64 * (j + 1),
                                              n * 512:(n + 1) * 512],
                                    oh16[:, hh * 64:(hh + 1) * 64],
                                    rec16b[:, n * 512:(n + 1) * 512],
                                    start=True, stop=True)
                        return i
                    ev_b = ph.op("tensor", nbc, inc=[("pe", 1)], waits=w)
                    ev_b_last[0] = ev_b
                    nrm_ev[hp] = ph.op("vector", lambda e, hp=hp, pb=pb: e.tensor_mul(
                        avout[:, hp * T:(hp + 1) * T],
                        avout[:, hp * T:(hp + 1) * T], pnorm[pb][:, :]),
                        inc=[("dve", 1)], waits=[ev_b])
                    ev_attn_done.append(nrm_ev[hp])
                ph.run()
        ctx_at2.close()

        if want("avout"):
            dump_sbuf("avout", avout[:, :], [128, DC * T])
        if stop == "avout":
            _finish(p)
            ctx_at3.close()
            return nc, dump_outs

        ctx_xr2 = ExitStack()
        xr2 = sb("xr2", [128, DC * T], F32, ctx_xr2)
        xr2_sl = mk_sl(xr2)

        ph = Phase(p, "reload")
        ev_rel = ph.op("sync", lambda e: e.dma_start(
            xr2[:, :].rearrange("p (c t) -> p c t", c=DC),
            xsp_d[:, :].rearrange("(c p) t -> p c t", p=128)),
            inc=[("drl", 16)], waits=[ev_b_last[0]])
        ph.run()

        with ExitStack() as cw:
            sp_sb = sb("sp_sb", [128, DC * 512], BF16, cw)
            sqw = [sb("sqw%d" % i, [128, 512], BF16, cw) for i in range(2)]
            t32w = [sb("t32w%d" % i, [128, 512], F32, cw) for i in range(2)]
            r_o0 = sb("r_o0", [1, 512], F32, cw)
            r_x0 = sb("r_x0", [1, 512], F32, cw)
            with ExitStack() as c2:
                pw = [ps("pw%d" % i, [128, 512], F32, c2) for i in range(2)]
                psq = ps("psq", [1, 512], F32, c2)
                ph = Phase(p, "wo_tail")
                ev_sp = {}
                ev_sqev = {}
                ev_smm = {}
                ev_add = {}
                ev_o1 = {}
                ev_x0c = {}
                for ts_ in range(NS):
                    for md in range(DC):
                        b = md % 2
                        w = [ev_lwo] + list(ev_attn_done)
                        if (ts_, md - 2) in ev_sp:
                            w.append(ev_sp[(ts_, md - 2)])
                        elif ts_ > 0:
                            w.append(ev_sp[(ts_ - 1, DC - 2 + md)])

                        def wmma(e, ts_=ts_, md=md, b=b):
                            for c in range(DC // 2):
                                i = e.matmul(
                                    pw[b][:, :],
                                    wo_sb[:, c * D + md * 128: c * D + (md + 1) * 128],
                                    avout[:, c * T + ts_ * 512: c * T + ts_ * 512 + 512],
                                    start=(c == 0), stop=False)
                            return i

                        def wmmb(e, ts_=ts_, md=md, b=b):
                            for c in range(DC // 2, DC):
                                i = e.matmul(
                                    pw[b][:, :],
                                    wo_sb[:, c * D + md * 128: c * D + (md + 1) * 128],
                                    avout[:, c * T + ts_ * 512: c * T + ts_ * 512 + 512],
                                    start=False, stop=(c == DC - 1))
                            return i
                        ph.op("tensor", wmma, inc=[("pe", 1)],
                              waits=[x for x in w if x not in ev_attn_done]
                              + ev_attn_done[:DC // 2])
                        ev_mm = ph.op("tensor", wmmb, inc=[("pe", 1)],
                                      waits=ev_attn_done[DC // 2:])
                        w2 = [ev_mm]
                        if ts_ > 0:
                            w2.append(ev_add[(ts_ - 1, md)])
                            w2.append(ev_sqev[(ts_ - 1, md)])
                        ev_t1 = ph.op("vector", lambda e, md=md, b=b: e.tensor_scalar(
                            t32w[b][:, :], pw[b][:, :], 0.1, 8.0, ALU.mult, ALU.min),
                            inc=[("dve", 1)], waits=w2)
                        ev_sp[(ts_, md)] = ph.op(
                            "vector", lambda e, md=md, b=b: e.tensor_scalar_max(
                                sp_sb[:, md * 512:(md + 1) * 512], t32w[b][:, :],
                                -8.0), inc=[("dve", 1)], waits=[ev_t1])
                        w3 = [ev_sp[(ts_, md)]]
                        if (ts_, md - 2) in ev_smm:
                            w3.append(ev_smm[(ts_, md - 2)])
                        elif ts_ > 0 and md < 2:
                            w3.append(ev_smm[(ts_ - 1, DC - 2 + md)])
                        ev_sqev[(ts_, md)] = ph.op(
                            "scalar", lambda e, md=md, b=b: e.activation(
                                sqw[b][:, :], sp_sb[:, md * 512:(md + 1) * 512],
                                AF.Square), inc=[("act", 1)], waits=w3)

                        def emit_sqmm(mk, ts_=ts_):
                            w4 = [ev_sqev[(ts_, mk)]]
                            if mk == 0 and ts_ > 0:
                                w4.append(ev_o1[ts_ - 1])

                            def sqmm(e, mk=mk):
                                lhs = ones_col0 if mk == 0 else ones_col
                                return e.matmul(psq[:, :], lhs[:, :],
                                                sqw[mk % 2][:, :],
                                                start=(mk == 0),
                                                stop=(mk == DC - 1))
                            ev_smm[(ts_, mk)] = ph.op("tensor", sqmm,
                                                      inc=[("pe", 1)], waits=w4)

                        if md >= 1:
                            emit_sqmm(md - 1)
                        if md == DC - 1:
                            emit_sqmm(DC - 1)
                    w5 = [ev_smm[(ts_, DC - 1)]]
                    if ts_ > 0:
                        w5.append(ev_x0c[ts_ - 1])
                    ev_o1[ts_] = ph.op("scalar", lambda e: e.activation(
                        r_o0[:, :], psq[:, :], AF.Ln, bias=c_1eps[0:1, 0:1]),
                        inc=[("act", 1)], waits=w5)
                    ev_o2 = ph.op("scalar", lambda e: e.activation(
                        r_o0[:, :], r_o0[:, :], AF.Exp, scale=0.5),
                        inc=[("act", 1)], waits=[ev_o1[ts_]])
                    w6 = [ev_o2, ev_rel]
                    if ts_ > 0:
                        w6.append(ev_x0c[ts_ - 1])
                    ev_x0 = ph.op("vector", lambda e, ts_=ts_: e.tensor_add(
                        r_x0[:, :], xr2[0:1, ts_ * 512: ts_ * 512 + 512], r_o0[:, :]),
                        inc=[("dve", 1)], waits=w6)
                    for c in range(DC):
                        ev_add[(ts_, c)] = ph.op(
                            "vector", lambda e, ts_=ts_, c=c: e.tensor_add(
                                xr2_sl(c, ts_ * 512, 512), xr2_sl(c, ts_ * 512, 512),
                                sp_sb[:, c * 512:(c + 1) * 512]),
                            inc=[("dve", 1)],
                            waits=[ev_sp[(ts_, c)], ev_x0, ev_rel])
                    ev_x0c[ts_] = ph.op("scalar", lambda e, ts_=ts_: e.copy(
                        xr2[0:1, ts_ * 512: ts_ * 512 + 512], r_x0[:, :]),
                        inc=[("act", 1)], waits=[ev_add[(ts_, 0)]])
                ph.run()
        ctx_at3.close()

        if want("attn"):
            dump_sbuf("attn", xr2[:, :], [128, DC * T])
        if stop == "attn":
            _finish(p)
            ctx_xr2.close()
            return nc, dump_outs

        f2_res = ffn_sublayer("f2", xr2_sl)

        ph = Phase(p, "out")
        for tc in range(T // 512):
            ph.op("sync", lambda e, tc=tc: e.dma_start(
                bass.AP(yT_d, tc * 512, [[T, 128], [128 * T, DC], [1, 512]]),
                bass.AP(xr2, tc * 512, [[DC * T, 128], [T, DC], [1, 512]])),
                inc=[("dou", 16)],
                waits=[f2_res[(tc, 0)], f2_res[(tc, 1)]])
        ph.run()
        _finish(p)
        ctx_xr2.close()

    return nc, dump_outs


def _finish(p):
    ph = Phase(p, "final_wait")
    sems = ["dxp", "dgp", "dgd", "dg0", "dg1", "dh0", "dh1", "dhw0", "dhw1",
            "dgw", "dsp", "drl", "dou", "dsr0", "dsr1", "dw10", "dw11"]
    ph.op("gpsimd", lambda e: e.nop(), waits=[p.now(s) for s in sems])
    ph.run()


def host_inputs(inputs, cfg=CFG):
    E, FC, DC = cfg.E, cfg.FC, cfg.DC
    x = np.asarray(inputs["L0_attn_x"])
    base = {
        "WrT": np.ascontiguousarray(np.asarray(inputs["L0_ffn_router_w"]).T),
        "rb": np.ascontiguousarray(
            np.asarray(inputs["L0_ffn_router_b"]).reshape(1, E)),
        "WeT": np.ascontiguousarray(
            np.asarray(inputs["L0_ffn_expert_w"]).transpose(0, 2, 1)),
        "Be": np.ascontiguousarray(np.asarray(inputs["L0_ffn_expert_b"])),
        "W1T": np.ascontiguousarray(np.asarray(inputs["L0_attn_W1"]).T),
        "b1T": np.ascontiguousarray(
            np.asarray(inputs["L0_attn_b1"]).reshape(FC, 128).T),
        "W2T": np.ascontiguousarray(np.asarray(inputs["L0_attn_W2"]).T),
        "b2T": np.ascontiguousarray(
            np.asarray(inputs["L0_attn_b2"]).reshape(DC, 128).T),
        "WqT": np.ascontiguousarray(np.asarray(inputs["L0_attn_Wq"]).T),
        "WkT": np.ascontiguousarray(np.asarray(inputs["L0_attn_Wk"]).T),
        "WvT": np.ascontiguousarray(np.asarray(inputs["L0_attn_Wv"]).T),
        "WoT": np.ascontiguousarray(np.asarray(inputs["L0_attn_Wo"]).T),
    }
    maps = []
    for c in range(x.shape[0]):
        m = dict(base)
        m["xT"] = np.ascontiguousarray(x[c].T)
        maps.append(m)
    return maps


def kernel(**inputs):
    nc, _ = build(CFG)
    in_maps = host_inputs(inputs, CFG)
    res = run_bass_kernel_spmd(nc, in_maps, core_ids=list(range(8)))
    out = np.stack([np.asarray(r["yT"]).T for r in res.results], axis=0)
    return np.ascontiguousarray(out.astype(np.float32))


# version 21 (final)
# speedup vs baseline: 1.1220x; 1.0183x over previous
import numpy as np
from contextlib import ExitStack

import concourse.bass as bass
import concourse.mybir as mybir
from concourse.bass_utils import run_bass_kernel_spmd

F32 = mybir.dt.float32
BF16 = mybir.dt.bfloat16
AF = mybir.ActivationFunctionType
ALU = mybir.AluOpType
AX = mybir.AxisListType

EPS = 1e-6
LN_EPS = 1e-5


class Cfg:
    def __init__(self, T=2048, D=1024, DFF=2048, H=16, E=8):
        assert D % 512 == 0 and DFF % 128 == 0 and T % 1024 == 0
        assert D // H == 64
        self.T, self.D, self.DFF, self.H, self.E = T, D, DFF, H, E
        self.DC = D // 128
        self.FC = DFF // 128
        self.TC = T // 128


CFG = Cfg()
NDUM = (2, 1)
ENGS = ("tensor", "vector", "scalar", "gpsimd", "sync")
_blk_ctr = [0]


class Prog:
    def __init__(self, nc, ctx):
        self.nc = nc
        self.ctx = ctx
        self.sems = {}
        self.cnt = {}

    def sem(self, name):
        if name not in self.sems:
            self.sems[name] = self.ctx.enter_context(self.nc.semaphore(name))
            self.cnt[name] = 0
        return self.sems[name]

    def now(self, name):
        self.sem(name)
        return (name, self.cnt[name])


class Phase:

    def __init__(self, p, name=""):
        self.p = p
        _blk_ctr[0] += 1
        self.name = "%s_%d" % (name, _blk_ctr[0])
        self.ops = {e: [] for e in ENGS}

    def op(self, eng, fn, inc=(), waits=()):
        p = self.p
        evs = []
        for sname, by in inc:
            p.sem(sname)
            p.cnt[sname] += by
            evs.append((sname, p.cnt[sname]))
        wd = {}
        for wv in waits:
            if wv is None:
                continue
            sname, val = wv
            wd[sname] = max(wd.get(sname, 0), val)
        self.ops[eng].append((sorted(wd.items()), fn, [(s, b) for s, b in inc]))
        return evs[0] if evs else None

    def run(self):
        p = self.p

        def mk_body(oplist):
            def body(eng):
                for waits, fn, incs in oplist:
                    for sname, val in waits:
                        if val > 0:
                            eng.wait_ge(p.sems[sname], val)
                    inst = fn(eng)
                    for sname, by in incs:
                        inst.then_inc(p.sems[sname], by)
            return body

        with p.nc.Block(self.name) as blk:
            for e in ENGS:
                if self.ops[e]:
                    getattr(blk, e)(mk_body(self.ops[e]))


def build(cfg=CFG, dumps=(), stop=None):
    T, D, DFF, H, E = cfg.T, cfg.D, cfg.DFF, cfg.H, cfg.E
    DC, FC, TC = cfg.DC, cfg.FC, cfg.TC
    KC = T // 128
    NS = T // 512
    QH = T // 1024

    nc = bass.Bass("TRN2", target_bir_lowering=False)

    xT_d = nc.dram_tensor("xT", [D, T], F32, kind="ExternalInput")
    WrT_d = nc.dram_tensor("WrT", [D, E], F32, kind="ExternalInput")
    rb_d = nc.dram_tensor("rb", [1, E], F32, kind="ExternalInput")
    WeT_d = nc.dram_tensor("WeT", [E, D, D], F32, kind="ExternalInput")
    Be_d = nc.dram_tensor("Be", [E, D], F32, kind="ExternalInput")
    W1T_d = nc.dram_tensor("W1T", [D, DFF], F32, kind="ExternalInput")
    b1T_d = nc.dram_tensor("b1T", [128, FC], F32, kind="ExternalInput")
    W2T_d = nc.dram_tensor("W2T", [DFF, D], F32, kind="ExternalInput")
    b2T_d = nc.dram_tensor("b2T", [128, DC], F32, kind="ExternalInput")
    WqT_d = nc.dram_tensor("WqT", [D, D], F32, kind="ExternalInput")
    WkT_d = nc.dram_tensor("WkT", [D, D], F32, kind="ExternalInput")
    WvT_d = nc.dram_tensor("WvT", [D, D], F32, kind="ExternalInput")
    WoT_d = nc.dram_tensor("WoT", [D, D], F32, kind="ExternalInput")
    yT_d = nc.dram_tensor("yT", [D, T], F32, kind="ExternalOutput")

    WeC_d = nc.dram_tensor("WeC", [E, D, D], BF16, kind="Internal")
    xsp_d = nc.dram_tensor("xsp", [D, T], F32, kind="Internal")

    dump_outs = {}

    def want(tag):
        return tag in dumps

    with ExitStack() as ctx0:
        p = Prog(nc, ctx0)

        def sb(name, shape, dtype, c=None, side="left"):
            return (c or ctx0).enter_context(
                nc.sbuf_tensor(name, shape, dtype, side=side))

        def ps(name, shape, dtype, c=None, side="left"):
            return (c or ctx0).enter_context(
                nc.psum_tensor(name, shape, dtype, side=side))

        ident = sb("ident", [128, 128], BF16)
        ones_col = sb("ones_col", [128, 1], BF16)
        ones_col0 = sb("ones_col0", [128, 1], BF16)
        mean_col = sb("mean_col", [128, 1], BF16)
        ones_row = sb("ones_row", [1, 128], BF16)
        sel_sb = sb("sel_sb", [8, E * 128], BF16)
        oh16 = sb("oh16", [H, H * 64], BF16)
        b1T = sb("b1T_s", [128, FC], F32)
        b2T = sb("b2T_s", [128, DC], F32)
        Be_bf = sb("Be_bf", [8, D], BF16)
        wr32 = sb("wr32", [128, DC * E], F32)
        wr_hi = sb("wr_hi", [128, DC * E], BF16)
        wr_lo = sb("wr_lo", [128, DC * E], BF16)
        rb32 = sb("rb32", [1, E], F32)
        rb_hi = sb("rb_hi", [1, E], BF16)
        rb_lo = sb("rb_lo", [1, E], BF16)
        negB = sb("negB", [128, 1], F32)
        c_lneps = sb("c_lneps", [128, 1], F32)
        c_1eps = sb("c_1eps", [128, 1], F32)
        c_neg1 = sb("c_neg1", [128, 1], F32)
        c_pos1 = sb("c_pos1", [128, 1], F32)

        ctx_xr = ExitStack()
        xr = sb("xr", [128, DC * T], F32, ctx_xr)

        def mk_sl(t):
            def sl(c, t0=0, tn=None):
                tn = T if tn is None else tn
                return t[:, c * T + t0: c * T + t0 + tn]
            return sl
        xr_sl = mk_sl(xr)

        ph = Phase(p, "prologue")
        ev_xT = ph.op("sync", lambda e: e.dma_start(
            xr[:, :].rearrange("p (c t) -> p c t", c=DC),
            xT_d[:, :].rearrange("(c p) t -> p c t", p=128)), inc=[("dxp", 16)])
        ph.op("sync", lambda e: e.dma_start(
            wr32[:, :].rearrange("p (c e) -> p c e", c=DC),
            WrT_d[:, :].rearrange("(c p) e -> p c e", p=128)), inc=[("dxp", 16)])
        ph.op("sync", lambda e: e.dma_start(rb32[:, :], rb_d[:, :]), inc=[("dxp", 16)])
        ph.op("sync", lambda e: e.dma_start(b1T[:, :], b1T_d[:, :]), inc=[("dxp", 16)])
        ph.op("sync", lambda e: e.dma_start(b2T[:, :], b2T_d[:, :]), inc=[("dxp", 16)])
        ev_be = ph.op("gpsimd", lambda e: e.dma_start(Be_bf[:, :], Be_d[:, :]), inc=[("dgp", 16)])
        ev = ph.op("gpsimd", lambda e: e.memset(ident[:, :], 0.0), inc=[("pl", 1)])
        ph.op("gpsimd", lambda e: e.affine_select(
            out=ident[:, :], in_=ident[:, :], compare_op=ALU.not_equal,
            fill=1.0, base=0, pattern=[[-1, 128]], channel_multiplier=1),
            inc=[("pl", 1)], waits=[ev])
        ph.op("gpsimd", lambda e: e.memset(ones_col[:, :], 1.0), inc=[("pl", 1)])
        ev = ph.op("gpsimd", lambda e: e.memset(ones_col0[:, :], 1.0), inc=[("pl", 1)])
        ph.op("gpsimd", lambda e: e.memset(ones_col0[0:1, :], 0.0),
              inc=[("pl", 1)], waits=[ev])
        ph.op("gpsimd", lambda e: e.memset(mean_col[:, :], 1.0 / D), inc=[("pl", 1)])
        ph.op("gpsimd", lambda e: e.memset(ones_row[:, :], 1.0), inc=[("pl", 1)])
        ph.op("gpsimd", lambda e: e.memset(c_lneps[:, :], LN_EPS), inc=[("pl", 1)])
        ph.op("gpsimd", lambda e: e.memset(c_1eps[:, :], 1.0 + EPS), inc=[("pl", 1)])
        ph.op("gpsimd", lambda e: e.memset(c_neg1[:, :], -1.0), inc=[("pl", 1)])
        ph.op("gpsimd", lambda e: e.memset(c_pos1[:, :], 1.0), inc=[("pl", 1)])
        ev = ph.op("gpsimd", lambda e: e.memset(sel_sb[:, :], 0.0), inc=[("pl", 1)])
        ph.op("gpsimd", lambda e: e.affine_select(
            out=bass.AP(sel_sb, 0, [[E * 128, 8], [128, E], [1, 128]]),
            in_=bass.AP(sel_sb, 0, [[E * 128, 8], [128, E], [1, 128]]),
            compare_op=ALU.not_equal, fill=1.0, base=0,
            pattern=[[-1, E], [0, 128]], channel_multiplier=1),
            inc=[("pl", 1)], waits=[ev])
        ev = ph.op("gpsimd", lambda e: e.memset(oh16[:, :], 0.0), inc=[("pl", 1)])
        ph.op("gpsimd", lambda e: e.affine_select(
            out=bass.AP(oh16, 0, [[H * 64, H], [64, H], [1, 64]]),
            in_=bass.AP(oh16, 0, [[H * 64, H], [64, H], [1, 64]]),
            compare_op=ALU.not_equal, fill=1.0, base=0,
            pattern=[[-1, H], [0, 64]], channel_multiplier=1),
            inc=[("pl", 1)], waits=[ev])
        ph.run()

        ph = Phase(p, "loads_wait")
        wall = [p.now("dxp"), p.now("dgp"), p.now("pl")]
        for e_ in ENGS:
            ph.op(e_, lambda e: e.nop(), waits=wall)
        ph.run()

        ph = Phase(p, "router_split")
        ev_hi = ph.op("scalar", lambda e: e.copy(wr_hi[:, :], wr32[:, :]),
                      inc=[("act", 1)])
        ph.op("vector", lambda e: e.tensor_sub(wr_lo[:, :], wr32[:, :], wr_hi[:, :]),
              inc=[("dve", 1)], waits=[ev_hi])
        ev_hi = ph.op("scalar", lambda e: e.copy(rb_hi[:, :], rb32[:, :]),
                      inc=[("act", 1)])
        ph.op("vector", lambda e: e.tensor_sub(rb_lo[:, :], rb32[:, :], rb_hi[:, :]),
              inc=[("dve", 1)], waits=[ev_hi])
        ph.run()

        def dump_sbuf(tag, ap, shape):
            d = nc.dram_tensor("dump_" + tag, list(shape), F32, kind="ExternalOutput")
            dump_outs[tag] = d
            ph = Phase(p, "dump_" + tag)
            ph.op("gpsimd", lambda e: e.dma_start(d[:, :], ap), inc=[("dgd", 16)])
            ph.op("gpsimd", lambda e: e.nop(), waits=[p.now("dgd")])
            ph.run()

        if want("xr0"):
            dump_sbuf("xr0", xr[:, :], [128, DC * T])

        def layernorm(tag, xcur_sl, out_hi, out_lo, chunk_consumer=None,
                      ph_ext=None, fold_r=False, r_bf_out=None):
            with ExitStack() as c1:
                xbf = [sb("xbf%s%d" % (tag, i), [128, T], BF16, c1) for i in range(2)]
                sqb = [sb("sqb%s%d" % (tag, i), [128, T], BF16, c1) for i in range(2)]
                m_row = sb("m_row" + tag, [1, T], F32, c1)
                v2_row = sb("v2_row" + tag, [1, T], F32, c1)
                ra = sb("ra_row" + tag, [1, T], F32, c1)
                m_bf = sb("m_bf" + tag, [1, T], BF16, c1)
                r_bf = r_bf_out if r_bf_out is not None \
                    else sb("r_bf" + tag, [1, T], BF16, c1)
                t32 = [sb("t32%s%d" % (tag, i), [128, T // 2], F32, c1)
                       for i in range(2)]
                with ExitStack() as c2:
                    pm = ps("pm" + tag, [1, T], F32, c2)
                    pv = ps("pv" + tag, [1, T], F32, c2)
                    ph = ph_ext if ph_ext is not None else Phase(p, "ln_" + tag)
                    ev_mm = {}
                    ev_cs = {}

                    def emit_mm(c):
                        b = c % 2

                        def mm(e, c=c, b=b):
                            for n in range(NS):
                                e.matmul(pm[:, n * 512:(n + 1) * 512], mean_col[:, :],
                                         xbf[b][:, n * 512:(n + 1) * 512],
                                         start=(c == 0), stop=(c == DC - 1))
                            for n in range(NS):
                                i = e.matmul(pv[:, n * 512:(n + 1) * 512],
                                             mean_col[:, :],
                                             sqb[b][:, n * 512:(n + 1) * 512],
                                             start=(c == 0), stop=(c == DC - 1))
                            return i
                        ev_mm[c] = ph.op("tensor", mm, inc=[("pe", 1)],
                                         waits=list(ev_cs[c]))

                    for c in range(DC):
                        b = c % 2
                        w = [ev_mm[c - 2]] if (c - 2) in ev_mm else []
                        ev_cast = ph.op("scalar", lambda e, c=c, b=b: e.copy(
                            xbf[b][:, :], xcur_sl(c)), inc=[("act", 1)], waits=w)
                        ev_sq = ph.op("scalar", lambda e, c=c, b=b: e.activation(
                            sqb[b][:, :], xcur_sl(c), AF.Square),
                            inc=[("act", 1)], waits=w)
                        ev_cs[c] = (ev_cast, ev_sq)
                        if c >= 1:
                            emit_mm(c - 1)
                    emit_mm(DC - 1)
                    ev = ev_mm[DC - 1]
                    ev_m = ph.op("scalar", lambda e: e.copy(m_row[:, :], pm[:, :]),
                                 inc=[("act", 1)], waits=[ev])
                    ev_v2 = ph.op("scalar", lambda e: e.copy(v2_row[:, :], pv[:, :]),
                                  inc=[("act", 1)], waits=[ev])
                    ev_msq = ph.op("vector", lambda e: e.tensor_mul(
                        ra[:, :], m_row[:, :], m_row[:, :]), inc=[("dve", 1)],
                        waits=[ev_m])
                    ev_var = ph.op("vector", lambda e: e.tensor_sub(
                        v2_row[:, :], v2_row[:, :], ra[:, :]), inc=[("dve", 1)],
                        waits=[ev_msq, ev_v2])
                    ev_ln = ph.op("scalar", lambda e: e.activation(
                        ra[:, :], v2_row[:, :], AF.Ln, bias=c_lneps[0:1, 0:1]),
                        inc=[("act", 1)], waits=[ev_var])
                    ev_rs = ph.op("scalar", lambda e: e.activation(
                        ra[:, :], ra[:, :], AF.Exp, scale=-0.5),
                        inc=[("act", 1)], waits=[ev_ln])
                    ev_mbf = ph.op("scalar", lambda e: e.copy(m_bf[:, :], m_row[:, :]),
                                   inc=[("act", 1)], waits=[ev_m])
                    ev_rbf = ph.op("scalar", lambda e: e.copy(r_bf[:, :], ra[:, :]),
                                   inc=[("act", 1)], waits=[ev_rs])
                with ExitStack() as c2:
                    TH = T // 2
                    pmb = ps("pmb" + tag, [128, TH], F32, c2)
                    prb = None if fold_r else ps("prb" + tag, [128, TH], F32, c2)
                    ev_hi = {}
                    ev_lo = {}
                    ev_ap = {}
                    for half in range(2):
                        t0 = half * TH
                        wbc = [ev_mbf, ev_rbf]
                        if half > 0:
                            wbc += [ev_ap[(c, 0)] for c in range(DC)]

                        def bc(e, t0=t0):
                            for n in range(TH // 512):
                                i = e.matmul(pmb[:, n * 512:(n + 1) * 512],
                                             ones_row[:, :],
                                             m_bf[:, t0 + n * 512:
                                                  t0 + (n + 1) * 512],
                                             start=True, stop=True)
                            if prb is not None:
                                for n in range(TH // 512):
                                    i = e.matmul(prb[:, n * 512:(n + 1) * 512],
                                                 ones_row[:, :],
                                                 r_bf[:, t0 + n * 512:
                                                      t0 + (n + 1) * 512],
                                                 start=True, stop=True)
                            return i
                        ev_bc = ph.op("tensor", bc, inc=[("pe", 1)], waits=wbc)
                        for c in range(DC):
                            b = c % 2
                            w = [ev_bc]
                            for k in ((c - 2, half), (DC - 2 + c, half - 1)):
                                if k in ev_ap:
                                    w.append(ev_ap[k])
                                    w.append(ev_hi[k])
                            ev_sub = ph.op("vector", lambda e, c=c, b=b, t0=t0: e.tensor_sub(
                                t32[b][:, :], xcur_sl(c, t0, TH), pmb[:, :]),
                                inc=[("dve", 1)], waits=w)
                            if out_lo is None:
                                ev_p = ph.op(
                                    "vector", lambda e, c=c, b=b, t0=t0: e.tensor_mul(
                                        out_hi[:, c * T + t0: c * T + t0 + TH],
                                        t32[b][:, :], prb[:, :]),
                                    inc=[("dve", 1)], waits=[ev_sub])
                                ev_hi[(c, half)] = ev_p
                            elif fold_r:
                                ev_p = ev_sub
                                ev_hi[(c, half)] = ph.op(
                                    "scalar", lambda e, c=c, b=b, t0=t0: e.copy(
                                        out_hi[:, c * T + t0: c * T + t0 + TH],
                                        t32[b][:, :]), inc=[("act", 1)],
                                    waits=[ev_sub])
                            else:
                                ev_p = ph.op(
                                    "vector", lambda e, c=c, b=b: e.tensor_mul(
                                        t32[b][:, :], t32[b][:, :], prb[:, :]),
                                    inc=[("dve", 1)], waits=[ev_sub])
                                ev_hi[(c, half)] = ph.op(
                                    "scalar", lambda e, c=c, b=b, t0=t0: e.copy(
                                        out_hi[:, c * T + t0: c * T + t0 + TH],
                                        t32[b][:, :]), inc=[("act", 1)],
                                    waits=[ev_p])
                            last = ev_p
                            if out_lo is not None:
                                ev_lo[(c, half)] = ph.op(
                                    "vector", lambda e, c=c, b=b, t0=t0: e.tensor_sub(
                                        out_lo[:, c * T + t0: c * T + t0 + TH],
                                        t32[b][:, :],
                                        out_hi[:, c * T + t0: c * T + t0 + TH]),
                                    inc=[("dve", 1)], waits=[ev_hi[(c, half)]])
                                last = ev_lo[(c, half)]
                            ev_ap[(c, half)] = last
                            if chunk_consumer is not None:
                                chunk_consumer(ph, c, half, ev_hi[(c, half)],
                                               ev_lo.get((c, half)))
                    if ph_ext is None:
                        ph.run()
            return r_bf, ev_rbf, ev_ap

        ffn_state = {"cached": False}

        def ffn_sublayer(tag, xcur_sl):
            with ExitStack() as cf:
                xnhi = sb("xnhi" + tag, [128, DC * T], BF16, cf)
                xnlo_ctx = ExitStack()
                xnlo = sb("xnlo" + tag, [128, DC * T], BF16, xnlo_ctx, side="right")
                wm_tok = sb("wm_tok" + tag, [128, TC * E], BF16, cf)
                wmT = sb("wmT" + tag, [8, T], BF16, cf)
                scr8 = sb("scr8" + tag, [128, E], F32, cf)
                msk8 = sb("msk8" + tag, [128, E], F32, cf)
                rmax = sb("rmax" + tag, [128, 1], F32, cf)
                nmax = sb("nmax" + tag, [128, 1], F32, cf)
                sume = sb("sume" + tag, [128, 1], F32, cf)
                wrec = sb("wrec" + tag, [128, 1], F32, cf)
                rt_ctx = ExitStack()
                rt_last = {}
                plg_box = {}

                def router_consumer(ph, c, half, ev_hi_e, ev_lo_e):
                    if "plg" not in plg_box:
                        plg_box["plg"] = ps("plg" + tag, [128, TC * E], F32,
                                            rt_ctx, side="right")
                    plg = plg_box["plg"]
                    tcs = list(range(half * (TC // 2), (half + 1) * (TC // 2)))

                    def rmm(e, c=c, half=half, tcs=tuple(tcs)):
                        i = None
                        for tc in tcs:
                            for j, (lh, rh) in enumerate(
                                    ((xnhi, wr_hi), (xnhi, wr_lo), (xnlo, wr_hi))):
                                i = e.matmul(
                                    plg[:, tc * E:(tc + 1) * E],
                                    lh[:, c * T + tc * 128: c * T + (tc + 1) * 128],
                                    rh[:, c * E:(c + 1) * E],
                                    start=(c == 0 and half == 0
                                           and tc == tcs[0] and j == 0),
                                    stop=False)
                            if c == DC - 1:
                                e.matmul(plg[:, tc * E:(tc + 1) * E], ones_row[:, :],
                                         rb_hi[:, :], start=False, stop=False)
                                i = e.matmul(plg[:, tc * E:(tc + 1) * E],
                                             ones_row[:, :], rb_lo[:, :],
                                             start=False,
                                             stop=(half == 1 and tc == tcs[-1]))
                        return i
                    rt_last["ev"] = ph.op("tensor", rmm, inc=[("pe", 1)],
                                          waits=[ev_hi_e, ev_lo_e])

                phm = Phase(p, "ffn_ln_" + tag)
                rbf_f = sb("rbf_" + tag, [1, T], BF16, cf)
                rT128 = sb("rT128" + tag, [128, TC], F32, cf)
                wm_tokr = sb("wm_tokr" + tag, [128, TC * E], BF16, cf)
                wmTr = sb("wmTr" + tag, [8, T], BF16, cf)
                _, ev_rbf, ev_apd = layernorm(
                    tag, xcur_sl, xnhi, xnlo, chunk_consumer=router_consumer,
                    ph_ext=phm, fold_r=True, r_bf_out=rbf_f)

                plg = plg_box["plg"]
                with ExitStack() as c2x:
                    ph = phm
                    ev_lg = rt_last["ev"]
                    prt = ps("prt" + tag, [128, 2 * TC], BF16, c2x)
                    rmax_t = sb("rmax_t" + tag, [128, TC], F32, cf)
                    sume_t = sb("sume_t" + tag, [128, TC], F32, cf)
                    scal_s = sb("scal_s" + tag, [128, TC * E], F32, cf)
                    scr_s = sb("scr_s" + tag, [128, TC * E], F32, cf)
                    scr_m = sb("scr_m" + tag, [128, TC * E], F32, cf)
                    scal3 = bass.AP(scal_s, 0, [[TC * E, 128], [E, TC], [1, E]])
                    plg3 = bass.AP(plg, 0, [[TC * E, 128], [E, TC], [1, E]])

                    def bc3(t):
                        return bass.AP(t, 0, [[TC, 128], [1, TC], [0, E]])

                    def trr(e):
                        for k in range(TC):
                            i = e.transpose(prt[:, 2 * k:2 * k + 1],
                                            rbf_f[0:1, k * 128:(k + 1) * 128],
                                            ident[0:1, 0:1])
                        return i
                    ev_rt = ph.op("tensor", trr, inc=[("pe", 1)],
                                  waits=[ev_rbf, ev_apd[(DC - 1, 1)]])
                    ev_rtc = ph.op("scalar", lambda e: e.copy(
                        rT128[:, :], bass.AP(prt, 0, [[2 * TC, 128], [2, TC]])),
                        inc=[("act", 1)], waits=[ev_rt])
                    ev_sc = ph.op("vector", lambda e: e.tensor_tensor(
                        scal3, plg3, bc3(rT128), ALU.mult),
                        inc=[("dve", 1)], waits=[ev_lg, ev_rtc])
                    ev1 = ph.op("vector", lambda e: e.tensor_reduce(
                        rmax_t[:, :], scal3, axis=AX.X, op=ALU.max),
                        inc=[("dve", 1)], waits=[ev_sc])
                    ev2 = ph.op("vector", lambda e: e.tensor_tensor(
                        bass.AP(scr_s, 0, [[TC * E, 128], [E, TC], [1, E]]),
                        scal3, bc3(rmax_t), ALU.subtract),
                        inc=[("dve", 1)], waits=[ev1])
                    ev3 = ph.op("vector", lambda e: e.tensor_tensor(
                        bass.AP(scr_m, 0, [[TC * E, 128], [E, TC], [1, E]]),
                        scal3, bc3(rmax_t), ALU.is_equal),
                        inc=[("dve", 1)], waits=[ev1])
                    ev4 = ph.op("scalar", lambda e: e.activation(
                        scr_s[:, :], scr_s[:, :], AF.Exp), inc=[("act", 1)],
                        waits=[ev2])
                    ev5 = ph.op("vector", lambda e: e.tensor_reduce(
                        sume_t[:, :],
                        bass.AP(scr_s, 0, [[TC * E, 128], [E, TC], [1, E]]),
                        axis=AX.X, op=ALU.add), inc=[("dve", 1)], waits=[ev4])
                    ev6 = ph.op("vector", lambda e: e.reciprocal(
                        sume_t[:, :], sume_t[:, :]), inc=[("dve", 1)], waits=[ev5])
                    ev_wm = ph.op("vector", lambda e: e.scalar_tensor_tensor(
                        bass.AP(wm_tok, 0, [[TC * E, 128], [E, TC], [1, E]]),
                        bass.AP(scr_m, 0, [[TC * E, 128], [E, TC], [1, E]]),
                        0.5, bc3(sume_t), ALU.mult, ALU.mult),
                        inc=[("dve", 1)], waits=[ev3, ev6])
                    ev_wmr = ph.op("vector", lambda e: e.tensor_tensor(
                        bass.AP(wm_tokr, 0, [[TC * E, 128], [E, TC], [1, E]]),
                        bass.AP(wm_tok, 0, [[TC * E, 128], [E, TC], [1, E]]),
                        bc3(rT128), ALU.mult),
                        inc=[("dve", 1)], waits=[ev_wm])
                rt_ctx.close()
                with ExitStack() as c3:
                    pwt = ps("pwt" + tag, [8, T], BF16, c3)

                    def tr(e, src=wm_tok):
                        for tc in range(TC):
                            i = e.transpose(pwt[:, tc * 128:(tc + 1) * 128],
                                            src[:, tc * E:(tc + 1) * E],
                                            ident[:, :])
                        return i
                    ev = ph.op("tensor", tr, inc=[("pe", 1)], waits=[ev_wm])
                    ev_wmT = ph.op("scalar", lambda e: e.copy(wmT[:, :], pwt[:, :]),
                                   inc=[("act", 1)], waits=[ev])
                    ev2t = ph.op("tensor",
                                 lambda e: tr(e, src=wm_tokr), inc=[("pe", 1)],
                                 waits=[ev_wmr, ev_wmT])
                    ev_wmTr = ph.op("scalar", lambda e: e.copy(
                        wmTr[:, :], pwt[:, :]), inc=[("act", 1)], waits=[ev2t])
                xnlo_ctx.close()

                maskb = sb("maskb" + tag, [128, E * T], BF16, cf)
                with ExitStack() as c2:
                    pbc = [ps("pbc%s%d" % (tag, i), [128, T], F32, c2)
                           for i in range(2)]
                    ev_ev = [None, None]
                    for e_i in range(E):
                        b = e_i % 2
                        w = [ev_wmTr]
                        if ev_ev[b]:
                            w.append(ev_ev[b])

                        def bmm(e, e_i=e_i, b=b):
                            for n in range(NS):
                                i = e.matmul(pbc[b][:, n * 512:(n + 1) * 512],
                                             sel_sb[:, e_i * 128:(e_i + 1) * 128],
                                             wmTr[:, n * 512:(n + 1) * 512],
                                             start=True, stop=True)
                            return i
                        ev_mm = ph.op("tensor", bmm, inc=[("pe", 1)], waits=w)
                        ev_ev[b] = ph.op("scalar", lambda e, e_i=e_i, b=b: e.copy(
                            maskb[:, e_i * T:(e_i + 1) * T], pbc[b][:, :]),
                            inc=[("act", 1)], waits=[ev_mm])
                    ev_maskb = ev_ev[(E - 1) % 2]
                    ev_maskb2 = ev_ev[E % 2]
                    phm.run()

                wbuf = [sb("wbuf%s%d" % (tag, i), [128, DC * D], BF16, cf)
                        for i in range(2)]
                xnm = [sb("xnm%s%d" % (tag, i), [128, DC * 512], BF16, cf)
                       for i in range(2)]
                with ExitStack() as c2:
                    py = ps("py" + tag, [128, DC * 512], F32, c2)
                    ph = Phase(p, "experts_" + tag)
                    cached = ffn_state["cached"]
                    load_ev = {}
                    wb_ev = {}
                    mm_ev = {}
                    res_ev = {}
                    TQn = T // 512
                    nli = TQn * E

                    def emit_load(li):
                        tc_, e_ = divmod(li, E)
                        w = []
                        if li >= 2:
                            w.append(mm_ev[li - 2])
                            if (li - 2) in wb_ev:
                                w.append(wb_ev[li - 2])
                        if not cached and tc_ == 0:
                            load_ev[li] = ph.op(
                                "gpsimd", lambda e, e_=e_, li=li: e.dma_start(
                                    wbuf[li % 2][:, :].rearrange(
                                        "p (c n) -> p c n", c=DC),
                                    WeT_d[e_].rearrange("(c p) n -> p c n", p=128)),
                                inc=[("dg%d" % (li % 2), 16)], waits=w)
                            wb_ev[li] = ph.op("sync", lambda e, e_=e_, li=li: e.dma_start(
                                WeC_d[e_].rearrange("(c p) n -> p c n", p=128),
                                wbuf[li % 2][:, :].rearrange(
                                    "p (c n) -> p c n", c=DC)),
                                inc=[("dhw%d" % (li % 2), 16)], waits=[load_ev[li]])
                        else:
                            load_ev[li] = ph.op(
                                "sync", lambda e, e_=e_, li=li: e.dma_start(
                                    wbuf[li % 2][:, :].rearrange(
                                        "p (c n) -> p c n", c=DC),
                                    WeC_d[e_].rearrange("(c p) n -> p c n", p=128)),
                                inc=[("dh%d" % (li % 2), 16)], waits=w)

                    emit_load(0)
                    emit_load(1)
                    for tc in range(TQn):
                        for e_i in range(E):
                            li = tc * E + e_i
                            w = [ev_maskb, ev_maskb2]
                            if li >= 2:
                                w.append(mm_ev[li - 2])

                            def vmask(e, tc=tc, e_i=e_i, li=li):
                                for c in range(DC):
                                    i = e.tensor_mul(
                                        xnm[li % 2][:, c * 512:(c + 1) * 512],
                                        xnhi[:, c * T + tc * 512:
                                             c * T + tc * 512 + 512],
                                        maskb[:, e_i * T + tc * 512:
                                              e_i * T + tc * 512 + 512])
                                return i
                            ev_vm = ph.op("vector", vmask, inc=[("dve", 1)], waits=w)
                            w = [ev_vm, load_ev[li]]
                            if tc > 0 and e_i == 0:
                                w.append(res_ev[(tc - 1, 0)])

                            def emma(e, tc=tc, e_i=e_i, li=li):
                                for m in range(DC // 2):
                                    for c in range(DC):
                                        i = e.matmul(
                                            py[:, m * 512:(m + 1) * 512],
                                            wbuf[li % 2][:, c * D + m * 128:
                                                         c * D + (m + 1) * 128],
                                            xnm[li % 2][:, c * 512:(c + 1) * 512],
                                            start=(e_i == 0 and c == 0), stop=False)
                                return i

                            def emmb(e, tc=tc, e_i=e_i, li=li):
                                for m in range(DC // 2, DC):
                                    for c in range(DC):
                                        i = e.matmul(
                                            py[:, m * 512:(m + 1) * 512],
                                            wbuf[li % 2][:, c * D + m * 128:
                                                         c * D + (m + 1) * 128],
                                            xnm[li % 2][:, c * 512:(c + 1) * 512],
                                            start=(e_i == 0 and c == 0), stop=False)
                                return i
                            ph.op("tensor", emma, inc=[("pe", 1)], waits=w)
                            wb_ = [] if not (tc > 0 and e_i == 0) \
                                else [res_ev[(tc - 1, 1)]]
                            mm_ev[li] = ph.op("tensor", emmb, inc=[("pe", 1)],
                                              waits=wb_)
                            if li + 2 < nli:
                                emit_load(li + 2)

                        def bmm2a(e, tc=tc):
                            for m in range(DC // 2):
                                i = e.matmul(py[:, m * 512:(m + 1) * 512],
                                             Be_bf[:, m * 128:(m + 1) * 128],
                                             wmT[:, tc * 512: tc * 512 + 512],
                                             start=False, stop=True)
                            return i

                        def bmm2b(e, tc=tc):
                            for m in range(DC // 2, DC):
                                i = e.matmul(py[:, m * 512:(m + 1) * 512],
                                             Be_bf[:, m * 128:(m + 1) * 128],
                                             wmT[:, tc * 512: tc * 512 + 512],
                                             start=False, stop=True)
                            return i
                        ev_ba = ph.op("tensor", bmm2a, inc=[("pe", 1)],
                                      waits=[ev_wmT])
                        ev_bb = ph.op("tensor", bmm2b, inc=[("pe", 1)], waits=[])

                        def radda(e, tc=tc):
                            for m in range(DC // 2):
                                i = e.tensor_add(
                                    xcur_sl(m, tc * 512, 512),
                                    xcur_sl(m, tc * 512, 512),
                                    py[:, m * 512:(m + 1) * 512])
                            return i

                        def raddb(e, tc=tc):
                            for m in range(DC // 2, DC):
                                i = e.tensor_add(
                                    xcur_sl(m, tc * 512, 512),
                                    xcur_sl(m, tc * 512, 512),
                                    py[:, m * 512:(m + 1) * 512])
                            return i
                        res_ev[(tc, 0)] = ph.op("vector", radda, inc=[("dve", 1)],
                                                waits=[ev_ba])
                        res_ev[(tc, 1)] = ph.op("vector", raddb, inc=[("dve", 1)],
                                                waits=[ev_bb])
                    ph.run()
                return res_ev

        f1_res = ffn_sublayer("f1", xr_sl)
        ffn_state["cached"] = True
        if want("ffn1"):
            dump_sbuf("ffn1", xr[:, :], [128, DC * T])
        if stop == "ffn1":
            _finish(p)
            ctx_xr.close()
            return nc, dump_outs

        ctx_at1 = ExitStack()
        xnT = sb("xnT", [128, DC * T], BF16, ctx_at1, side="right")
        g_sb = sb("g_sb", [128, DC * T], BF16, ctx_at1, side="right")

        ph = Phase(p, "spill")
        ev_spill = ph.op("sync", lambda e: e.dma_start(
            xsp_d[:, :].rearrange("(c p) t -> p c t", p=128),
            xr[:, :].rearrange("p (c t) -> p c t", c=DC)), inc=[("dsp", 16)],
            waits=[f1_res[(T // 512 - 1, 0)], f1_res[(T // 512 - 1, 1)]])
        ph.run()

        vch_ctx = ExitStack()
        clipb = [sb("clipb%d" % i, [128, T // 2], BF16, vch_ctx, side="right")
                 for i in range(2)]
        sqvb = [sb("sqvb%d" % i, [128, T // 2], BF16, vch_ctx, side="right")
                for i in range(2)]
        pss_ctx = ExitStack()
        pss_box = {}
        vch_ev = {"cl": {}, "sq": {}, "mm": {}}

        def vchain_consumer(ph, c, half, ev_hi_e, ev_lo_e):
            if "pss" not in pss_box:
                pss_box["pss"] = ps("pss", [1, T], F32, pss_ctx, side="right")
            pss = pss_box["pss"]
            TH = T // 2
            t0 = half * TH
            gi = half * DC + c
            b = gi % 2
            w = [ev_hi_e]
            if gi >= 2:
                w.append(vch_ev["sq"][gi - 2])
            ev_cl = ph.op("vector", lambda e, c=c, t0=t0, b=b: e.tensor_scalar(
                clipb[b][:, :], xnT[:, c * T + t0: c * T + t0 + TH],
                8.0, -8.0, ALU.min, ALU.max), inc=[("dve", 1)], waits=w)
            vch_ev["cl"][gi] = ev_cl
            w2 = [ev_cl]
            if gi >= 2:
                w2.append(vch_ev["mm"][gi - 2])
            ev_sq = ph.op("scalar", lambda e, b=b: e.activation(
                sqvb[b][:, :], clipb[b][:, :], AF.Square),
                inc=[("act", 1)], waits=w2)
            vch_ev["sq"][gi] = ev_sq

            def smm(e, c=c, t0=t0, b=b):
                lhs = ones_col0 if c == 0 else ones_col
                for n in range(T // 2 // 512):
                    i = e.matmul(pss[:, t0 + n * 512: t0 + (n + 1) * 512],
                                 lhs[:, :], sqvb[b][:, n * 512:(n + 1) * 512],
                                 start=(c == 0), stop=(c == DC - 1))
                return i
            vch_ev["mm"][gi] = ph.op("tensor", smm, inc=[("pe", 1)], waits=[ev_sq])

        layernorm("a", xr_sl, xnT, None, chunk_consumer=vchain_consumer)
        ev_ssdone = [vch_ev["mm"][DC - 1], vch_ev["mm"][2 * DC - 1]]

        ph = Phase(p, "spill_wait")
        for e_ in ENGS:
            ph.op(e_, lambda e: e.nop(), waits=[ev_spill])
        ph.run()
        ctx_xr.close()

        if want("lna"):
            dump_sbuf("lna", xnT[:, :], [128, DC * T])
        if stop == "lna":
            _finish(p)
            ctx_at1.close()
            return nc, dump_outs

        vch_ctx.close()
        cv2 = ExitStack()
        R = [sb("vcr%d" % i, [1, T], F32, cv2) for i in range(5)]
        s_bf = sb("s_bf", [1, T], BF16, cv2)
        dv0s_bf = sb("dv0s_bf", [1, T], BF16, cv2)
        vce = {}

        def emit_vchain(ph):
            pss = pss_box["pss"]
            ev_ss = ph.op("scalar", lambda e: e.copy(R[0][:, :], pss[:, :]),
                          inc=[("act", 1)], waits=ev_ssdone)
            yield
            ev1 = ph.op("scalar", lambda e: e.activation(
                R[1][:, :], R[0][:, :], AF.Ln, bias=c_1eps[0:1, 0:1]),
                inc=[("act", 1)], waits=[ev_ss])
            yield
            ev2 = ph.op("scalar", lambda e: e.activation(
                R[1][:, :], R[1][:, :], AF.Exp, scale=0.5),
                inc=[("act", 1)], waits=[ev1])
            yield
            ev3 = ph.op("vector", lambda e: e.tensor_scalar_max(
                R[2][:, :], R[1][:, :], 1.0 + EPS), inc=[("dve", 1)], waits=[ev2])
            yield
            ev4 = ph.op("scalar", lambda e: e.activation(
                R[3][:, :], R[2][:, :], AF.Square), inc=[("act", 1)], waits=[ev3])
            yield
            ev5 = ph.op("scalar", lambda e: e.activation(
                R[3][:, :], R[3][:, :], AF.Ln, bias=c_neg1[0:1, 0:1]),
                inc=[("act", 1)], waits=[ev4])
            yield
            ev6 = ph.op("scalar", lambda e: e.activation(
                R[3][:, :], R[3][:, :], AF.Exp, scale=0.5), inc=[("act", 1)],
                waits=[ev5])
            yield
            ev7 = ph.op("vector", lambda e: e.tensor_add(
                R[3][:, :], R[3][:, :], R[2][:, :]), inc=[("dve", 1)],
                waits=[ev6])
            yield
            ev8 = ph.op("scalar", lambda e: e.activation(
                R[3][:, :], R[3][:, :], AF.Ln), inc=[("act", 1)], waits=[ev7])
            yield
            ev9 = ph.op("vector", lambda e: e.tensor_scalar_max(
                R[4][:, :], R[3][:, :], 0.001), inc=[("dve", 1)], waits=[ev8])
            yield
            ev10 = ph.op("vector", lambda e: e.tensor_sub(
                R[3][:, :], R[1][:, :], R[2][:, :]), inc=[("dve", 1)],
                waits=[ev3, ev9])
            yield
            ev11 = ph.op("scalar", lambda e: e.activation(
                R[1][:, :], R[3][:, :], AF.Square), inc=[("act", 1)],
                waits=[ev10, ev9])
            yield
            ev12 = ph.op("vector", lambda e: e.tensor_sub(
                R[1][:, :], R[0][:, :], R[1][:, :]), inc=[("dve", 1)],
                waits=[ev11])
            yield
            ev13 = ph.op("vector", lambda e: e.tensor_scalar_max(
                R[1][:, :], R[1][:, :], EPS), inc=[("dve", 1)], waits=[ev12])
            yield
            ev14 = ph.op("scalar", lambda e: e.activation(
                R[1][:, :], R[1][:, :], AF.Ln), inc=[("act", 1)], waits=[ev13])
            yield
            ev15 = ph.op("scalar", lambda e: e.activation(
                R[1][:, :], R[1][:, :], AF.Exp, scale=0.5), inc=[("act", 1)],
                waits=[ev14])
            yield
            ev16 = ph.op("vector", lambda e: e.tensor_scalar_add(
                R[1][:, :], R[1][:, :], EPS), inc=[("dve", 1)], waits=[ev15])
            yield
            ev17 = ph.op("vector", lambda e: e.reciprocal(
                R[1][:, :], R[1][:, :]), inc=[("dve", 1)], waits=[ev16])
            yield
            ev18 = ph.op("vector", lambda e: e.tensor_mul(
                R[2][:, :], R[4][:, :], R[1][:, :]), inc=[("dve", 1)],
                waits=[ev17, ev10])
            yield
            vce["sbf"] = ph.op("scalar", lambda e: e.copy(
                s_bf[:, :], R[2][:, :]), inc=[("act", 1)], waits=[ev18])
            yield
            ev19 = ph.op("vector", lambda e: e.tensor_mul(
                R[3][:, :], R[3][:, :], R[2][:, :]), inc=[("dve", 1)],
                waits=[ev18, ev11])
            yield
            vce["d0"] = ph.op("scalar", lambda e: e.copy(
                dv0s_bf[:, :], R[3][:, :]), inc=[("act", 1)], waits=[ev19])
            yield

        with ExitStack() as cm:
            W1s = sb("W1s", [128, DC * DFF], BF16, cm)
            W2s = sb("W2s", [128, FC * D], BF16, cm)
            h1 = sb("h1", [128, FC * 512], BF16, cm)
            th = [sb("th%d" % i, [128, 512], F32, cm) for i in range(2)]
            ph = Phase(p, "metric")
            vgen = emit_vchain(ph)
            HC2 = DC // 2
            ev_w1h = []
            for hw in range(2):
                ev_w1h.append(ph.op("gpsimd", lambda e, hw=hw: e.dma_start(
                    W1s[:, hw * HC2 * DFF:(hw + 1) * HC2 * DFF].rearrange(
                        "p (c n) -> p c n", c=HC2),
                    W1T_d[hw * (D // 2):(hw + 1) * (D // 2), :].rearrange(
                        "(c p) n -> p c n", p=128)),
                    inc=[("dw1%d" % hw, 16)]))
            ev_w1 = ev_w1h[1]
            ev_w2 = ph.op("gpsimd", lambda e: e.dma_start(
                W2s[:, :].rearrange("p (c n) -> p c n", c=FC),
                W2T_d[:, :].rearrange("(c p) n -> p c n", p=128)), inc=[("dgw", 16)])
            ev_adj = ph.op("vector", lambda e: e.tensor_scalar_add(
                xnT[0:1, 0:T], xnT[0:1, 0:T], -1.0), inc=[("dve", 1)])
            with ExitStack() as c2:
                ph1 = [ps("ph1%d" % i, [128, 512], F32, c2) for i in range(2)]
                pg = [ps("pg%d" % i, [128, 512], F32, c2) for i in range(2)]
                ev_h1ev = {}
                ev_thev = {}
                ev_gw = {}
                for q in range(T // 512):
                    for m in range(FC):
                        b = m % 2
                        w = [ev_adj]
                        if (q, m - 2) in ev_h1ev:
                            w.append(ev_h1ev[(q, m - 2)])
                        elif q > 0:
                            w.append(ev_h1ev[(q - 1, FC - 2 + m)])
                        if q > 0 and m < 2:
                            w.append(ev_gw[(q - 1, DC - 1)])

                        def h1mma(e, q=q, m=m, b=b):
                            for c in range(HC2):
                                i = e.matmul(
                                    ph1[b][:, :],
                                    W1s[:, c * DFF + m * 128: c * DFF + (m + 1) * 128],
                                    xnT[:, c * T + q * 512: c * T + q * 512 + 512],
                                    start=(c == 0), stop=False)
                            return i

                        def h1mmb(e, q=q, m=m, b=b):
                            for c in range(HC2, DC):
                                i = e.matmul(
                                    ph1[b][:, :],
                                    W1s[:, c * DFF + m * 128: c * DFF + (m + 1) * 128],
                                    xnT[:, c * T + q * 512: c * T + q * 512 + 512],
                                    start=False, stop=(c == DC - 1))
                            return i
                        wa = [x for x in w if x is not ev_w2] + [ev_w1h[0]]
                        ph.op("tensor", h1mma, inc=[("pe", 1)], waits=wa)
                        ev_mm = ph.op("tensor", h1mmb, inc=[("pe", 1)],
                                      waits=[ev_w1h[1]])
                        ev_h1ev[(q, m)] = ph.op(
                            "scalar", lambda e, q=q, m=m, b=b: e.activation(
                                h1[:, m * 512:(m + 1) * 512], ph1[b][:, :], AF.Tanh,
                                bias=b1T[:, m:m + 1]), inc=[("act", 1)],
                            waits=[ev_mm])
                        next(vgen, None)
                    for md in range(DC):
                        b = md % 2
                        w = [ev_w2, ev_h1ev[(q, FC - 1)]]
                        if (q, md - 2) in ev_thev:
                            w.append(ev_thev[(q, md - 2)])
                        elif q > 0:
                            w.append(ev_thev[(q - 1, DC - 2 + md)])

                        def gmm(e, q=q, md=md, b=b):
                            for c in range(FC):
                                i = e.matmul(
                                    pg[b][:, :],
                                    W2s[:, c * D + md * 128: c * D + (md + 1) * 128],
                                    h1[:, c * 512:(c + 1) * 512],
                                    start=(c == 0), stop=(c == FC - 1))
                            return i
                        ev_gw[(q, md)] = ph.op("tensor", gmm, inc=[("pe", 1)], waits=w)
                        ev_th = ph.op("scalar", lambda e, q=q, md=md, b=b: e.activation(
                            th[b][:, :], pg[b][:, :], AF.Tanh, bias=b2T[:, md:md + 1]),
                            inc=[("act", 1)], waits=[ev_gw[(q, md)]])
                        ev_thev[(q, md)] = ev_th
                        ev_ab = ph.op("scalar", lambda e, q=q, md=md, b=b: e.activation(
                            g_sb[:, md * T + q * 512: md * T + q * 512 + 512],
                            th[b][:, :], AF.Abs, scale=0.1, bias=c_pos1[:, 0:1]),
                            inc=[("act", 1)], waits=[ev_th])
                        if md == 0:
                            ev_thev[(q, md)] = ph.op(
                                "scalar", lambda e, q=q, b=b: e.activation(
                                    g_sb[0:1, q * 512: q * 512 + 512],
                                    th[b][0:1, :], AF.Abs, scale=0.1, bias=c_neg1[0:1, 0:1]),
                                inc=[("act", 1)], waits=[ev_ab])
                        next(vgen, None)
                for _ in vgen:
                    pass
                ph.run()
        if want("g"):
            dump_sbuf("g", g_sb[:, :], [128, DC * T])
        if stop == "g":
            _finish(p)
            ctx_at1.close()
            return nc, dump_outs

        ctx_at2 = ExitStack()

        with ExitStack() as cv:
            clipv = [sb("clipv%d" % i, [128, T], BF16, cv) for i in range(2)]
            ev_sbf = vce["sbf"]
            ev_d0 = vce["d0"]
            pss_ctx.close()
            vch_ctx.close()
            wpb = [sb("wpb%d" % i, [128, DC * D], BF16, ctx_at1, side="right")
                   for i in range(2)]
            with ExitStack() as c2:
                psb = ps("psb", [128, T], F32, c2)
                ph = Phase(p, "vbuild")
                ev_lq = ph.op("gpsimd", lambda e: e.dma_start(
                    wpb[0][:, :].rearrange("p (c n) -> p c n", c=DC),
                    WqT_d[:, :].rearrange("(c p) n -> p c n", p=128)),
                    inc=[("dgw", 16)])
                ev_lk = ph.op("gpsimd", lambda e: e.dma_start(
                    wpb[1][:, :].rearrange("p (c n) -> p c n", c=DC),
                    WkT_d[:, :].rearrange("(c p) n -> p c n", p=128)),
                    inc=[("dgw", 16)])

                def sbc(e):
                    for n in range(NS):
                        i = e.matmul(psb[:, n * 512:(n + 1) * 512], ones_row[:, :],
                                     s_bf[:, n * 512:(n + 1) * 512],
                                     start=True, stop=True)
                    return i
                ev_bc = ph.op("tensor", sbc, inc=[("pe", 1)], waits=[ev_sbf])
                ev_v0 = None
                ev_cl2 = {}
                ev_vv2 = {}
                for c in range(DC):
                    b = c % 2
                    w = []
                    if c >= 2:
                        w.append(ev_vv2[c - 2])
                    ev_cl2[c] = ph.op("vector", lambda e, c=c, b=b: e.tensor_scalar(
                        clipv[b][:, :], xnT[:, c * T:(c + 1) * T],
                        8.0, -8.0, ALU.min, ALU.max), inc=[("dve", 1)], waits=w)
                    ev_vv2[c] = ph.op("vector", lambda e, c=c, b=b: e.tensor_mul(
                        xnT[:, c * T:(c + 1) * T], clipv[b][:, :],
                        psb[:, :]), inc=[("dve", 1)], waits=[ev_cl2[c], ev_bc])
                    if c == 0:
                        ev_v0 = ev_vv2[c]
                ev_row0 = ph.op("scalar", lambda e: e.copy(
                    xnT[0:1, 0:T], dv0s_bf[:, :]),
                    inc=[("act", 1)], waits=[ev_v0, ev_d0])
                ph.run()
        cv2.close()
        vT = xnT
        if want("vT"):
            dump_sbuf("vT", vT[:, :], [128, DC * T])
        if stop == "vT":
            _finish(p)
            ctx_at1.close()
            return nc, dump_outs

        QG = sb("QG", [128, DC * T], BF16, ctx_at2)
        with ExitStack() as c2:
            pq = [ps("pq%d" % i, [128, T], F32, c2) for i in range(2)]
            ph = Phase(p, "qproj")
            ev_qg = {}
            for m in range(DC):
                b = m % 2
                w = [ev_lk]
                if m >= 2:
                    w.append(ev_qg[m - 2])
                else:
                    w += [ev_row0, ev_vv2[DC - 1], ev_vv2[DC - 2]]

                def qmm(e, m=m, b=b):
                    for c in range(DC):
                        for n in range(NS):
                            i = e.matmul(
                                pq[b][:, n * 512:(n + 1) * 512],
                                wpb[0][:, c * D + m * 128: c * D + (m + 1) * 128],
                                vT[:, c * T + n * 512: c * T + (n + 1) * 512],
                                start=(c == 0), stop=(c == DC - 1))
                    return i
                ev_mm = ph.op("tensor", qmm, inc=[("pe", 1)], waits=w)
                ev_qg[m] = ph.op("vector", lambda e, m=m, b=b: e.tensor_mul(
                    QG[:, m * T:(m + 1) * T], pq[b][:, :],
                    g_sb[:, m * T:(m + 1) * T]), inc=[("dve", 1)], waits=[ev_mm])
            ev_neg = ph.op("vector", lambda e: e.tensor_scalar_mul(
                QG[0:1, 0:T], QG[0:1, 0:T], -1.0), inc=[("dve", 1)],
                waits=[ev_qg[0]])
            ph.run()

            kT = sb("kT", [128, DC * T], BF16, ctx_at2)
            vv = sb("vv", [128, TC * H * 65], BF16, ctx_at2)
            ph = Phase(p, "kproj")
            ev_kev = {}
            for m in range(DC):
                b = m % 2
                w = [ev_lk]
                if m >= 2:
                    w.append(ev_kev[m - 2])
                else:
                    w.append(ev_qg[DC - 2 + m])
                    w.append(ev_neg)

                def kmm(e, m=m, b=b):
                    for c in range(DC):
                        for n in range(NS):
                            i = e.matmul(
                                pq[b][:, n * 512:(n + 1) * 512],
                                wpb[1][:, c * D + m * 128: c * D + (m + 1) * 128],
                                vT[:, c * T + n * 512: c * T + (n + 1) * 512],
                                start=(c == 0), stop=(c == DC - 1))
                    return i
                ev_mm = ph.op("tensor", kmm, inc=[("pe", 1)], waits=w)
                ev_kev[m] = ph.op("scalar", lambda e, m=m, b=b: e.copy(
                    kT[:, m * T:(m + 1) * T], pq[b][:, :]), inc=[("act", 1)],
                    waits=[ev_mm])
            ev_lv = ph.op("gpsimd", lambda e: e.dma_start(
                wpb[0][:, :].rearrange("p (c n) -> p c n", c=DC),
                WvT_d[:, :].rearrange("(c p) n -> p c n", p=128)),
                inc=[("dgw", 16)], waits=[ev_qg[DC - 1]])
            ph.run()

        cb = ExitStack()
        sqs2 = [sb("sqb2%d" % i, [128, T // 2], BF16, cb) for i in range(2)]
        r_m = sb("r_m", [1, 1], F32, cb)
        nb_bf = sb("nb_bf", [1, 1], BF16, cb)
        with ExitStack() as c2:
            pv = [ps("pv%d" % i, [128, D], F32, c2) for i in range(2)]
            pqn = ps("pqn", [1, T], F32, c2)
            ph = Phase(p, "vvproj")
            ev_one = ph.op("gpsimd", lambda e: e.memset(
                bass.AP(vv, 64, [[TC * H * 65, 128], [H * 65, TC], [65, H], [1, 1]]),
                1.0), inc=[("pl", 1)])
            ev_vev = {}
            ev_sq = {}
            ev_nmm = {}
            norm_srcs = []
            for s_, c, ev_ in ([(QG, c, ev_qg[c]) for c in range(DC)] +
                               [(kT, c, ev_kev[c]) for c in range(DC)]):
                for hh in (0, 1):
                    norm_srcs.append((s_, c, hh, ev_))
            for tt in range(TC):
                b = tt % 2
                w = [ev_lv]
                if tt >= 2:
                    w.append(ev_vev[tt - 2])

                def vmm(e, tt=tt, b=b):
                    for c in range(DC):
                        for n in range(D // 512):
                            i = e.matmul(
                                pv[b][:, n * 512:(n + 1) * 512],
                                vT[:, c * T + tt * 128: c * T + (tt + 1) * 128],
                                wpb[0][:, c * D + n * 512: c * D + (n + 1) * 512],
                                start=(c == 0), stop=(c == DC - 1))
                    return i
                ev_mm = ph.op("tensor", vmm, inc=[("pe", 1)], waits=w)
                ev_vev[tt] = ph.op("scalar", lambda e, tt=tt, b=b: e.copy(
                    bass.AP(vv, tt * H * 65, [[TC * H * 65, 128], [65, H], [1, 64]]),
                    bass.AP(pv[b], 0, [[D, 128], [64, H], [1, 64]])),
                    inc=[("act", 1)], waits=[ev_mm, ev_one])
                NK = len(norm_srcs)
                TH2 = T // 2

                def emit_nmm(k):
                    s_, c, hh, ev_ = norm_srcs[k]

                    def nmm(e, k=k, c=c, hh=hh):
                        for n in range(TH2 // 512):
                            i = e.matmul(
                                pqn[:, hh * TH2 + n * 512: hh * TH2 + (n + 1) * 512],
                                ones_col[:, :],
                                sqs2[k % 2][:, n * 512:(n + 1) * 512],
                                start=(c == 0 and s_ is QG),
                                stop=(c == DC - 1 and s_ is kT))
                        return i
                    ev_nmm[k] = ph.op("tensor", nmm, inc=[("pe", 1)],
                                      waits=[ev_sq[k]])

                for ksub in (2 * tt, 2 * tt + 1):
                    if ksub < NK:
                        src_t, c, hh, ev_src = norm_srcs[ksub]
                        w2 = [ev_src]
                        if (ksub - 2) in ev_nmm:
                            w2.append(ev_nmm[ksub - 2])
                        ev_sq[ksub] = ph.op(
                            "scalar",
                            lambda e, src_t=src_t, c=c, hh=hh, ksub=ksub:
                            e.activation(
                                sqs2[ksub % 2][:, :],
                                src_t[:, c * T + hh * TH2:
                                      c * T + (hh + 1) * TH2],
                                AF.Square), inc=[("act", 1)], waits=w2)
                        if ksub >= 1:
                            emit_nmm(ksub - 1)
            emit_nmm(NK - 1)
            ev = ev_nmm[NK - 1]
            ev2 = ph.op("vector", lambda e: e.reduce_max(
                r_m[:, :], pqn[:, :], axis=AX.X), inc=[("dve", 1)], waits=[ev])
            ev3 = ph.op("vector", lambda e: e.tensor_scalar(
                r_m[:, :], r_m[:, :], 1.5 / 16.0, 43.0, ALU.mult, ALU.min),
                inc=[("dve", 1)], waits=[ev2])
            ev4 = ph.op("vector", lambda e: e.tensor_scalar_mul(
                r_m[:, :], r_m[:, :], -1.0), inc=[("dve", 1)], waits=[ev3])
            ev5 = ph.op("scalar", lambda e: e.copy(nb_bf[:, :], r_m[:, :]),
                        inc=[("act", 1)], waits=[ev4])
            ph.run()
        ctx_at1.close()

        if want("QG"):
            dump_sbuf("QG", QG[:, :], [128, DC * T])
            dump_sbuf("kT", kT[:, :], [128, DC * T])
            dump_sbuf("vv", vv[:, :], [128, TC * H * 65])
        if stop == "QG":
            _finish(p)
            ctx_at2.close()
            return nc, dump_outs

        with ExitStack() as c2:
            pnb = ps("pnb", [128, 1], F32, c2)
            ph = Phase(p, "bbcast")
            ev6 = ph.op("tensor", lambda e: e.matmul(
                pnb[:, :], ones_row[:, :], nb_bf[:, :], start=True, stop=True),
                inc=[("pe", 1)], waits=[ev5])
            ev_nb = ph.op("scalar", lambda e: e.copy(negB[:, :], pnb[:, :]),
                          inc=[("act", 1)], waits=[ev6])
            ph.run()
        cb.close()

        if want("bb"):
            dump_sbuf("bb", negB[:, :], [128, 1])
        if stop == "bb":
            _finish(p)
            ctx_at2.close()
            return nc, dump_outs

        ctx_at3 = ExitStack()
        avout = sb("avout", [128, DC * T], BF16, ctx_at3, side="right")
        wo_sb = sb("wo_sb", [128, DC * D], BF16, ctx_at3, side="right")
        ph = Phase(p, "wo_load")
        ev_lwo = ph.op("gpsimd", lambda e: e.dma_start(
            wo_sb[:, :].rearrange("p (c n) -> p c n", c=DC),
            WoT_d[:, :].rearrange("(c p) n -> p c n", p=128)), inc=[("dgw", 16)])
        ph.run()

        with ExitStack() as ca:
            NB = 2
            NPT = 5
            PT = [sb("PT%d" % i, [128, 1024], BF16, ca) for i in range(NPT)]
            srow = [sb("srow%d" % i, [1, 512], F32, ca) for i in range(4)]
            sums16 = sb("sums16", [H, T], F32, ca)
            QQ = T // 512
            with ExitStack() as c2:
                pS = [ps("pS%d" % i, [128, 1024], F32, c2) for i in range(NB)]
                pav = [[ps("pav%d_%d" % (par, j), [65, 512], F32, c2)
                        for j in range(2)] for par in range(2)]
                ph = Phase(p, "attn_core")
                ev_exp = {}
                ev_avp = {}
                ev_evac = {}
                ev_srr = {}
                ev_sdma = {}
                it_g = [0]
                for hp in range(H // 2):
                    for qq in range(QQ):
                        hq = hp * QQ + qq

                        def emit_av(kc, hq=hq, hp=hp, qq=qq):
                            it = hq * KC + kc
                            w = [ev_exp[it]]
                            if kc == 0 and hq > 1:
                                w.append(ev_evac[(hq - 2, 0)])
                                w.append(ev_evac[(hq - 2, 1)])
                                w.append(ev_srr[(hq - 2, 0)])
                                w.append(ev_srr[(hq - 2, 1)])

                            def avmm(e, hp=hp, kc=kc, it=it, hq=hq):
                                for j in range(2):
                                    i = e.matmul(
                                        pav[hq % 2][j][:, :],
                                        vv[:, kc * H * 65 + (2 * hp + j) * 65:
                                           kc * H * 65 + (2 * hp + j) * 65 + 65],
                                        PT[it % NPT][:, j * 512:(j + 1) * 512],
                                        start=(kc == 0), stop=(kc == KC - 1))
                                return i
                            ev_avp[it] = ph.op("tensor", avmm, inc=[("pe", 1)],
                                               waits=w)

                        for kc in range(KC):
                            it = hq * KC + kc
                            b = kc % NB
                            w = []
                            if it >= NB:
                                w.append(ev_exp[it - NB])

                            def smm(e, hp=hp, qq=qq, kc=kc, b=b):
                                q0 = qq * 512
                                e.matmul(
                                    pS[b][:, 0:512],
                                    kT[0:64, hp * T + kc * 128:
                                       hp * T + (kc + 1) * 128],
                                    QG[0:64, hp * T + q0: hp * T + q0 + 512],
                                    start=True, stop=True)
                                i = e.matmul(
                                    pS[b][:, 512:1024],
                                    kT[64:128, hp * T + kc * 128:
                                       hp * T + (kc + 1) * 128],
                                    QG[64:128, hp * T + q0: hp * T + q0 + 512],
                                    start=True, stop=True)
                                return i
                            ev_s = ph.op("tensor", smm, inc=[("pe", 1)], waits=w)
                            wv_ = [ev_s, ev_nb]
                            if it >= NPT:
                                wv_.append(ev_avp[it - NPT])
                            ev_exp[it] = ph.op(
                                "scalar", lambda e, it=it, b=b: e.activation(
                                    PT[it % NPT][:, :], pS[b][:, :], AF.Exp,
                                    bias=negB[:, 0:1], scale=0.125),
                                inc=[("act", 1)], waits=wv_)
                            if kc >= 1:
                                emit_av(kc - 1)
                        emit_av(KC - 1)
                        for j in range(2):
                            half = j * 64
                            ev_evac[(hq, j)] = ph.op(
                                "vector",
                                lambda e, hp=hp, qq=qq, j=j, half=half, hq=hq:
                                e.tensor_copy(
                                    avout[half:half + 64,
                                          hp * T + qq * 512: hp * T + qq * 512 + 512],
                                    pav[hq % 2][j][0:64, :]),
                                inc=[("dve", 1)],
                                waits=[ev_avp[hq * KC + KC - 1]])
                            si = 2 * (hq % 2) + j
                            w2 = [ev_avp[hq * KC + KC - 1]]
                            prev = ev_sdma.get((hq - 2, j))
                            if prev:
                                w2.append(prev)
                            ev_sr = ph.op("vector",
                                          lambda e, j=j, si=si, hq=hq: e.tensor_copy(
                                              srow[si][:, :],
                                              pav[hq % 2][j][64:65, :]),
                                          inc=[("dve", 1)], waits=w2)
                            ev_srr[(hq, j)] = ev_sr
                            ev_sdma[(hq, j)] = ph.op(
                                "sync", lambda e, hp=hp, qq=qq, j=j, si=si: e.dma_start(
                                    sums16[2 * hp + j: 2 * hp + j + 1,
                                           qq * 512:(qq + 1) * 512], srow[si][:, :]),
                                inc=[("dsr%d" % j, 16)], waits=[ev_sr])
                ph.run()
            rec16 = sb("rec16", [H, T], F32, ca)
            rec16b = sb("rec16b", [H, T], BF16, ca)
            ev_attn_done = []
            with ExitStack() as c2:
                pnorm = [ps("pnorm%d" % i, [128, T], F32, c2) for i in range(2)]
                ph = Phase(p, "avnorm")
                ev_r = ph.op("vector", lambda e: e.reciprocal(
                    rec16[:, :], sums16[:, :]), inc=[("dve", 1)],
                    waits=[p.now("dsr0"), p.now("dsr1")])
                ev_rb = ph.op("scalar", lambda e: e.copy(rec16b[:, :], rec16[:, :]),
                              inc=[("act", 1)], waits=[ev_r])
                nrm_ev = {}
                ev_b_last = [None]
                for hp in range(DC):
                    pb = hp % 2
                    w = [ev_rb, ev_evac[(hp * (T // 512) + T // 512 - 1, 1)]]
                    if hp >= 2:
                        w.append(nrm_ev[hp - 2])

                    def nbc(e, hp=hp, pb=pb):
                        for j in range(2):
                            hh = 2 * hp + j
                            for n in range(NS):
                                i = e.matmul(
                                    pnorm[pb][64 * j:64 * (j + 1),
                                              n * 512:(n + 1) * 512],
                                    oh16[:, hh * 64:(hh + 1) * 64],
                                    rec16b[:, n * 512:(n + 1) * 512],
                                    start=True, stop=True)
                        return i
                    ev_b = ph.op("tensor", nbc, inc=[("pe", 1)], waits=w)
                    ev_b_last[0] = ev_b
                    nrm_ev[hp] = ph.op("vector", lambda e, hp=hp, pb=pb: e.tensor_mul(
                        avout[:, hp * T:(hp + 1) * T],
                        avout[:, hp * T:(hp + 1) * T], pnorm[pb][:, :]),
                        inc=[("dve", 1)], waits=[ev_b])
                    ev_attn_done.append(nrm_ev[hp])
                ph.run()
        ctx_at2.close()

        if want("avout"):
            dump_sbuf("avout", avout[:, :], [128, DC * T])
        if stop == "avout":
            _finish(p)
            ctx_at3.close()
            return nc, dump_outs

        ctx_xr2 = ExitStack()
        xr2 = sb("xr2", [128, DC * T], F32, ctx_xr2)
        xr2_sl = mk_sl(xr2)

        ph = Phase(p, "reload")
        ev_rel = ph.op("sync", lambda e: e.dma_start(
            xr2[:, :].rearrange("p (c t) -> p c t", c=DC),
            xsp_d[:, :].rearrange("(c p) t -> p c t", p=128)),
            inc=[("drl", 16)], waits=[ev_b_last[0]])
        ph.run()

        with ExitStack() as cw:
            sp_sb = sb("sp_sb", [128, DC * 512], BF16, cw)
            sqw = [sb("sqw%d" % i, [128, 512], BF16, cw) for i in range(2)]
            t32w = [sb("t32w%d" % i, [128, 512], F32, cw) for i in range(2)]
            r_o0 = sb("r_o0", [1, 512], F32, cw)
            r_x0 = sb("r_x0", [1, 512], F32, cw)
            with ExitStack() as c2:
                pw = [ps("pw%d" % i, [128, 512], F32, c2) for i in range(2)]
                psq = ps("psq", [1, 512], F32, c2)
                ph = Phase(p, "wo_tail")
                ev_sp = {}
                ev_sqev = {}
                ev_smm = {}
                ev_add = {}
                ev_o1 = {}
                ev_x0c = {}
                for ts_ in range(NS):
                    for md in range(DC):
                        b = md % 2
                        w = [ev_lwo] + list(ev_attn_done)
                        if (ts_, md - 2) in ev_sp:
                            w.append(ev_sp[(ts_, md - 2)])
                        elif ts_ > 0:
                            w.append(ev_sp[(ts_ - 1, DC - 2 + md)])

                        def wmma(e, ts_=ts_, md=md, b=b):
                            for c in range(DC // 2):
                                i = e.matmul(
                                    pw[b][:, :],
                                    wo_sb[:, c * D + md * 128: c * D + (md + 1) * 128],
                                    avout[:, c * T + ts_ * 512: c * T + ts_ * 512 + 512],
                                    start=(c == 0), stop=False)
                            return i

                        def wmmb(e, ts_=ts_, md=md, b=b):
                            for c in range(DC // 2, DC):
                                i = e.matmul(
                                    pw[b][:, :],
                                    wo_sb[:, c * D + md * 128: c * D + (md + 1) * 128],
                                    avout[:, c * T + ts_ * 512: c * T + ts_ * 512 + 512],
                                    start=False, stop=(c == DC - 1))
                            return i
                        ph.op("tensor", wmma, inc=[("pe", 1)],
                              waits=[x for x in w if x not in ev_attn_done]
                              + ev_attn_done[:DC // 2])
                        ev_mm = ph.op("tensor", wmmb, inc=[("pe", 1)],
                                      waits=ev_attn_done[DC // 2:])
                        w2 = [ev_mm]
                        if ts_ > 0:
                            w2.append(ev_add[(ts_ - 1, md)])
                            w2.append(ev_sqev[(ts_ - 1, md)])
                        ev_t1 = ph.op("vector", lambda e, md=md, b=b: e.tensor_scalar(
                            t32w[b][:, :], pw[b][:, :], 0.1, 8.0, ALU.mult, ALU.min),
                            inc=[("dve", 1)], waits=w2)
                        ev_sp[(ts_, md)] = ph.op(
                            "vector", lambda e, md=md, b=b: e.tensor_scalar_max(
                                sp_sb[:, md * 512:(md + 1) * 512], t32w[b][:, :],
                                -8.0), inc=[("dve", 1)], waits=[ev_t1])
                        w3 = [ev_sp[(ts_, md)]]
                        if (ts_, md - 2) in ev_smm:
                            w3.append(ev_smm[(ts_, md - 2)])
                        elif ts_ > 0 and md < 2:
                            w3.append(ev_smm[(ts_ - 1, DC - 2 + md)])
                        ev_sqev[(ts_, md)] = ph.op(
                            "scalar", lambda e, md=md, b=b: e.activation(
                                sqw[b][:, :], sp_sb[:, md * 512:(md + 1) * 512],
                                AF.Square), inc=[("act", 1)], waits=w3)

                        def emit_sqmm(mk, ts_=ts_):
                            w4 = [ev_sqev[(ts_, mk)]]
                            if mk == 0 and ts_ > 0:
                                w4.append(ev_o1[ts_ - 1])

                            def sqmm(e, mk=mk):
                                lhs = ones_col0 if mk == 0 else ones_col
                                return e.matmul(psq[:, :], lhs[:, :],
                                                sqw[mk % 2][:, :],
                                                start=(mk == 0),
                                                stop=(mk == DC - 1))
                            ev_smm[(ts_, mk)] = ph.op("tensor", sqmm,
                                                      inc=[("pe", 1)], waits=w4)

                        if md >= 1:
                            emit_sqmm(md - 1)
                        if md == DC - 1:
                            emit_sqmm(DC - 1)
                    w5 = [ev_smm[(ts_, DC - 1)]]
                    if ts_ > 0:
                        w5.append(ev_x0c[ts_ - 1])
                    ev_o1[ts_] = ph.op("scalar", lambda e: e.activation(
                        r_o0[:, :], psq[:, :], AF.Ln, bias=c_1eps[0:1, 0:1]),
                        inc=[("act", 1)], waits=w5)
                    ev_o2 = ph.op("scalar", lambda e: e.activation(
                        r_o0[:, :], r_o0[:, :], AF.Exp, scale=0.5),
                        inc=[("act", 1)], waits=[ev_o1[ts_]])
                    w6 = [ev_o2, ev_rel]
                    if ts_ > 0:
                        w6.append(ev_x0c[ts_ - 1])
                    ev_x0 = ph.op("vector", lambda e, ts_=ts_: e.tensor_add(
                        r_x0[:, :], xr2[0:1, ts_ * 512: ts_ * 512 + 512], r_o0[:, :]),
                        inc=[("dve", 1)], waits=w6)
                    for c in range(DC):
                        ev_add[(ts_, c)] = ph.op(
                            "vector", lambda e, ts_=ts_, c=c: e.tensor_add(
                                xr2_sl(c, ts_ * 512, 512), xr2_sl(c, ts_ * 512, 512),
                                sp_sb[:, c * 512:(c + 1) * 512]),
                            inc=[("dve", 1)],
                            waits=[ev_sp[(ts_, c)], ev_x0, ev_rel])
                    ev_x0c[ts_] = ph.op("scalar", lambda e, ts_=ts_: e.copy(
                        xr2[0:1, ts_ * 512: ts_ * 512 + 512], r_x0[:, :]),
                        inc=[("act", 1)], waits=[ev_add[(ts_, 0)]])
                ph.run()
        ctx_at3.close()

        if want("attn"):
            dump_sbuf("attn", xr2[:, :], [128, DC * T])
        if stop == "attn":
            _finish(p)
            ctx_xr2.close()
            return nc, dump_outs

        f2_res = ffn_sublayer("f2", xr2_sl)

        ph = Phase(p, "out")
        for tc in range(T // 512):
            ph.op("sync", lambda e, tc=tc: e.dma_start(
                bass.AP(yT_d, tc * 512, [[T, 128], [128 * T, DC], [1, 512]]),
                bass.AP(xr2, tc * 512, [[DC * T, 128], [T, DC], [1, 512]])),
                inc=[("dou", 16)],
                waits=[f2_res[(tc, 0)], f2_res[(tc, 1)]])
        ph.run()
        _finish(p)
        ctx_xr2.close()

    return nc, dump_outs


def _finish(p):
    ph = Phase(p, "final_wait")
    sems = ["dxp", "dgp", "dgd", "dg0", "dg1", "dh0", "dh1", "dhw0", "dhw1",
            "dgw", "dsp", "drl", "dou", "dsr0", "dsr1", "dw10", "dw11"]
    ph.op("gpsimd", lambda e: e.nop(), waits=[p.now(s) for s in sems])
    ph.run()


def host_inputs(inputs, cfg=CFG):
    E, FC, DC = cfg.E, cfg.FC, cfg.DC
    x = np.asarray(inputs["L0_attn_x"])
    base = {
        "WrT": np.ascontiguousarray(np.asarray(inputs["L0_ffn_router_w"]).T),
        "rb": np.ascontiguousarray(
            np.asarray(inputs["L0_ffn_router_b"]).reshape(1, E)),
        "WeT": np.ascontiguousarray(
            np.asarray(inputs["L0_ffn_expert_w"]).transpose(0, 2, 1)),
        "Be": np.ascontiguousarray(np.asarray(inputs["L0_ffn_expert_b"])),
        "W1T": np.ascontiguousarray(np.asarray(inputs["L0_attn_W1"]).T),
        "b1T": np.ascontiguousarray(
            np.asarray(inputs["L0_attn_b1"]).reshape(FC, 128).T),
        "W2T": np.ascontiguousarray(np.asarray(inputs["L0_attn_W2"]).T),
        "b2T": np.ascontiguousarray(
            np.asarray(inputs["L0_attn_b2"]).reshape(DC, 128).T),
        "WqT": np.ascontiguousarray(np.asarray(inputs["L0_attn_Wq"]).T),
        "WkT": np.ascontiguousarray(np.asarray(inputs["L0_attn_Wk"]).T),
        "WvT": np.ascontiguousarray(np.asarray(inputs["L0_attn_Wv"]).T),
        "WoT": np.ascontiguousarray(np.asarray(inputs["L0_attn_Wo"]).T),
    }
    maps = []
    for c in range(x.shape[0]):
        m = dict(base)
        m["xT"] = np.ascontiguousarray(x[c].T)
        maps.append(m)
    return maps


def kernel(**inputs):
    nc, _ = build(CFG)
    in_maps = host_inputs(inputs, CFG)
    res = run_bass_kernel_spmd(nc, in_maps, core_ids=list(range(8)))
    out = np.stack([np.asarray(r["yT"]).T for r in res.results], axis=0)
    return np.ascontiguousarray(out.astype(np.float32))
